# version 07
# baseline (speedup 1.0000x reference)
import numpy as np
import ml_dtypes
from contextlib import ExitStack
import concourse.bass as bass
import concourse.mybir as mybir
from concourse.bass_utils import run_bass_kernel_spmd

F32 = mybir.dt.float32
BF16 = mybir.dt.bfloat16
I32 = mybir.dt.int32
AF = mybir.ActivationFunctionType
OP = mybir.AluOpType
AX = mybir.AxisListType
bf16 = ml_dtypes.bfloat16

NCORE = 8
D = 4096
NF = D // 128
T = 8192
TC = T // NCORE


class Sem:
    def __init__(self, nc, es, name):
        self.h = es.enter_context(nc.semaphore(name))
        self.n = 0

    def inc(self, inst, k=1):
        inst.then_inc(self.h, k)
        self.n += k
        return self.n


class Sched:
    def __init__(self):
        self.ev = []
        self.pending_gate = {}

    def start_phase(self, gate):
        for eng in ("sync", "tensor", "vector", "scalar", "gpsimd"):
            self.pending_gate[eng] = list(gate)

    def E(self, eng, waits, fn, inc=None, k=1):
        waits = [w for w in waits if w is not None and w[1] is not None]
        if eng in self.pending_gate:
            waits = self.pending_gate.pop(eng) + waits
        self.ev.append((eng, waits, fn, inc, k))
        if inc is not None:
            inc.n += k
            return inc.n
        return None

    def emit(self, blk):
        for name, deco in (("sync", blk.sync), ("tensor", blk.tensor),
                           ("vector", blk.vector), ("scalar", blk.scalar),
                           ("gpsimd", blk.gpsimd)):
            def body(eng, _name=name):
                for (en, waits, fn, inc, k) in self.ev:
                    if en != _name:
                        continue
                    for (sm, val) in waits:
                        eng.wait_ge(sm.h, val)
                    inst = fn(eng)
                    if inc is not None:
                        inst.then_inc(inc.h, k)
            deco(body)


def build_front(Tc, n_adds, rope, kvd, with_bias):
    tw = min(512, Tc)
    nth = Tc // tw
    nkc = kvd // 128
    nc = bass.Bass("TRN2", target_bir_lowering=False)
    x = nc.dram_tensor("x", [D, Tc], F32, kind="ExternalInput")
    adds = [nc.dram_tensor(f"a{i}", [D, Tc], BF16, kind="ExternalInput") for i in range(n_adds)]
    wq = nc.dram_tensor("wq", [D, D], F32, kind="ExternalInput")
    wk = nc.dram_tensor("wk", [D, kvd], F32, kind="ExternalInput")
    wv = nc.dram_tensor("wv", [D, kvd], F32, kind="ExternalInput")
    bqt = nc.dram_tensor("bq", [128, NF], F32, kind="ExternalInput") if with_bias else None
    bkt = nc.dram_tensor("bk", [128, nkc], F32, kind="ExternalInput") if with_bias else None
    bvt = nc.dram_tensor("bv", [128, nkc], F32, kind="ExternalInput") if with_bias else None
    if rope:
        cosd = nc.dram_tensor("cos", [64, Tc], F32, kind="ExternalInput")
        sind = nc.dram_tensor("sin", [64, Tc], F32, kind="ExternalInput")
    xo = nc.dram_tensor("xo", [D, Tc], F32, kind="ExternalOutput") if n_adds else None
    xnd = nc.dram_tensor("xn", [D, Tc], BF16, kind="ExternalOutput")
    qd = nc.dram_tensor("q", [D, Tc], BF16, kind="ExternalOutput")
    kd = nc.dram_tensor("k", [kvd, Tc], BF16, kind="ExternalOutput")
    vd = nc.dram_tensor("v", [kvd, Tc], BF16, kind="ExternalOutput")

    with ExitStack() as es:
        sb = lambda n, s, d=F32: es.enter_context(nc.sbuf_tensor(n, s, d))
        ones_b = sb("ones_b", [128, 1], BF16)
        ones_r = sb("ones_r", [1, 128], F32)
        eps_c = sb("eps_c", [1, 1], F32)
        xn_sb = sb("xn_sb", [128, NF * Tc], BF16)
        mrow = sb("mrow", [1, Tc])
        arow = sb("arow", [1, Tc])
        rrow = sb("rrow", [1, Tc])
        if rope:
            cos_sb = sb("cos_sb", [64, Tc])
            sin_sb = sb("sin_sb", [64, Tc])

        with ExitStack() as eAB:
            mbc = [eAB.enter_context(nc.psum_tensor(f"mbc{t}", [128, tw], F32)) for t in range(nth)]
            rbc = [eAB.enter_context(nc.psum_tensor(f"rbc{t}", [128, tw], F32)) for t in range(nth)]

            with ExitStack() as eA:
                sbA = lambda n, s, d=F32: eA.enter_context(nc.sbuf_tensor(n, s, d))
                x_b = [sbA(f"x_b{i}", [128, Tc]) for i in range(2)]
                a_b = [[sbA(f"a_b{j}_{i}", [128, Tc], BF16) for i in range(2)] for j in range(n_adds)]
                xw_b = [sbA(f"xw_b{i}", [128, Tc]) for i in range(2)]
                xc_b = [sbA(f"xc_b{i}", [128, Tc], BF16) for i in range(2)]
                sq_b = [sbA(f"sq_b{i}", [128, Tc], BF16) for i in range(2)]
                srow = sbA("srow", [1, Tc])
                qrow = sbA("qrow", [1, Tc])
                msq = sbA("msq", [1, Tc])
                vrow = sbA("vrow", [1, Tc])
                stdr = sbA("stdr", [1, Tc])
                sum_ps = [eA.enter_context(nc.psum_tensor(f"sum_ps{t}", [1, tw], F32)) for t in range(nth)]
                ssq_ps = [eA.enter_context(nc.psum_tensor(f"ssq_ps{t}", [1, tw], F32)) for t in range(nth)]
                blkA = eA.enter_context(nc.Block())
                dmx = [Sem(nc, eA, "dmx0"), Sem(nc, eA, "dmx1")]
                drt = Sem(nc, eA, "drt")
                sv = Sem(nc, eA, "svA")
                sc = Sem(nc, eA, "scA")
                mm = Sem(nc, eA, "mmA")
                gm = [Sem(nc, eA, "gmA0"), Sem(nc, eA, "gmA1")]
                per_f = 1 + n_adds
                VA = n_adds + 1

                @blkA.sync
                def _(sync):
                    for f in range(NF):
                        b = f % 2
                        if f >= 2:
                            sync.wait_ge(sv.h, 1 + (f - 1) * VA)
                            if n_adds == 0:
                                sync.wait_ge(sc.h, f - 1)
                        sync.dma_start(out=x_b[b][:, :], in_=x[f * 128:(f + 1) * 128, :]).then_inc(dmx[b].h, 16)
                        for j in range(n_adds):
                            sync.dma_start(out=a_b[j][b][:, :],
                                           in_=adds[j][f * 128:(f + 1) * 128, :]).then_inc(dmx[b].h, 16)
                    if rope:
                        sync.dma_start(out=cos_sb[:, :], in_=cosd[:, :]).then_inc(drt.h, 16)
                        sync.dma_start(out=sin_sb[:, :], in_=sind[:, :]).then_inc(drt.h, 16)
                    sync.wait_ge(dmx[0].h, (NF // 2) * per_f * 16)
                    sync.wait_ge(dmx[1].h, (NF // 2) * per_f * 16)
                    if rope:
                        sync.wait_ge(drt.h, 32)
                dmx[0].n = dmx[1].n = (NF // 2) * per_f * 16
                drt.n = 32 if rope else 0

                @blkA.vector
                def _(vector):
                    vector.memset(ones_b[:, :], 1.0)
                    vector.memset(ones_r[:, :], 1.0)
                    sv.inc(vector.memset(eps_c[:, :], 1e-5))
                    for f in range(NF):
                        b = f % 2
                        vector.wait_ge(dmx[b].h, (f // 2 + 1) * per_f * 16)
                        if n_adds:
                            if f >= 2:
                                vector.wait_ge(sc.h, f - 1)
                                vector.wait_ge(gm[b].h, 16 * (f // 2))
                            sv.inc(vector.tensor_tensor(out=xw_b[b][:, :], in0=x_b[b][:, :],
                                                        in1=a_b[0][b][:, :], op=OP.add))
                            if n_adds == 2:
                                vector.wait_ge(sv.h, sv.n)
                                sv.inc(vector.tensor_tensor(out=xw_b[b][:, :], in0=xw_b[b][:, :],
                                                            in1=a_b[1][b][:, :], op=OP.add))
                            vector.wait_ge(sv.h, sv.n)
                            src = xw_b[b]
                        else:
                            if f >= 2:
                                vector.wait_ge(sc.h, f - 1)
                            src = x_b[b]
                        if f >= 2:
                            vector.wait_ge(mm.h, f - 1)
                        sv.inc(vector.tensor_copy(out=xc_b[b][:, :], in_=src[:, :]))
                    vector.wait_ge(mm.h, NF)
                    for t in range(nth):
                        sv.inc(vector.tensor_copy(out=srow[:, t * tw:(t + 1) * tw], in_=sum_ps[t][:, :]))
                    vector.wait_ge(sv.h, sv.n)
                    sv.inc(vector.tensor_scalar(out=mrow[:, :], in0=srow[:, :],
                                                scalar1=1.0 / D, scalar2=None, op0=OP.mult))
                    for t in range(nth):
                        sv.inc(vector.tensor_copy(out=qrow[:, t * tw:(t + 1) * tw], in_=ssq_ps[t][:, :]))
                    vector.wait_ge(sv.h, sv.n)
                    sv.inc(vector.tensor_scalar(out=arow[:, :], in0=qrow[:, :],
                                                scalar1=1.0 / D, scalar2=None, op0=OP.mult))
                    vector.wait_ge(sv.h, sv.n)
                    vector.wait_ge(sc.h, NF + 1)
                    sv.inc(vector.tensor_tensor(out=vrow[:, :], in0=arow[:, :],
                                                in1=msq[:, :], op=OP.subtract))
                    vector.wait_ge(sc.h, NF + 2)
                    sv.inc(vector.reciprocal(out=rrow[:, :], in_=stdr[:, :]))

                @blkA.scalar
                def _(scalar):
                    for f in range(NF):
                        b = f % 2
                        if n_adds:
                            scalar.wait_ge(sv.h, 1 + f * VA + n_adds)
                            src = xw_b[b]
                        else:
                            scalar.wait_ge(dmx[b].h, (f // 2 + 1) * per_f * 16)
                            src = x_b[b]
                        if f >= 2:
                            scalar.wait_ge(mm.h, f - 1)
                        sc.inc(scalar.activation(out=sq_b[b][:, :], in_=src[:, :], func=AF.Square))
                    sv0 = 1 + NF * VA
                    e_mrow = sv0 + nth + 1
                    e_vrow = e_mrow + nth + 2
                    scalar.wait_ge(sv.h, e_mrow)
                    sc.inc(scalar.activation(out=msq[:, :], in_=mrow[:, :], func=AF.Square))
                    scalar.wait_ge(sv.h, e_vrow)
                    sc.inc(scalar.activation(out=stdr[:, :], in_=vrow[:, :], func=AF.Sqrt, bias=eps_c[:, :]))

                @blkA.tensor
                def _(tensor):
                    for f in range(NF):
                        b = f % 2
                        tensor.wait_ge(sv.h, 1 + (f + 1) * VA)
                        tensor.wait_ge(sc.h, f + 1)
                        for t in range(nth):
                            tensor.matmul(sum_ps[t][:, :], ones_b[:, :], xc_b[b][:, t * tw:(t + 1) * tw],
                                          start=(f == 0), stop=(f == NF - 1))
                            i = tensor.matmul(ssq_ps[t][:, :], ones_b[:, :], sq_b[b][:, t * tw:(t + 1) * tw],
                                              start=(f == 0), stop=(f == NF - 1))
                        mm.inc(i)
                    sv0 = 1 + NF * VA
                    e_mrow = sv0 + nth + 1
                    e_rrow = e_mrow + nth + 3
                    tensor.wait_ge(sv.h, e_mrow)
                    for t in range(nth):
                        mm.inc(tensor.matmul(mbc[t][:, :], ones_r[:, :], mrow[:, t * tw:(t + 1) * tw],
                                             start=True, stop=True))
                    tensor.wait_ge(sv.h, e_rrow)
                    for t in range(nth):
                        mm.inc(tensor.matmul(rbc[t][:, :], ones_r[:, :], rrow[:, t * tw:(t + 1) * tw],
                                             start=True, stop=True))

                @blkA.gpsimd
                def _(gpsimd):
                    if n_adds:
                        for f in range(NF):
                            b = f % 2
                            gpsimd.wait_ge(sv.h, 1 + f * VA + n_adds)
                            gm[b].inc(gpsimd.dma_start(out=xo[f * 128:(f + 1) * 128, :],
                                                       in_=xw_b[b][:, :]), 16)
                        gpsimd.wait_ge(gm[0].h, gm[0].n)
                        gpsimd.wait_ge(gm[1].h, gm[1].n)

            with ExitStack() as eB:
                sbB = lambda n, s, d=F32: eB.enter_context(nc.sbuf_tensor(n, s, d))
                y_b = [sbB(f"y_b{i}", [128, Tc]) for i in range(2)]
                tmp = [sbB(f"tmpB{t}", [128, tw]) for t in range(nth)]
                blkB = eB.enter_context(nc.Block())
                dmb = [Sem(nc, eB, "dmb0"), Sem(nc, eB, "dmb1")]
                svb = Sem(nc, eB, "svB")
                gmb = Sem(nc, eB, "gmB")
                xsrc = xo if n_adds else x
                VB = 2 * nth

                @blkB.sync
                def _(sync):
                    for f in range(NF):
                        b = f % 2
                        if f >= 2:
                            sync.wait_ge(svb.h, (f - 1) * VB)
                        sync.dma_start(out=y_b[b][:, :],
                                       in_=xsrc[f * 128:(f + 1) * 128, :]).then_inc(dmb[b].h, 16)
                    sync.wait_ge(dmb[0].h, (NF // 2) * 16)
                    sync.wait_ge(dmb[1].h, (NF // 2) * 16)
                dmb[0].n = dmb[1].n = (NF // 2) * 16

                @blkB.vector
                def _(vector):
                    for f in range(NF):
                        b = f % 2
                        vector.wait_ge(dmb[b].h, (f // 2 + 1) * 16)
                        for t in range(nth):
                            vector.wait_ge(svb.h, svb.n)
                            svb.inc(vector.tensor_tensor(out=tmp[t][:, :],
                                                         in0=y_b[b][:, t * tw:(t + 1) * tw],
                                                         in1=mbc[t][:, :], op=OP.subtract))
                            vector.wait_ge(svb.h, svb.n)
                            svb.inc(vector.tensor_tensor(
                                out=xn_sb[:, f * Tc + t * tw: f * Tc + (t + 1) * tw],
                                in0=tmp[t][:, :], in1=rbc[t][:, :], op=OP.mult))

                @blkB.gpsimd
                def _(gpsimd):
                    for f in range(NF):
                        gpsimd.wait_ge(svb.h, (f + 1) * VB)
                        gmb.inc(gpsimd.dma_start(out=xnd[f * 128:(f + 1) * 128, :],
                                                 in_=xn_sb[:, f * Tc:(f + 1) * Tc]), 16)
                    gpsimd.wait_ge(gmb.h, gmb.n)

        groups = []
        for c0 in range(0, NF, 4):
            groups.append((wq, qd, c0, "q", rope))
        for c0 in range(0, nkc, 4):
            groups.append((wk, kd, c0, "k", rope))
        for c0 in range(0, nkc, 4):
            groups.append((wv, vd, c0, "v", False))
        boff = {"q": 0, "k": NF, "v": NF + nkc}
        bmap = {"q": bqt, "k": bkt, "v": bvt}

        with ExitStack() as eC:
            sbC = lambda n, s, d=F32: eC.enter_context(nc.sbuf_tensor(n, s, d))
            wf = [sbC(f"wf{i}", [128, 512]) for i in range(2)]
            wb = [sbC(f"wb{i}", [128, 512], BF16) for i in range(2)]
            ob = [sbC(f"obC{i}", [128, tw], BF16) for i in range(4 * nth)]
            rt = [sbC(f"rt{i}", [64, tw]) for i in range(4)]
            bcol = sbC("bcol", [128, NF + 2 * nkc]) if with_bias else None
            pg = [[eC.enter_context(nc.psum_tensor(f"pg{c}_{t}", [128, tw], F32)) for t in range(nth)]
                  for c in range(4)]
            blkC = eC.enter_context(nc.Block())
            dw = [Sem(nc, eC, "dw0"), Sem(nc, eC, "dw1")]
            dbias = Sem(nc, eC, "dbias")
            pw = Sem(nc, eC, "pw")
            mmc = Sem(nc, eC, "mmc")
            evv = Sem(nc, eC, "evv")
            gmc = Sem(nc, eC, "gmc")
            NG = len(groups)
            bias_dmas = 3 if with_bias else 0

            def ev_per_tile(gi):
                return 6 if groups[gi][4] else 1

            ev_before = [0] * (NG + 1)
            for i in range(NG):
                ev_before[i + 1] = ev_before[i] + 4 * nth * ev_per_tile(i)

            @blkC.sync
            def _(sync):
                if with_bias:
                    sync.dma_start(out=bcol[:, 0:NF], in_=bqt[:, :]).then_inc(dbias.h, 16)
                    sync.dma_start(out=bcol[:, NF:NF + nkc], in_=bkt[:, :]).then_inc(dbias.h, 16)
                    sync.dma_start(out=bcol[:, NF + nkc:NF + 2 * nkc], in_=bvt[:, :]).then_inc(dbias.h, 16)
                for gi, (wt, ot, c0, bk_, rp) in enumerate(groups):
                    for f in range(NF):
                        b = f % 2
                        n = gi * NF + f
                        if n >= 2:
                            sync.wait_ge(pw.h, n - 1)
                        sync.dma_start(out=wf[b][:, :],
                                       in_=wt[f * 128:(f + 1) * 128, c0 * 128:c0 * 128 + 512]
                                       ).then_inc(dw[n % 2].h, 16)
                sync.wait_ge(dw[0].h, ((NG * NF + 1) // 2) * 16)
                sync.wait_ge(dw[1].h, (NG * NF // 2) * 16)
            dw[0].n = ((NG * NF + 1) // 2) * 16
            dw[1].n = (NG * NF // 2) * 16

            @blkC.gpsimd
            def _(gpsimd):
                for gi, (wt, ot, c0, bk_, rp) in enumerate(groups):
                    ept = ev_per_tile(gi)
                    for f in range(NF):
                        b = f % 2
                        n = gi * NF + f
                        gpsimd.wait_ge(dw[n % 2].h, (n // 2 + 1) * 16)
                        if n >= 2:
                            gpsimd.wait_ge(mmc.h, n - 1)
                        pw.inc(gpsimd.tensor_copy(out=wb[b][:, :], in_=wf[b][:, :]))
                    for c in range(4):
                        for t in range(nth):
                            k = c * nth + t
                            gpsimd.wait_ge(evv.h, ev_before[gi] + (k + 1) * ept)
                            gmc.inc(gpsimd.dma_start(
                                out=ot[(c0 + c) * 128:(c0 + c + 1) * 128, t * tw:(t + 1) * tw],
                                in_=ob[k][:, :]), 16)
                gpsimd.wait_ge(gmc.h, gmc.n)

            @blkC.tensor
            def _(tensor):
                for gi in range(NG):
                    for f in range(NF):
                        n = gi * NF + f
                        b = f % 2
                        tensor.wait_ge(pw.h, n + 1)
                        if f == 0 and gi > 0:
                            tensor.wait_ge(evv.h, ev_before[gi])
                        for c in range(4):
                            for t in range(nth):
                                i = tensor.matmul(pg[c][t][:, :], wb[b][:, c * 128:(c + 1) * 128],
                                                  xn_sb[:, f * Tc + t * tw: f * Tc + (t + 1) * tw],
                                                  start=(f == 0), stop=(f == NF - 1))
                        mmc.inc(i)

            @blkC.vector
            def _(vector):
                if with_bias:
                    vector.wait_ge(dbias.h, 48)
                for gi, (wt, ot, c0, bk_, rp) in enumerate(groups):
                    ept = ev_per_tile(gi)
                    vector.wait_ge(mmc.h, (gi + 1) * NF)
                    for c in range(4):
                        for t in range(nth):
                            k = c * nth + t
                            if gi > 0 and k == 0:
                                vector.wait_ge(gmc.h, 16 * gi * 4 * nth)
                            p = pg[c][t]
                            if rp:
                                vector.wait_ge(evv.h, evv.n)
                                evv.inc(vector.tensor_tensor(out=rt[0][:, :], in0=p[0:64, :],
                                                             in1=cos_sb[:, t * tw:(t + 1) * tw], op=OP.mult))
                                evv.inc(vector.tensor_tensor(out=rt[1][:, :], in0=p[64:128, :],
                                                             in1=sin_sb[:, t * tw:(t + 1) * tw], op=OP.mult))
                                evv.inc(vector.tensor_tensor(out=rt[2][:, :], in0=p[64:128, :],
                                                             in1=cos_sb[:, t * tw:(t + 1) * tw], op=OP.mult))
                                evv.inc(vector.tensor_tensor(out=rt[3][:, :], in0=p[0:64, :],
                                                             in1=sin_sb[:, t * tw:(t + 1) * tw], op=OP.mult))
                                vector.wait_ge(evv.h, evv.n)
                                evv.inc(vector.tensor_tensor(out=ob[k][0:64, :], in0=rt[0][:, :],
                                                             in1=rt[1][:, :], op=OP.subtract))
                                evv.inc(vector.tensor_tensor(out=ob[k][64:128, :], in0=rt[2][:, :],
                                                             in1=rt[3][:, :], op=OP.add))
                            elif bk_ is not None and bmap[bk_] is not None:
                                evv.inc(vector.tensor_scalar(
                                    out=ob[k][:, :], in0=p[:, :],
                                    scalar1=bcol[:, boff[bk_] + c0 + c: boff[bk_] + c0 + c + 1],
                                    scalar2=None, op0=OP.add))
                            else:
                                evv.inc(vector.tensor_copy(out=ob[k][:, :], in_=p[:, :]))
    return nc


ATTN_MARKS = []
ATTN_TRUNC = None
ATTN_DEBUG = False

def build_attn(T_, n_kv, gsz, scale):
    NQT = T_ // 512
    NCH = T_ // 512
    nc = bass.Bass("TRN2", target_bir_lowering=False)
    kin = [nc.dram_tensor(f"k{h}", [128, T_], BF16, kind="ExternalInput") for h in range(n_kv)]
    vin = [nc.dram_tensor(f"v{h}", [128, T_], BF16, kind="ExternalInput") for h in range(n_kv)]
    qin = [nc.dram_tensor(f"q{i}", [128, T_], BF16, kind="ExternalInput") for i in range(n_kv * gsz)]
    dcold = nc.dram_tensor("dcol", [128, 4], F32, kind="ExternalInput")
    qkiod = nc.dram_tensor("qkio", [128, 512], F32, kind="ExternalInput")
    ctxd = [nc.dram_tensor(f"ctx{i}", [128, T_], BF16, kind="ExternalOutput") for i in range(n_kv * gsz)]
    if ATTN_DEBUG:
        dbgp = [nc.dram_tensor(f"dbgp{i}", [128, 512], BF16, kind="ExternalOutput") for i in range(4)]
        dbgs = [nc.dram_tensor(f"dbgs{i}", [128, 512], F32, kind="ExternalOutput") for i in range(4)]
        dbgrow = nc.dram_tensor("dbgrow", [1, 512], F32, kind="ExternalOutput")
        dbgrr = nc.dram_tensor("dbgrr", [1, 512], F32, kind="ExternalOutput")
        dbgbc = nc.dram_tensor("dbgbc", [128, 512], F32, kind="ExternalOutput")
        dbgctx = nc.dram_tensor("dbgctx", [128, 512], F32, kind="ExternalOutput")
    dbg_idx = [0]

    ev = []

    def E(eng, waits, fn, inc=None, k=1):
        ev.append((eng, [w for w in waits if w is not None and w[1] is not None], fn, inc, k))
        if inc is not None:
            inc.n += k
            return inc.n
        return None

    with ExitStack() as es:
        sb = lambda n, s_, d=F32: es.enter_context(nc.sbuf_tensor(n, s_, d))
        ps_ = lambda n, s_: es.enter_context(nc.psum_tensor(n, s_, F32))
        k_sb = sb("k_sb", [128, T_], BF16)
        v_sb = sb("v_sb", [128, T_], BF16)
        q_sb = [sb(f"q_sb{s_}", [128, T_], BF16) for s_ in range(2)]
        p_b = [[sb(f"p_{s_}{i}", [128, 512], BF16) for i in range(2)] for s_ in range(2)]
        pm_b = [[sb(f"pm_{s_}{i}", [128, 512], BF16) for i in range(2)] for s_ in range(2)]
        cbuf = [[sb(f"cbuf_{s_}{i}", [128, 512], BF16) for i in range(2)] for s_ in range(2)]
        bc_sb = [sb(f"bc_sb{s_}", [128, 512]) for s_ in range(2)]
        bias_sb = [sb(f"bias_sb{s_}", [128, 1]) for s_ in range(2)]
        nsq = [sb(f"nsq{i}", [128, 512], BF16) for i in range(2)]
        nrow_k = sb("nrow_k", [1, NCH])
        nrow_q = [sb(f"nrow_q{s_}", [1, NCH]) for s_ in range(2)]
        nk_t = sb("nk_t", [1, 1])
        nq_t = [sb(f"nq_t{s_}", [1, 1]) for s_ in range(2)]
        prod = [sb(f"prod{s_}", [1, 1]) for s_ in range(2)]
        cmax = [sb(f"cmax{s_}", [1, 1]) for s_ in range(2)]
        negc = [sb(f"negc{s_}", [1, 1]) for s_ in range(2)]
        srow = [sb(f"srow{s_}", [1, 512]) for s_ in range(2)]
        rr = [sb(f"rr{s_}", [1, 512]) for s_ in range(2)]
        dcol = sb("dcol_sb", [128, 4])
        dbgctx_sb = sb("dbgctx_sb", [128, 512]) if ATTN_DEBUG else None
        qkio = sb("qkio_sb", [128, 512])
        ones_b = sb("ones_battn", [128, 1], BF16)
        ones_r = sb("ones_rattn", [1, 128], F32)
        s_ps = [[ps_(f"s_ps{s_}_{i}", [128, 512]) for i in range(2)] for s_ in range(2)]
        ctx_ps = [ps_(f"ctx_ps{s_}", [128, 512]) for s_ in range(2)]
        sum_ps = [ps_(f"sum_ps{s_}", [128, 512]) for s_ in range(2)]
        blk = es.enter_context(nc.Block())
        dk = Sem(nc, es, "dk")
        dq = Sem(nc, es, "dq")
        dc = Sem(nc, es, "dc")
        mm = Sem(nc, es, "mmat")
        se = Sem(nc, es, "seat")
        svv = Sem(nc, es, "svat")
        gma = [[Sem(nc, es, f"gmat{s_}{p_}") for p_ in range(2)] for s_ in range(2)]

        E("vector", [], lambda e: e.memset(ones_b[:, :], 1.0))
        E("vector", [], lambda e: e.memset(ones_r[:, :], 1.0), svv)
        E("sync", [], lambda e: e.dma_start(out=dcol[:, :], in_=dcold[:, :]), dc, 16)
        E("sync", [], lambda e: e.dma_start(out=qkio[:, :], in_=qkiod[:, :]), dc, 16)

        def norms(src_sb, n_row, nt, dwait):
            base_mm = mm.n
            base_sv = svv.n
            mmv, sev, rdv = [], [], []
            for c in range(NCH):
                b = c % 2
                w = [dwait]
                w.append((mm, mmv[c - 2]) if c >= 2 else (mm, base_mm))
                sev.append(E("scalar", w,
                             (lambda cc, bb: lambda e: e.activation(
                                 out=nsq[bb][:, :], in_=src_sb[:, cc * 512:(cc + 1) * 512],
                                 func=AF.Square))(c, b), se))
                w2 = [(se, sev[c])]
                w2.append((svv, rdv[c - 2]) if c >= 2 else (svv, base_sv))
                mmv.append(E("tensor", w2,
                             (lambda bb: lambda e: e.matmul(sum_ps[bb][:, :], ones_b[:, :],
                                                            nsq[bb][:, :], start=True, stop=True))(b), mm))
                rdv.append(E("vector", [(mm, mmv[c])],
                             (lambda cc, bb: lambda e: e.tensor_reduce(
                                 out=n_row[:, cc:cc + 1], in_=sum_ps[bb][:, :],
                                 op=OP.max, axis=AX.X))(c, b), svv))
            return E("vector", [(svv, svv.n)],
                     lambda e: e.tensor_reduce(out=nt[:, :], in_=n_row[:, 0:NCH],
                                               op=OP.max, axis=AX.X), svv)

        prev_exp_last = [None, None]
        prev_bc_c = [None, None]
        prev_norm_v = [None, None]
        prev_sum_v = [None, None]
        dma_v = {}
        first = True

        for kv in range(n_kv):
            ATTN_MARKS.append(("kv", kv, len(ev)))
            wk_ = [] if first else [(mm, mm.n), (se, se.n), (svv, svv.n)]
            E("sync", wk_, (lambda h: lambda e: e.dma_start(out=k_sb[:, :], in_=kin[h][:, :]))(kv), dk, 16)
            E("sync", [], (lambda h: lambda e: e.dma_start(out=v_sb[:, :], in_=vin[h][:, :]))(kv), dk, 16)
            dk_done = (dk, dk.n)
            nkv = norms(k_sb, nrow_k, nk_t, dk_done)
            for p2 in range(gsz // 2):
                hA = kv * gsz + 2 * p2
                hB = hA + 1
                wq_ = [] if first else [(mm, mm.n), (se, se.n)]
                first = False
                E("sync", wq_, (lambda h: lambda e: e.dma_start(out=q_sb[0][:, :], in_=qin[h][:, :]))(hA), dq, 16)
                E("sync", [], (lambda h: lambda e: e.dma_start(out=q_sb[1][:, :], in_=qin[h][:, :]))(hB), dq, 16)
                dq_done = (dq, dq.n)
                nqv = [norms(q_sb[s_], nrow_q[s_], nq_t[s_], dq_done) for s_ in range(2)]
                bias_ready = [None, None]
                for s_ in range(2):
                    v1 = E("vector", [(svv, nqv[s_]), (svv, nkv)],
                           (lambda ss: lambda e: e.tensor_tensor(out=prod[ss][:, :], in0=nk_t[:, :],
                                                                 in1=nq_t[ss][:, :], op=OP.mult))(s_), svv)
                    c1 = E("scalar", [(svv, v1)],
                           (lambda ss: lambda e: e.activation(out=cmax[ss][:, :], in_=prod[ss][:, :],
                                                              func=AF.Sqrt, scale=scale * scale))(s_), se)
                    v2 = E("vector", [(se, c1)],
                           (lambda ss: lambda e: e.tensor_scalar(out=negc[ss][:, :], in0=cmax[ss][:, :],
                                                                 scalar1=-1.0, scalar2=None,
                                                                 op0=OP.mult))(s_), svv)
                    m1 = E("tensor", [(svv, v2), (se, prev_bc_c[s_]), (se, prev_exp_last[s_])],
                           (lambda ss: lambda e: e.matmul(s_ps[ss][0][:, 0:1], ones_r[:, :],
                                                          negc[ss][:, :], start=True, stop=True))(s_), mm)
                    bias_ready[s_] = E("scalar", [(mm, m1)],
                                       (lambda ss: lambda e: e.copy(out=bias_sb[ss][:, :],
                                                                    in_=s_ps[ss][0][:, 0:1]))(s_), se)
                    prev_bc_c[s_] = bias_ready[s_]
                dbg_wait = {}
                for qt in range(NQT):
                    ATTN_MARKS.append(("qt", (kv, p2, qt), len(ev)))
                    nkt = 4 * qt + 4
                    exp_v = {}
                    pv_v = {}
                    psrc = {}
                    pready = {}

                    def emit_pv(i):
                        for s_ in range(2):
                            pbuf = psrc[(s_, i)]
                            w = [pready[(s_, i)]]
                            if i == 0:
                                w += [(svv, prev_norm_v[s_]), (svv, prev_sum_v[s_]), dk_done]
                            E("tensor", w,
                              (lambda ss, ii, nn, pb: lambda e: e.matmul(
                                  ctx_ps[ss][:, :], v_sb[:, ii * 128:(ii + 1) * 128],
                                  pb[:, :], start=(ii == 0), stop=(ii == nn - 1)))(s_, i, nkt, pbuf), mm)
                            r = E("tensor", [],
                                  (lambda ss, ii, nn, pb: lambda e: e.matmul(
                                      sum_ps[ss][:, :], ones_b[:, :],
                                      pb[:, :], start=(ii == 0), stop=(ii == nn - 1)))(s_, i, nkt, pbuf), mm)
                        pv_v[i] = r

                    for i in range(nkt):
                        b = i % 2
                        for s_ in range(2):
                            w = []
                            if i >= 2:
                                w.append((se, exp_v[(s_, i - 2)]))
                            elif b == 0:
                                w.append((se, prev_exp_p0[s_]))
                            else:
                                w.append((se, prev_exp_p1[s_]))
                            if i == 0:
                                w += [dq_done, dk_done]
                            if (("s", i % 2) in dbg_wait) and s_ == 0:
                                w.append(dbg_wait[("s", i % 2)])
                            sv_ = E("tensor", w,
                                    (lambda ss, ii, qq: lambda e: e.matmul(
                                        s_ps[ss][ii % 2][:, :], k_sb[:, ii * 128:(ii + 1) * 128],
                                        q_sb[ss][:, qq * 512:(qq + 1) * 512],
                                        start=True, stop=True))(s_, i, qt), mm)
                            exp_w = [(mm, sv_), (se, bias_ready[s_])]
                            if i >= 2:
                                exp_w.append((mm, pv_v[i - 2]))
                            else:
                                exp_w.append((svv, prev_norm_v[s_]))
                            exp_v[(s_, i)] = E("scalar", exp_w,
                                               (lambda ss, ii: lambda e: e.activation(
                                                   out=p_b[ss][ii % 2][:, :], in_=s_ps[ss][ii % 2][:, :],
                                                   func=AF.Exp, scale=scale, bias=bias_sb[ss][:, :]))(s_, i), se)
                        diag = i >= 4 * qt
                        for s_ in range(2):
                            if diag:
                                o = i - 4 * qt
                                w = [(se, exp_v[(s_, i)]), (dc, 32)]
                                if i >= 2:
                                    w.append((mm, pv_v[i - 2]))
                                else:
                                    w.append((svv, prev_norm_v[s_]))
                                if (("p", i % 2) in dbg_wait) and s_ == 0:
                                    w.append(dbg_wait[("p", i % 2)])
                                mv = E("vector", w,
                                       (lambda ss, ii, oo: lambda e: e.scalar_tensor_tensor(
                                           out=pm_b[ss][ii % 2][:, :], in0=qkio[:, :],
                                           scalar=dcol[:, oo:oo + 1], in1=p_b[ss][ii % 2][:, :],
                                           op0=OP.is_ge, op1=OP.mult))(s_, i, o), svv)
                                psrc[(s_, i)] = pm_b[s_][i % 2]
                                pready[(s_, i)] = (svv, mv)
                                if ATTN_DEBUG and kv == 0 and p2 == 0 and qt == 0 and s_ == 0:
                                    dseq = gma[1][1]
                                    dv = E("gpsimd", [(svv, mv), (dseq, dseq.n)],
                                           (lambda ii: lambda e: e.dma_start(
                                               out=dbgp[ii][:, :], in_=pm_b[0][ii % 2][:, :]))(i),
                                           dseq, 16)
                                    dbg_wait[("p", i % 2)] = (dseq, dv)
                                    wdb = [(mm, sv_), (svv, mv)]
                                    if ("bc1",) in dbg_wait:
                                        wdb.append(dbg_wait[("bc1",)])
                                    cdb = E("scalar", wdb,
                                            (lambda ii: lambda e: e.copy(
                                                out=bc_sb[1][:, :], in_=s_ps[0][ii % 2][:, :]))(i), se)
                                    dv2 = E("gpsimd", [(se, cdb), (dseq, dseq.n)],
                                           (lambda ii: lambda e: e.dma_start(
                                               out=dbgs[ii][:, :], in_=bc_sb[1][:, :]))(i),
                                           dseq, 16)
                                    dbg_wait[("s", i % 2)] = (se, cdb)
                                    dbg_wait[("bc1",)] = (dseq, dv2)
                            else:
                                psrc[(s_, i)] = p_b[s_][i % 2]
                                pready[(s_, i)] = (se, exp_v[(s_, i)])
                        if i >= 1:
                            emit_pv(i - 1)
                    emit_pv(nkt - 1)
                    par = qt % 2
                    for s_ in range(2):
                        wv1 = [(mm, pv_v[nkt - 1])]
                        if s_ == 0 and ("epidma",) in dbg_wait:
                            wv1.append(dbg_wait[("epidma",)])
                        v1 = E("vector", wv1,
                               (lambda ss: lambda e: e.tensor_copy(out=srow[ss][:, :],
                                                                   in_=sum_ps[ss][:, :]))(s_), svv)
                        prev_sum_v[s_] = v1
                        v2 = E("vector", [(svv, v1)],
                               (lambda ss: lambda e: e.reciprocal(out=rr[ss][:, :],
                                                                  in_=srow[ss][:, :]))(s_), svv)
                        m1 = E("tensor", [(svv, v2), (se, exp_v[(s_, nkt - 1)]), (se, exp_v[(s_, nkt - 2)])],
                               (lambda ss: lambda e: e.matmul(s_ps[ss][0][:, :], ones_r[:, :],
                                                              rr[ss][:, :], start=True, stop=True))(s_), mm)
                        wbc = [(mm, m1)]
                        if s_ == 1 and ("bc1",) in dbg_wait:
                            wbc.append(dbg_wait[("bc1",)])
                        c1 = E("scalar", wbc,
                               (lambda ss: lambda e: e.copy(out=bc_sb[ss][:, :],
                                                            in_=s_ps[ss][0][:, :]))(s_), se)
                        prev_bc_c[s_] = c1
                        if ATTN_DEBUG and kv == 0 and p2 == 0 and qt == 0 and s_ == 0:
                            dseq = gma[1][1]
                            E("gpsimd", [(svv, v1), (dseq, dseq.n)],
                              lambda e: e.dma_start(out=dbgrow[:, :], in_=srow[0][:, :]), dseq, 16)
                            E("gpsimd", [(svv, v2), (dseq, dseq.n)],
                              lambda e: e.dma_start(out=dbgrr[:, :], in_=rr[0][:, :]), dseq, 16)
                            E("gpsimd", [(se, c1), (dseq, dseq.n)],
                              lambda e: e.dma_start(out=dbgbc[:, :], in_=bc_sb[0][:, :]), dseq, 16)
                            cdbg = E("scalar", [(mm, pv_v[nkt - 1]), (se, se.n)],
                                     lambda e: e.copy(out=dbgctx_sb[:, :], in_=ctx_ps[0][:, :]), se)
                            dbg_wait[("epidma",)] = (dseq, E("gpsimd", [(se, cdbg), (dseq, dseq.n)],
                              lambda e: e.dma_start(out=dbgctx[:, :], in_=dbgctx_sb[:, :]), dseq, 16))
                        w = [(se, c1)]
                        if (s_, par) in dma_v:
                            w.append((gma[s_][par], dma_v[(s_, par)]))
                        v3 = E("vector", w,
                               (lambda ss, pp: lambda e: e.tensor_tensor(
                                   out=cbuf[ss][pp][:, :], in0=ctx_ps[ss][:, :],
                                   in1=bc_sb[ss][:, :], op=OP.mult))(s_, par), svv)
                        prev_norm_v[s_] = v3
                        h = hA if s_ == 0 else hB
                        dma_v[(s_, par)] = E("gpsimd", [(svv, v3)],
                                             (lambda ss, pp, hh, qq: lambda e: e.dma_start(
                                                 out=ctxd[hh][:, qq * 512:(qq + 1) * 512],
                                                 in_=cbuf[ss][pp][:, :]))(s_, par, h, qt), gma[s_][par], 16)
                        prev_exp_last[s_] = exp_v[(s_, nkt - 1)]

        if ATTN_TRUNC is not None:
            del ev[ATTN_TRUNC:]
        for s_ in range(2):
            for p_ in range(2):
                gv = max([e_[4] if False else 0 for e_ in ev] + [0])
                tot = sum(k_ for (en_, w_, f_, i_, k_) in ev if i_ is gma[s_][p_])
                E("gpsimd", [(gma[s_][p_], tot)], lambda e: e.nop())
        ATTN_MARKS.append(("end", None, len(ev)))

        engines = {"sync": blk.sync, "tensor": blk.tensor, "vector": blk.vector,
                   "scalar": blk.scalar, "gpsimd": blk.gpsimd}
        for name, deco in engines.items():
            def body(eng, _name=name):
                for (en, waits, fn, inc, k) in ev:
                    if en != _name:
                        continue
                    for (sm, val) in waits:
                        eng.wait_ge(sm.h, val)
                    inst = fn(eng)
                    if inc is not None:
                        inst.then_inc(inc.h, k)
            deco(body)
    return nc


def build_ffn0(T_):
    NT = T_ // 512
    nc = bass.Bass("TRN2", target_bir_lowering=False)
    xnd = nc.dram_tensor("xn", [D, T_], BF16, kind="ExternalInput")
    qwd = nc.dram_tensor("qw", [D, 512], I32, kind="ExternalInput")
    sczd = nc.dram_tensor("scz", [1, NF * 512], F32, kind="ExternalInput")
    zrzd = nc.dram_tensor("zrz", [1, NF * 512], F32, kind="ExternalInput")
    bd = nc.dram_tensor("bias", [128, 4], F32, kind="ExternalInput")
    yd = nc.dram_tensor("y", [512, T_], BF16, kind="ExternalOutput")

    with ExitStack() as es:
        sb = lambda n, s_, d=F32: es.enter_context(nc.sbuf_tensor(n, s_, d))
        ones_r = sb("ones_rf", [1, 128], F32)
        wt_all = sb("wt_all", [128, NF * 512], BF16)
        scz = sb("scz_sb", [1, NF * 512])
        zrz = sb("zrz_sb", [1, NF * 512])
        bcol = sb("bcolf", [128, 4])

        with ExitStack() as e1:
            sb1 = lambda n, s_, d=F32: e1.enter_context(nc.sbuf_tensor(n, s_, d))
            qwb = [sb1(f"qwb{i}", [128, 512], I32) for i in range(2)]
            qf = [sb1(f"qf{i}", [128, 512]) for i in range(2)]
            t2 = [sb1(f"t2_{i}", [128, 512]) for i in range(2)]
            psc = [e1.enter_context(nc.psum_tensor(f"psc{i}", [128, 512], F32)) for i in range(2)]
            pzr = [e1.enter_context(nc.psum_tensor(f"pzr{i}", [128, 512], F32)) for i in range(2)]
            blk1 = e1.enter_context(nc.Block())
            dq_ = [Sem(nc, e1, "dqw0"), Sem(nc, e1, "dqw1")]
            drow = Sem(nc, e1, "drow")
            mm1 = Sem(nc, e1, "mmf1")
            sv1 = Sem(nc, e1, "svf1")

            @blk1.sync
            def _(sync):
                sync.dma_start(out=scz[:, :], in_=sczd[:, :]).then_inc(drow.h, 16)
                sync.dma_start(out=zrz[:, :], in_=zrzd[:, :]).then_inc(drow.h, 16)
                sync.dma_start(out=bcol[:, :], in_=bd[:, :]).then_inc(drow.h, 16)
                for g in range(NF):
                    b = g % 2
                    if g >= 2:
                        sync.wait_ge(sv1.h, 3 * (g - 1))
                    sync.dma_start(out=qwb[b][:, :],
                                   in_=qwd[g * 128:(g + 1) * 128, :]).then_inc(dq_[b].h, 16)
                sync.wait_ge(dq_[0].h, (NF // 2) * 16)
                sync.wait_ge(dq_[1].h, (NF // 2) * 16)
                sync.wait_ge(drow.h, 48)

            @blk1.tensor
            def _(tensor):
                tensor.wait_ge(drow.h, 48)
                tensor.wait_ge(sv1.h, 1)
                for g in range(NF):
                    b = g % 2
                    if g >= 2:
                        tensor.wait_ge(sv1.h, 1 + 3 * (g - 1))
                    tensor.matmul(psc[b][:, :], ones_r[:, :], scz[:, g * 512:(g + 1) * 512],
                                  start=True, stop=True)
                    mm1.inc(tensor.matmul(pzr[b][:, :], ones_r[:, :], zrz[:, g * 512:(g + 1) * 512],
                                          start=True, stop=True))

            @blk1.vector
            def _(vector):
                sv1.inc(vector.memset(ones_r[:, :], 1.0))
                for g in range(NF):
                    b = g % 2
                    vector.wait_ge(dq_[b].h, (g // 2 + 1) * 16)
                    sv1.inc(vector.tensor_copy(out=qf[b][:, :], in_=qwb[b][:, :]))
                    vector.wait_ge(mm1.h, g + 1)
                    vector.wait_ge(sv1.h, sv1.n)
                    sv1.inc(vector.tensor_tensor(out=t2[b][:, :], in0=qf[b][:, :],
                                                 in1=psc[b][:, :], op=OP.mult))
                    vector.wait_ge(sv1.h, sv1.n)
                    sv1.inc(vector.tensor_tensor(out=wt_all[:, g * 512:(g + 1) * 512],
                                                 in0=t2[b][:, :], in1=pzr[b][:, :], op=OP.add))

        with ExitStack() as e2:
            sb2 = lambda n, s_, d=F32: e2.enter_context(nc.sbuf_tensor(n, s_, d))
            xb = [sb2(f"xbf{i}", [128, 512], BF16) for i in range(2)]
            ob = [sb2(f"obf{i}", [128, 512], BF16) for i in range(8)]
            pg = [e2.enter_context(nc.psum_tensor(f"pgf{c}", [128, 512], F32)) for c in range(4)]
            blk2 = e2.enter_context(nc.Block())
            dx = [Sem(nc, e2, "dxf0"), Sem(nc, e2, "dxf1")]
            mm2 = Sem(nc, e2, "mmf2")
            ev2 = Sem(nc, e2, "evf2")
            gm2 = Sem(nc, e2, "gmf2")

            @blk2.sync
            def _(sync):
                for t in range(NT):
                    for g in range(NF):
                        n = t * NF + g
                        b = n % 2
                        if n >= 2:
                            sync.wait_ge(mm2.h, n - 1)
                        sync.dma_start(out=xb[b][:, :],
                                       in_=xnd[g * 128:(g + 1) * 128, t * 512:(t + 1) * 512]
                                       ).then_inc(dx[b].h, 16)
                sync.wait_ge(dx[0].h, (NT * NF // 2) * 16)
                sync.wait_ge(dx[1].h, (NT * NF // 2) * 16)

            @blk2.tensor
            def _(tensor):
                for t in range(NT):
                    for g in range(NF):
                        n = t * NF + g
                        b = n % 2
                        tensor.wait_ge(dx[b].h, (n // 2 + 1) * 16)
                        if g == 0 and t > 0:
                            tensor.wait_ge(ev2.h, 4 * t)
                        for c in range(4):
                            i = tensor.matmul(pg[c][:, :], wt_all[:, g * 512 + c * 128: g * 512 + (c + 1) * 128],
                                              xb[b][:, :], start=(g == 0), stop=(g == NF - 1))
                        mm2.inc(i)

            @blk2.vector
            def _(vector):
                for t in range(NT):
                    vector.wait_ge(mm2.h, (t + 1) * NF)
                    for c in range(4):
                        k = (t % 2) * 4 + c
                        if t >= 2:
                            vector.wait_ge(gm2.h, 16 * ((t - 2) * 4 + c + 1))
                        ev2.inc(vector.tensor_scalar(out=ob[k][:, :], in0=pg[c][:, :],
                                                     scalar1=bcol[:, c:c + 1], scalar2=None,
                                                     op0=OP.add))

            @blk2.gpsimd
            def _(gpsimd):
                for t in range(NT):
                    for c in range(4):
                        k = (t % 2) * 4 + c
                        gpsimd.wait_ge(ev2.h, t * 4 + c + 1)
                        gm2.inc(gpsimd.dma_start(out=yd[c * 128:(c + 1) * 128, t * 512:(t + 1) * 512],
                                                 in_=ob[k][:, :]), 16)
                gpsimd.wait_ge(gm2.h, gm2.n)
    return nc


def build_wo_base(T_):
    NT = T_ // 512
    nc = bass.Bass("TRN2", target_bir_lowering=False)
    cxd = nc.dram_tensor("cx", [D, T_], BF16, kind="ExternalInput")
    xnd = nc.dram_tensor("xn", [D, T_], BF16, kind="ExternalInput")
    wod = nc.dram_tensor("wo", [D, 512], F32, kind="ExternalInput")
    wbd = nc.dram_tensor("wb", [D, 512], F32, kind="ExternalInput")
    zd = nc.dram_tensor("z", [512, T_], BF16, kind="ExternalOutput")

    with ExitStack() as es:
        sb = lambda n, s_, d=F32: es.enter_context(nc.sbuf_tensor(n, s_, d))
        wo_all = sb("wo_all", [128, NF * 512], BF16)
        wb_all = sb("wb_all", [128, NF * 512], BF16)

        with ExitStack() as e1:
            sb1 = lambda n, s_, d=F32: e1.enter_context(nc.sbuf_tensor(n, s_, d))
            wf = [sb1(f"wfw{i}", [128, 512]) for i in range(2)]
            blk1 = e1.enter_context(nc.Block())
            dw_ = [Sem(nc, e1, "dww0"), Sem(nc, e1, "dww1")]
            sv1 = Sem(nc, e1, "svw1")

            @blk1.sync
            def _(sync):
                for j in range(2 * NF):
                    b = j % 2
                    src = wod if j < NF else wbd
                    g = j % NF
                    if j >= 2:
                        sync.wait_ge(sv1.h, j - 1)
                    sync.dma_start(out=wf[b][:, :],
                                   in_=src[g * 128:(g + 1) * 128, :]).then_inc(dw_[b].h, 16)
                sync.wait_ge(dw_[0].h, NF * 16)
                sync.wait_ge(dw_[1].h, NF * 16)

            @blk1.vector
            def _(vector):
                for j in range(2 * NF):
                    b = j % 2
                    dst = wo_all if j < NF else wb_all
                    g = j % NF
                    vector.wait_ge(dw_[b].h, (j // 2 + 1) * 16)
                    sv1.inc(vector.tensor_copy(out=dst[:, g * 512:(g + 1) * 512], in_=wf[b][:, :]))

        with ExitStack() as e2:
            sb2 = lambda n, s_, d=F32: e2.enter_context(nc.sbuf_tensor(n, s_, d))
            cb = [sb2(f"cbw{i}", [128, 512], BF16) for i in range(2)]
            xb = [sb2(f"xbw{i}", [128, 512], BF16) for i in range(2)]
            ob = [sb2(f"obw{i}", [128, 512], BF16) for i in range(8)]
            pg = [e2.enter_context(nc.psum_tensor(f"pgw{c}", [128, 512], F32)) for c in range(4)]
            blk2 = e2.enter_context(nc.Block())
            dc_ = [Sem(nc, e2, "dcw0"), Sem(nc, e2, "dcw1")]
            dx_ = [Sem(nc, e2, "dxw0"), Sem(nc, e2, "dxw1")]
            mm2 = Sem(nc, e2, "mmw2")
            ev2 = Sem(nc, e2, "evw2")
            gm2 = Sem(nc, e2, "gmw2")

            @blk2.sync
            def _(sync):
                for t in range(NT):
                    for g in range(NF):
                        n = t * NF + g
                        b = n % 2
                        if n >= 2:
                            sync.wait_ge(mm2.h, n - 1)
                        sync.dma_start(out=cb[b][:, :],
                                       in_=cxd[g * 128:(g + 1) * 128, t * 512:(t + 1) * 512]
                                       ).then_inc(dc_[b].h, 16)
                        sync.dma_start(out=xb[b][:, :],
                                       in_=xnd[g * 128:(g + 1) * 128, t * 512:(t + 1) * 512]
                                       ).then_inc(dx_[b].h, 16)
                sync.wait_ge(dc_[0].h, (NT * NF // 2) * 16)
                sync.wait_ge(dc_[1].h, (NT * NF // 2) * 16)
                sync.wait_ge(dx_[0].h, (NT * NF // 2) * 16)
                sync.wait_ge(dx_[1].h, (NT * NF // 2) * 16)

            @blk2.tensor
            def _(tensor):
                for t in range(NT):
                    for g in range(NF):
                        n = t * NF + g
                        b = n % 2
                        tensor.wait_ge(dc_[b].h, (n // 2 + 1) * 16)
                        tensor.wait_ge(dx_[b].h, (n // 2 + 1) * 16)
                        if g == 0 and t > 0:
                            tensor.wait_ge(ev2.h, 4 * t)
                        for c in range(4):
                            tensor.matmul(pg[c][:, :], wo_all[:, g * 512 + c * 128: g * 512 + (c + 1) * 128],
                                          cb[b][:, :], start=(g == 0), stop=False)
                            i = tensor.matmul(pg[c][:, :], wb_all[:, g * 512 + c * 128: g * 512 + (c + 1) * 128],
                                              xb[b][:, :], start=False, stop=(g == NF - 1))
                        mm2.inc(i)

            @blk2.vector
            def _(vector):
                for t in range(NT):
                    vector.wait_ge(mm2.h, (t + 1) * NF)
                    for c in range(4):
                        k = (t % 2) * 4 + c
                        if t >= 2:
                            vector.wait_ge(gm2.h, 16 * ((t - 2) * 4 + c + 1))
                        ev2.inc(vector.tensor_copy(out=ob[k][:, :], in_=pg[c][:, :]))

            @blk2.gpsimd
            def _(gpsimd):
                for t in range(NT):
                    for c in range(4):
                        k = (t % 2) * 4 + c
                        gpsimd.wait_ge(ev2.h, t * 4 + c + 1)
                        gm2.inc(gpsimd.dma_start(out=zd[c * 128:(c + 1) * 128, t * 512:(t + 1) * 512],
                                                 in_=ob[k][:, :]), 16)
                gpsimd.wait_ge(gm2.h, gm2.n)
    return nc


def build_lora(Tc):
    NTC = Tc // 128
    twL = min(512, Tc)
    nth = Tc // twL
    nc = bass.Bass("TRN2", target_bir_lowering=False)
    xnd = nc.dram_tensor("xn", [D, Tc], BF16, kind="ExternalInput")
    gwd = nc.dram_tensor("gw", [D, 28], F32, kind="ExternalInput")
    lad = nc.dram_tensor("la", [D, 224], F32, kind="ExternalInput")
    lb1d = nc.dram_tensor("lb1", [128, D], F32, kind="ExternalInput")
    lb2d = nc.dram_tensor("lb2", [96, D], F32, kind="ExternalInput")
    sel1d = nc.dram_tensor("sel1", [28, 128], BF16, kind="ExternalInput")
    sel2d = nc.dram_tensor("sel2", [28, 96], BF16, kind="ExternalInput")
    identd = nc.dram_tensor("ident", [128, 128], BF16, kind="ExternalInput")
    lod = nc.dram_tensor("lo", [D, Tc], BF16, kind="ExternalOutput")

    S = Sched()
    with ExitStack() as es:
        sb = lambda n, s_, d=F32: es.enter_context(nc.sbuf_tensor(n, s_, d))
        ps_ = lambda n, s_, d=F32: es.enter_context(nc.psum_tensor(n, s_, d))
        xn_sb = sb("xn_sbL", [128, NF * Tc], BF16)
        gwb = sb("gwb", [128, NF * 28], BF16)
        lab = sb("lab", [128, NF * 224], BF16)
        lb1b = sb("lb1b", [128, D], BF16)
        lb2b = sb("lb2b", [96, D], BF16)
        sel1 = sb("sel1_sb", [28, 128], BF16)
        sel2 = sb("sel2_sb", [28, 96], BF16)
        ident = sb("ident_sb", [128, 128], BF16)
        gt_sb = sb("gt_sb", [28, Tc], BF16)
        hs = [sb("hs1", [128, Tc], BF16), sb("hs2", [96, Tc], BF16)]
        ge_sb = [sb("ge_sb1", [128, Tc]), sb("ge_sb2", [96, Tc])]
        wf = [sb(f"wfl{i}", [128, 512]) for i in range(2)]
        lg = sb("lg", [128, 28])
        m1 = sb("m1c", [128, 1]); m2 = sb("m2c", [128, 1])
        eq1 = sb("eq1", [128, 28]); eq2 = sb("eq2", [128, 28])
        l2m = sb("l2m", [128, 28]); dd = sb("ddc", [128, 1])
        p2c = sb("p2c", [128, 1]); p1c = sb("p1c", [128, 1])
        g1t = sb("g1t", [128, 28]); gat = sb("gat", [128, 28])
        gab = sb("gab", [128, 28], BF16)
        ob = [sb(f"obl{i}", [128, 512], BF16) for i in range(4)]
        dl = Sem(nc, es, "dll")
        dwl = [Sem(nc, es, "dwl0"), Sem(nc, es, "dwl1")]
        svl = Sem(nc, es, "svl")
        scl = Sem(nc, es, "scl")
        mml = Sem(nc, es, "mml")
        gml = [Sem(nc, es, "gml0"), Sem(nc, es, "gml1")]

        S.E("sync", [], lambda e: e.dma_start(
            out=xn_sb[:, :], in_=bass.AP(xnd, 0, [[Tc, 128], [128 * Tc, NF], [1, Tc]])), dl, 16)
        S.E("sync", [], lambda e: e.dma_start(out=sel1[:, :], in_=sel1d[:, :]), dl, 16)
        S.E("sync", [], lambda e: e.dma_start(out=sel2[:, :], in_=sel2d[:, :]), dl, 16)
        S.E("sync", [], lambda e: e.dma_start(out=ident[:, :], in_=identd[:, :]), dl, 16)
        dl_done = (dl, dl.n)
        jobs = []
        for g in range(NF):
            jobs.append((gwd, g * 128, 0, gwb, g * 28, 28, 128))
        for g in range(NF):
            jobs.append((lad, g * 128, 0, lab, g * 224, 224, 128))
        for t in range(D // 512):
            jobs.append((lb1d, 0, t * 512, lb1b, t * 512, 512, 128))
        for t in range(D // 512):
            jobs.append((lb2d, 0, t * 512, lb2b, t * 512, 512, 96))
        castv = []
        for j, (src, soff, coff, dst, doff, w, rows) in enumerate(jobs):
            b = j % 2
            wv = [] if j < 2 else [(svl, castv[j - 2])]
            dv = S.E("sync", wv, (lambda sr, so, co, ww, rr, bb: lambda e: e.dma_start(
                out=wf[bb][0:rr, 0:ww], in_=sr[so:so + rr, co:co + ww]))(src, soff, coff, w, rows, b),
                dwl[b], 16)
            castv.append(S.E("vector", [(dwl[b], dv)],
                             (lambda ds, do, ww, rr, bb: lambda e: e.tensor_copy(
                                 out=ds[0:rr, do:do + ww], in_=wf[bb][0:rr, 0:ww]))(dst, doff, w, rows, b),
                             svl))
        cast_gw = castv[NF - 1]
        cast_la = castv[2 * NF - 1]
        cast_lb = castv[-1]

        es_gate = ExitStack()
        pl = es_gate.enter_context(nc.psum_tensor("plL", [128, 28], F32))
        pt = es_gate.enter_context(nc.psum_tensor("ptL", [28, 128], BF16))
        prev_lg = None
        prev_tp = None
        prev_gtc = None
        for tch in range(NTC):
            wv = [dl_done, (svl, cast_gw)]
            if prev_lg is not None:
                wv.append((scl, prev_lg))
            for g in range(NF):
                i = S.E("tensor", wv if g == 0 else [],
                        (lambda gg, tt: lambda e: e.matmul(
                            pl[:, :], xn_sb[:, gg * Tc + tt * 128: gg * Tc + (tt + 1) * 128],
                            gwb[:, gg * 28:(gg + 1) * 28],
                            start=(gg == 0), stop=(gg == NF - 1)))(g, tch), mml)
            m_pl = i
            prev_lg = S.E("scalar", [(mml, m_pl)],
                          lambda e: e.copy(out=lg[:, :], in_=pl[:, :]), scl)
            v = S.E("vector", [(scl, prev_lg), (svl, svl.n)],
                    lambda e: e.tensor_reduce(out=m1[:, :], in_=lg[:, :], op=OP.max, axis=AX.X), svl)
            v = S.E("vector", [(svl, v)],
                    lambda e: e.tensor_scalar(out=eq1[:, :], in0=lg[:, :], scalar1=m1[:, :],
                                              scalar2=None, op0=OP.is_equal), svl)
            v = S.E("vector", [(svl, v)],
                    lambda e: e.scalar_tensor_tensor(out=l2m[:, :], in0=eq1[:, :], scalar=-1e30,
                                                     in1=lg[:, :], op0=OP.mult, op1=OP.add), svl)
            v = S.E("vector", [(svl, v)],
                    lambda e: e.tensor_reduce(out=m2[:, :], in_=l2m[:, :], op=OP.max, axis=AX.X), svl)
            v_dd = S.E("vector", [(svl, v)],
                       lambda e: e.tensor_tensor(out=dd[:, :], in0=m2[:, :], in1=m1[:, :],
                                                 op=OP.subtract), svl)
            c_p2 = S.E("scalar", [(svl, v_dd)],
                       lambda e: e.activation(out=p2c[:, :], in_=dd[:, :], func=AF.Sigmoid), scl)
            v = S.E("vector", [(scl, c_p2)],
                    lambda e: e.tensor_scalar(out=p1c[:, :], in0=p2c[:, :], scalar1=-1.0,
                                              scalar2=1.0, op0=OP.mult, op1=OP.add), svl)
            v = S.E("vector", [(svl, v)],
                    lambda e: e.tensor_scalar(out=eq2[:, :], in0=l2m[:, :], scalar1=m2[:, :],
                                              scalar2=None, op0=OP.is_equal), svl)
            v = S.E("vector", [(svl, v)],
                    lambda e: e.tensor_scalar(out=g1t[:, :], in0=eq1[:, :], scalar1=p1c[:, :],
                                              scalar2=None, op0=OP.mult), svl)
            v = S.E("vector", [(svl, v)],
                    lambda e: e.scalar_tensor_tensor(out=gat[:, :], in0=eq2[:, :], scalar=p2c[:, :],
                                                     in1=g1t[:, :], op0=OP.mult, op1=OP.add), svl)
            wv = [(svl, v)]
            if prev_tp is not None:
                wv.append((mml, prev_tp))
            v_gab = S.E("vector", wv, lambda e: e.tensor_copy(out=gab[:, :], in_=gat[:, :]), svl)
            wv = [(svl, v_gab), dl_done]
            if prev_gtc is not None:
                wv.append((scl, prev_gtc))
            prev_tp = S.E("tensor", wv,
                          lambda e: e.transpose(pt[:, :], gab[:, :], ident[:, :]), mml)
            prev_gtc = S.E("scalar", [(mml, prev_tp)],
                           (lambda tt: lambda e: e.copy(out=gt_sb[:, tt * 128:(tt + 1) * 128],
                                                        in_=pt[:, :]))(tch), scl)
        gate_done = prev_gtc
        es_gate.close()

        es_h = ExitStack()
        widths = [128, 96]
        ph = [[es_h.enter_context(nc.psum_tensor(f"phL{er}_{t}", [widths[er], twL], F32))
               for t in range(nth)] for er in range(2)]
        pge = [[es_h.enter_context(nc.psum_tensor(f"pgeL{er}_{t}", [widths[er], twL], F32))
                for t in range(nth)] for er in range(2)]
        sels = [sel1, sel2]
        hs_v = []
        for er in range(2):
            wdt = widths[er]
            for t in range(nth):
                for g in range(NF):
                    i = S.E("tensor", [dl_done, (svl, cast_la), (scl, gate_done)] if g == 0 else [],
                            (lambda ee, tt, gg, ww: lambda e: e.matmul(
                                ph[ee][tt][:, :],
                                lab[:, gg * 224 + ee * 128: gg * 224 + ee * 128 + ww],
                                xn_sb[:, gg * Tc + tt * twL: gg * Tc + (tt + 1) * twL],
                                start=(gg == 0), stop=(gg == NF - 1)))(er, t, g, wdt), mml)
                m_ph = i
                m_ge = S.E("tensor", [(scl, gate_done)],
                           (lambda ee, tt, ww: lambda e: e.matmul(
                               pge[ee][tt][:, :], sels[ee][:, 0:ww],
                               gt_sb[:, tt * twL:(tt + 1) * twL],
                               start=True, stop=True))(er, t, wdt), mml)
                c_ge = S.E("scalar", [(mml, m_ge)],
                           (lambda ee, tt, ww: lambda e: e.copy(
                               out=ge_sb[ee][0:ww, tt * twL:(tt + 1) * twL],
                               in_=pge[ee][tt][:, :]))(er, t, wdt), scl)
                hs_v.append(S.E("vector", [(mml, m_ph), (scl, c_ge), (svl, svl.n)],
                                (lambda ee, tt, ww: lambda e: e.tensor_tensor(
                                    out=hs[ee][0:ww, tt * twL:(tt + 1) * twL],
                                    in0=ph[ee][tt][:, :],
                                    in1=ge_sb[ee][0:ww, tt * twL:(tt + 1) * twL],
                                    op=OP.mult))(er, t, wdt), svl))
        hs_done = hs_v[-1]
        es_h.close()

        plo = [ps_(f"ploL{i}", [128, twL]) for i in range(2)]
        prev_ev = {}
        dma_vL = {}
        for oc in range(NF):
            for t in range(nth):
                n = oc * nth + t
                b = n % 2
                wv = [(svl, hs_done), (svl, cast_lb)]
                if b in prev_ev and prev_ev[b] is not None:
                    wv.append((svl, prev_ev[b]))
                S.E("tensor", wv,
                    (lambda o_, tt: lambda e: e.matmul(
                        plo[(o_ * nth + tt) % 2][:, :], lb1b[:, o_ * 128:(o_ + 1) * 128],
                        hs[0][:, tt * twL:(tt + 1) * twL], start=True, stop=False))(oc, t), mml)
                m_lo = S.E("tensor", [],
                           (lambda o_, tt: lambda e: e.matmul(
                               plo[(o_ * nth + tt) % 2][:, :], lb2b[:, o_ * 128:(o_ + 1) * 128],
                               hs[1][:, tt * twL:(tt + 1) * twL], start=False, stop=True))(oc, t), mml)
                wv = [(mml, m_lo)]
                if (b,) in dma_vL:
                    wv.append((gml[b], dma_vL[(b,)]))
                ve = S.E("vector", wv,
                         (lambda bb: lambda e: e.tensor_copy(out=ob[bb][:, 0:twL],
                                                             in_=plo[bb][:, :]))(b), svl)
                prev_ev[b] = ve
                dma_vL[(b,)] = S.E("gpsimd", [(svl, ve)],
                                   (lambda o_, tt, bb: lambda e: e.dma_start(
                                       out=lod[o_ * 128:(o_ + 1) * 128, tt * twL:(tt + 1) * twL],
                                       in_=ob[bb][:, 0:twL]))(oc, t, b), gml[b], 16)
        S.E("gpsimd", [(gml[0], gml[0].n)], lambda e: e.nop())
        S.E("gpsimd", [(gml[1], gml[1].n)], lambda e: e.nop())

        blk = es.enter_context(nc.Block())
        S.emit(blk)
    return nc


def build_final(Tc):
    nc = bass.Bass("TRN2", target_bir_lowering=False)
    xd = nc.dram_tensor("x", [D, Tc], F32, kind="ExternalInput")
    zd = nc.dram_tensor("z", [D, Tc], BF16, kind="ExternalInput")
    ld = nc.dram_tensor("lo", [D, Tc], BF16, kind="ExternalInput")
    od = nc.dram_tensor("out", [D, Tc], F32, kind="ExternalOutput")
    with ExitStack() as es:
        sb = lambda n, s_, d=F32: es.enter_context(nc.sbuf_tensor(n, s_, d))
        x_b = [sb(f"x_bF{i}", [128, Tc]) for i in range(2)]
        z_b = [sb(f"z_bF{i}", [128, Tc], BF16) for i in range(2)]
        l_b = [sb(f"l_bF{i}", [128, Tc], BF16) for i in range(2)]
        o_b = [sb(f"o_bF{i}", [128, Tc]) for i in range(2)]
        blk = es.enter_context(nc.Block())
        dm = [Sem(nc, es, "dmF0"), Sem(nc, es, "dmF1")]
        sv = Sem(nc, es, "svF")
        gm = [Sem(nc, es, "gmF0"), Sem(nc, es, "gmF1")]

        @blk.sync
        def _(sync):
            for f in range(NF):
                b = f % 2
                if f >= 2:
                    sync.wait_ge(sv.h, 2 * (f - 1))
                sync.dma_start(out=x_b[b][:, :], in_=xd[f * 128:(f + 1) * 128, :]).then_inc(dm[b].h, 16)
                sync.dma_start(out=z_b[b][:, :], in_=zd[f * 128:(f + 1) * 128, :]).then_inc(dm[b].h, 16)
                sync.dma_start(out=l_b[b][:, :], in_=ld[f * 128:(f + 1) * 128, :]).then_inc(dm[b].h, 16)
            sync.wait_ge(dm[0].h, (NF // 2) * 48)
            sync.wait_ge(dm[1].h, (NF // 2) * 48)

        @blk.vector
        def _(vector):
            for f in range(NF):
                b = f % 2
                vector.wait_ge(dm[b].h, (f // 2 + 1) * 48)
                if f >= 2:
                    vector.wait_ge(gm[b].h, (f // 2) * 16)
                sv.inc(vector.tensor_tensor(out=o_b[b][:, :], in0=x_b[b][:, :],
                                            in1=z_b[b][:, :], op=OP.add))
                vector.wait_ge(sv.h, sv.n)
                sv.inc(vector.tensor_tensor(out=o_b[b][:, :], in0=o_b[b][:, :],
                                            in1=l_b[b][:, :], op=OP.add))

        @blk.gpsimd
        def _(gpsimd):
            for f in range(NF):
                b = f % 2
                gpsimd.wait_ge(sv.h, 2 * (f + 1))
                gm[b].inc(gpsimd.dma_start(out=od[f * 128:(f + 1) * 128, :],
                                           in_=o_b[b][:, :]), 16)
            gpsimd.wait_ge(gm[0].h, gm[0].n)
            gpsimd.wait_ge(gm[1].h, gm[1].n)
    return nc


from concourse.aot_env import allow_ici_all_to_all


class Mega:
    def __init__(self):
        self.nc = bass.Bass("TRN2", target_bir_lowering=False)
        self.S = Sched()
        self.es = ExitStack()
        self._sems = {}

    def sem(self, name):
        if name not in self._sems:
            self._sems[name] = Sem(self.nc, self.es, name)
        return self._sems[name]

    def gate(self):
        return [(sm, sm.n) for name, sm in self._sems.items()
                if sm.n > 0 and not name.startswith("cc_")]


def _q_rowloc(L, h):
    if L == 1:
        return 128 * h
    c = (h % 16) // 2
    i = 2 * ((h % 16) % 2) + (h // 16)
    return 512 * c + 128 * i


def emit_front(m, L, x_ap, add_ap_fn, wq, wk, wvt, kvd, with_bias, bias_ts, rope,
               cs_ts, xn_sb, xn_stage, xo, qs, ks, vs, add_waits=(), hook_k=None, hook_q=None):
    nc, S = m.nc, m.S
    Tc = TC
    tw = min(512, Tc)
    nth = Tc // tw
    ntk = Tc // 128
    nkc = kvd // 128
    n_adds = 2 if add_ap_fn else 0
    sfx = f"F{L}"
    dA = [m.sem("ds0"), m.sem("ds1")]
    dB = [m.sem("ds2"), m.sem("ds3")]
    dW = [m.sem(f"ds{i}") for i in range(4, 8)]
    drt = m.sem("d4x")
    sv = m.sem("sv")
    sc = m.sem("sc")
    mm = m.sem("mm")
    gq = m.sem("gq")
    gk = m.sem("gk")
    gv = m.sem("gv")
    gxn = m.sem("gxn")
    gx = [m.sem("gx0"), m.sem("gx1")]

    with ExitStack() as eF:
        sb = lambda n, s_, d=F32: eF.enter_context(nc.sbuf_tensor(n + sfx, s_, d))
        ones_b = sb("ones_b", [128, 1], BF16)
        ones_r = sb("ones_r", [1, 128], F32)
        onesc = sb("onesc", [1, 128], BF16)
        eps_c = sb("eps_c", [1, 1], F32)
        mrow = sb("mrow", [1, Tc]); arow = sb("arow", [1, Tc]); rrow = sb("rrow", [1, Tc])
        srow = sb("srow", [1, Tc]); qrow = sb("qrow", [1, Tc])
        msq = sb("msq", [1, Tc]); vrow = sb("vrow", [1, Tc]); stdr = sb("stdr", [1, Tc])
        x_b = [sb(f"x_b{i}", [128, Tc]) for i in range(2)]
        a_b = [[sb(f"a_b{j}_{i}", [128, Tc], BF16) for i in range(2)] for j in range(n_adds)]
        xw_b = [sb(f"xw_b{i}", [128, Tc]) for i in range(2)] if n_adds else None
        xc_b = [sb(f"xc_b{i}", [128, Tc], BF16) for i in range(2)]
        sq_b = [sb(f"sq_b{i}", [128, Tc], BF16) for i in range(2)]
        y_b = x_b
        tmpB = [sb(f"tmpB{t}", [128, tw]) for t in range(nth)]
        wf4 = [sb(f"wf4{i}", [128, 2048]) for i in range(3)]
        wb4 = [sb(f"wb4{i}", [128, 2048], BF16) for i in range(3)]
        ob = [sb(f"ob{i}", [128, tw], BF16) for i in range(4 * nth)]
        vo = [sb(f"vo{i}", [128, 512], BF16) for i in range(2)]
        rt = [sb(f"rt{i}", [64, tw]) for i in range(4)]
        if with_bias:
            bcol = sb("bcol", [128, NF + nkc])
            bvrow = sb("bvrow", [1, kvd], BF16)
            bvrow_f = sb("bvrow_f", [1, kvd])
        if rope:
            cos_sb = sb("cos_sb", [64, Tc])
            sin_sb = sb("sin_sb", [64, Tc])

        m.S.start_phase(m.gate())

        S.E("vector", [], lambda e: e.memset(ones_b[:, :], 1.0))
        S.E("vector", [], lambda e: e.memset(ones_r[:, :], 1.0))
        sv_ones = S.E("vector", [], lambda e: e.memset(onesc[:, :], 1.0), sv)
        sv_eps = S.E("vector", [], lambda e: e.memset(eps_c[:, :], 1e-5), sv)
        if rope:
            S.E("sync", [], lambda e: e.dma_start(out=cos_sb[:, :], in_=cs_ts[0][:, :]), drt, 16)
            S.E("sync", [], lambda e: e.dma_start(out=sin_sb[:, :], in_=cs_ts[1][:, :]), drt, 16)
        sv_bv = None
        if with_bias:
            S.E("sync", [], lambda e: e.dma_start(out=bcol[:, 0:NF], in_=bias_ts[0][:, :]), drt, 16)
            S.E("sync", [], lambda e: e.dma_start(out=bcol[:, NF:NF + nkc], in_=bias_ts[1][:, :]), drt, 16)
            S.E("sync", [], lambda e: e.dma_start(out=bvrow_f[:, :], in_=bias_ts[2][:, :]), drt, 16)
            sv_bv = S.E("vector", [(drt, drt.n)],
                        lambda e: e.tensor_copy(out=bvrow[:, :], in_=bvrow_f[:, :]), sv)
        drt_all = (drt, drt.n)

        with ExitStack() as eAB:
            psA = lambda n, s_: eAB.enter_context(nc.psum_tensor(n + sfx, s_, F32))
            sum_ps = [psA(f"sum_ps{t}", [1, tw]) for t in range(nth)]
            ssq_ps = [psA(f"ssq_ps{t}", [1, tw]) for t in range(nth)]
            mbc = [psA(f"mbc{t}", [128, tw]) for t in range(nth)]
            rbc = [psA(f"rbc{t}", [128, tw]) for t in range(nth)]

            sq_v = []; cast_v = []; mm_v = []
            for f in range(NF):
                b = f % 2
                wv_ = list(add_waits) if f == 0 else []
                if f >= 2:
                    wv_.append((sv, cast_v[f - 2]))
                    wv_.append((sc, sq_v[f - 2]))
                S.E("sync", wv_, (lambda ff, bb: lambda e: e.dma_start(
                    out=x_b[bb][:, :], in_=x_ap[ff * 128:(ff + 1) * 128, :]))(f, b), dA[b], 16)
                for j in range(n_adds):
                    S.E("sync", [], (lambda ff, bb, jj: lambda e: e.dma_start(
                        out=a_b[jj][bb][:, :], in_=add_ap_fn(jj, ff)))(f, b, j), dA[b], 16)
                dcur = (dA[b], dA[b].n)
                if n_adds:
                    wv_ = [dcur]
                    if f >= 2:
                        wv_.append((gx[b], 16 * (f // 2)))
                    v = S.E("vector", wv_, (lambda bb: lambda e: e.tensor_tensor(
                        out=xw_b[bb][:, :], in0=x_b[bb][:, :], in1=a_b[0][bb][:, :], op=OP.add))(b), sv)
                    v = S.E("vector", [(sv, v)], (lambda bb: lambda e: e.tensor_tensor(
                        out=xw_b[bb][:, :], in0=xw_b[bb][:, :], in1=a_b[1][bb][:, :], op=OP.add))(b), sv)
                    S.E("gpsimd", [(sv, v)], (lambda ff, bb: lambda e: e.dma_start(
                        out=xo[ff * 128:(ff + 1) * 128, :], in_=xw_b[bb][:, :]))(f, b), gx[b], 16)
                    src = xw_b[b]
                    src_rdy = [(sv, v)]
                else:
                    src = x_b[b]
                    src_rdy = [dcur]
                sq_v.append(S.E("scalar", src_rdy + ([(mm, mm_v[f - 2])] if f >= 2 else []),
                                (lambda bb, ss: lambda e: e.activation(
                                    out=sq_b[bb][:, :], in_=ss[:, :], func=AF.Square))(b, src), sc))
                cast_v.append(S.E("vector", src_rdy + ([(mm, mm_v[f - 2])] if f >= 2 else []),
                                  (lambda bb, ss: lambda e: e.tensor_copy(
                                      out=xc_b[bb][:, :], in_=ss[:, :]))(b, src), sv))
                wv_ = [(sv, cast_v[f]), (sc, sq_v[f])]
                for t in range(nth):
                    S.E("tensor", wv_ if t == 0 else [],
                        (lambda ff, bb, tt: lambda e: e.matmul(
                            sum_ps[tt][:, :], ones_b[:, :], xc_b[bb][:, tt * tw:(tt + 1) * tw],
                            start=(ff == 0), stop=(ff == NF - 1)))(f, b, t))
                    last = (t == nth - 1)
                    mv = S.E("tensor", [],
                             (lambda ff, bb, tt: lambda e: e.matmul(
                                 ssq_ps[tt][:, :], ones_b[:, :], sq_b[bb][:, tt * tw:(tt + 1) * tw],
                                 start=(ff == 0), stop=(ff == NF - 1)))(f, b, t),
                             mm if last else None)
                mm_v.append(mv)
            v = None
            for t in range(nth):
                v = S.E("vector", [(mm, mm_v[-1])] if t == 0 else [],
                        (lambda tt: lambda e: e.tensor_copy(out=srow[:, tt * tw:(tt + 1) * tw],
                                                            in_=sum_ps[tt][:, :]))(t), sv)
            v_m = S.E("vector", [(sv, v)], lambda e: e.tensor_scalar(
                out=mrow[:, :], in0=srow[:, :], scalar1=1.0 / D, scalar2=None, op0=OP.mult), sv)
            for t in range(nth):
                v = S.E("vector", [],
                        (lambda tt: lambda e: e.tensor_copy(out=qrow[:, tt * tw:(tt + 1) * tw],
                                                            in_=ssq_ps[tt][:, :]))(t), sv)
            v_a = S.E("vector", [(sv, v)], lambda e: e.tensor_scalar(
                out=arow[:, :], in0=qrow[:, :], scalar1=1.0 / D, scalar2=None, op0=OP.mult), sv)
            c_msq = S.E("scalar", [(sv, v_m)], lambda e: e.activation(
                out=msq[:, :], in_=mrow[:, :], func=AF.Square), sc)
            v_vr = S.E("vector", [(sv, v_a), (sc, c_msq)], lambda e: e.tensor_tensor(
                out=vrow[:, :], in0=arow[:, :], in1=msq[:, :], op=OP.subtract), sv)
            c_std = S.E("scalar", [(sv, v_vr), (sv, sv_eps)], lambda e: e.activation(
                out=stdr[:, :], in_=vrow[:, :], func=AF.Sqrt, bias=eps_c[:, :]), sc)
            v_rr = S.E("vector", [(sc, c_std)], lambda e: e.reciprocal(out=rrow[:, :], in_=stdr[:, :]), sv)
            bc_last = None
            for t in range(nth):
                S.E("tensor", [(sv, v_m)] if t == 0 else [],
                    (lambda tt: lambda e: e.matmul(mbc[tt][:, :], ones_r[:, :],
                                                   mrow[:, tt * tw:(tt + 1) * tw], start=True, stop=True))(t))
            for t in range(nth):
                bc_last = S.E("tensor", [(sv, v_rr)] if t == 0 else [],
                              (lambda tt: lambda e: e.matmul(rbc[tt][:, :], ones_r[:, :],
                                                             rrow[:, tt * tw:(tt + 1) * tw],
                                                             start=True, stop=True))(t),
                              mm if t == nth - 1 else None)

            xsrc = xo if n_adds else x_ap
            xn_v = []
            for f in range(NF):
                b = f % 2
                wv_ = []
                if n_adds:
                    wv_.append((gx[b], 16 * (f // 2 + 1)))
                if f < 2:
                    wv_ += [(sv, cast_v[-1]), (sc, sq_v[-1])]
                if f >= 2:
                    wv_.append((sv, xn_v[f - 2]))
                S.E("sync", wv_, (lambda ff, bb: lambda e: e.dma_start(
                    out=y_b[bb][:, :], in_=xsrc[ff * 128:(ff + 1) * 128, :]))(f, b), dB[b], 16)
                dcur = (dB[b], dB[b].n)
                v = None
                for t in range(nth):
                    wv_ = ([dcur, (mm, bc_last)] if t == 0 else []) + [(sv, sv.n)]
                    v0 = S.E("vector", wv_,
                             (lambda bb, tt: lambda e: e.tensor_tensor(
                                 out=tmpB[tt][:, :], in0=y_b[bb][:, tt * tw:(tt + 1) * tw],
                                 in1=mbc[tt][:, :], op=OP.subtract))(b, t), sv)
                    v = S.E("vector", [(sv, v0)],
                            (lambda ff, tt: lambda e: e.tensor_tensor(
                                out=xn_sb[:, ff * Tc + tt * tw: ff * Tc + (tt + 1) * tw],
                                in0=tmpB[tt][:, :], in1=rbc[tt][:, :], op=OP.mult))(f, t), sv)
                xn_v.append(v)
                S.E("gpsimd", [(sv, v)],
                    (lambda ff: lambda e: e.dma_start(
                        out=xn_stage[ff * 128:(ff + 1) * 128, :],
                        in_=xn_sb[:, ff * Tc:(ff + 1) * Tc]))(f), gxn, 16)
            xn_done_v = xn_v[-1]
            xn_done_sv = xn_done_v
        xn_staged = (gxn, gxn.n)

        npg = max(4 * nth, ntk)
        with ExitStack() as eC:
            pg = [eC.enter_context(nc.psum_tensor(f"pg{c}{sfx}", [128, 512], F32))
                  for c in range(npg)]
            groups = [("k", wk, c0) for c0 in range(0, nkc, 2)] + \
                     [("q", wq, c0) for c0 in range(0, NF, 2)]
            NG = len(groups)
            cast_w = []; mmg_v = []; last_ev = None
            prev_ev_bank = [None] * npg
            gpos = 2 * nth
            NB = NF // 8
            for gi, (kind, wt, c0) in enumerate(groups):
                if kind == "q" and gi == NG - (NF // 2) and hook_k is not None:
                    hook_k()
                gsem = gq if kind == "q" else gk
                gq_before = gq.n
                gk_before = gk.n
                half = (gi % 2) * gpos
                wcols = wt.shape[1]
                for f8 in range(NB):
                    n4 = gi * NB + f8
                    bw = n4 % 3
                    wv_ = [] if n4 < 3 else [(sv, cast_w[n4 - 3])]
                    dv = S.E("sync", wv_, (lambda wt_, ff, cc, bb, wc: lambda e: e.dma_start(
                        out=wf4[bb][:, :],
                        in_=bass.AP(wt_, (ff * 8 * 128) * wc + cc * 128,
                                    [[wc, 128], [128 * wc, 8], [1, 256]])
                    ))(wt, f8, c0, bw, wcols), dW[bw], 16)
                    wv_ = [(dW[bw], dv)]
                    if n4 >= 3:
                        wv_.append((mm, mmg_v[8 * (n4 - 3) + 7]))
                    cast_w.append(S.E("vector", wv_, (lambda bb: lambda e: e.tensor_copy(
                        out=wb4[bb][:, :], in_=wf4[bb][:, :]))(bw), sv))
                    for j in range(8):
                        f = f8 * 8 + j
                        wv_ = [(sv, cast_w[n4])] if j == 0 else []
                        if f == 0:
                            wv_.append((sv, xn_done_v))
                        mv = None
                        for c in range(2):
                            for t in range(nth):
                                last = (c == 1 and t == nth - 1)
                                w2 = list(wv_) if (c == 0 and t == 0) else []
                                if f == 0 and prev_ev_bank[half + c * nth + t] is not None:
                                    w2.append((sv, prev_ev_bank[half + c * nth + t]))
                                mv = S.E("tensor", w2,
                                         (lambda ff, bb, jj, cc, tt, hh: lambda e: e.matmul(
                                             pg[hh + cc * nth + tt][:, 0:tw],
                                             wb4[bb][:, jj * 256 + cc * 128: jj * 256 + (cc + 1) * 128],
                                             xn_sb[:, ff * Tc + tt * tw: ff * Tc + (tt + 1) * tw],
                                             start=(ff == 0), stop=(ff == NF - 1)))(f, bw, j, c, t, half),
                                         mm if last else None)
                        mmg_v.append(mv)
                prev_full = (gsem, gq_before if kind == "q" else gk_before)
                for c in range(2):
                    h = c0 + c
                    for t in range(nth):
                        k = c * nth + t
                        wv_ = [(mm, mmg_v[-1])]
                        if k == 0:
                            wv_.append(prev_full)
                            wv_.append((gq, gq_before))
                            wv_.append((gk, gk_before))
                        p = pg[half + k]
                        if rope:
                            v = S.E("vector", wv_ + [(sv, sv.n), drt_all],
                                    (lambda pp, tt: lambda e: e.tensor_tensor(
                                        out=rt[0][:, :], in0=pp[0:64, 0:tw],
                                        in1=cos_sb[:, tt * tw:(tt + 1) * tw], op=OP.mult))(p, t), sv)
                            S.E("vector", [], (lambda pp, tt: lambda e: e.tensor_tensor(
                                out=rt[1][:, :], in0=pp[64:128, 0:tw],
                                in1=sin_sb[:, tt * tw:(tt + 1) * tw], op=OP.mult))(p, t), sv)
                            S.E("vector", [], (lambda pp, tt: lambda e: e.tensor_tensor(
                                out=rt[2][:, :], in0=pp[64:128, 0:tw],
                                in1=cos_sb[:, tt * tw:(tt + 1) * tw], op=OP.mult))(p, t), sv)
                            v = S.E("vector", [], (lambda pp, tt: lambda e: e.tensor_tensor(
                                out=rt[3][:, :], in0=pp[0:64, 0:tw],
                                in1=sin_sb[:, tt * tw:(tt + 1) * tw], op=OP.mult))(p, t), sv)
                            v = S.E("vector", [(sv, v)], (lambda kk: lambda e: e.tensor_tensor(
                                out=ob[kk][0:64, :], in0=rt[0][:, :], in1=rt[1][:, :],
                                op=OP.subtract))(k), sv)
                            v = S.E("vector", [], (lambda kk: lambda e: e.tensor_tensor(
                                out=ob[kk][64:128, :], in0=rt[2][:, :], in1=rt[3][:, :],
                                op=OP.add))(k), sv)
                        elif with_bias:
                            off = (0 if kind == "q" else NF) + h
                            v = S.E("vector", wv_ + [drt_all],
                                    (lambda pp, kk, oo: lambda e: e.tensor_scalar(
                                        out=ob[kk][:, :], in0=pp[:, 0:tw],
                                        scalar1=bcol[:, oo:oo + 1], scalar2=None, op0=OP.add))(p, k, off), sv)
                        else:
                            v = S.E("vector", wv_, (lambda pp, kk: lambda e: e.tensor_copy(
                                out=ob[kk][:, :], in_=pp[:, 0:tw]))(p, k), sv)
                        last_ev = v
                        prev_ev_bank[half + k] = v
                        row = _q_rowloc(L, h) if kind == "q" else 128 * h
                        dst = qs if kind == "q" else ks
                        S.E("gpsimd", [(sv, v)],
                            (lambda dd, rr, tt, kk: lambda e: e.dma_start(
                                out=dd[rr:rr + 128, tt * tw:(tt + 1) * tw],
                                in_=ob[kk][:, :]))(dst, row, t, k), gsem, 16)
            if hook_q is not None:
                hook_q()
            qk_staged = [(gq, gq.n), (gk, gk.n)]

            vcols = kvd // NCORE
            owners = 512 // vcols
            v_last_ev = None
            for vg in range(kvd // 512):
                gv_before = gv.n
                mv = None
                wcols = wvt.shape[1]
                for f4 in range(NF // 4):
                    n4 = len(cast_w)
                    bw = n4 % 3
                    wv_ = [(sv, cast_w[n4 - 3])] if n4 >= 3 else []
                    dv = S.E("sync", wv_, (lambda ff, cc, bb, wc: lambda e: e.dma_start(
                        out=wf4[bb][:, :],
                        in_=bass.AP(wvt, (ff * 4 * 128) * wc + cc * 512,
                                    [[wc, 128], [128 * wc, 4], [1, 512]])
                    ))(f4, vg, bw, wcols), dW[bw], 16)
                    wv_ = [(dW[bw], dv)]
                    if len(mmg_v) >= 9:
                        wv_.append((mm, mmg_v[-9]))
                    cast_w.append(S.E("vector", wv_, (lambda bb: lambda e: e.tensor_copy(
                        out=wb4[bb][:, :], in_=wf4[bb][:, :]))(bw), sv))
                    cw = cast_w[-1]
                    for j in range(4):
                        f = f4 * 4 + j
                        wv_ = [(sv, cw)] if j == 0 else []
                        for tk in range(ntk):
                            last_tk = (tk == ntk - 1)
                            w2 = list(wv_) if tk == 0 else []
                            if f == 0:
                                bk_ev = prev_ev_bank[tk] if v_last_ev is None else v_last_ev
                                if bk_ev is not None:
                                    w2.append((sv, bk_ev))
                            mv = S.E("tensor", w2,
                                     (lambda ff, bb, jj, kk: lambda e: e.matmul(
                                         pg[kk][:, :], xn_sb[:, ff * Tc + kk * 128: ff * Tc + (kk + 1) * 128],
                                         wb4[bb][:, jj * 512:(jj + 1) * 512], start=(ff == 0),
                                         stop=(ff == NF - 1 and not with_bias)))(f, bw, j, tk),
                                     mm if last_tk else None)
                        mmg_v.append(mv)
                if with_bias:
                    for tk in range(ntk):
                        last = (tk == ntk - 1)
                        mv = S.E("tensor", [(sv, sv_bv), (sv, sv_ones)] if tk == 0 else [],
                                 (lambda kk, cc: lambda e: e.matmul(
                                     pg[kk][:, :], onesc[:, :],
                                     bvrow[:, cc * 512:(cc + 1) * 512], start=False, stop=True))(tk, vg),
                                 mm if last else None)
                    mmg_v[-1] = mv
                m_end = mmg_v[-1]
                for tk in range(ntk):
                    b2 = tk % 2
                    wv_ = [(mm, m_end)]
                    wv_.append((gv, gv_before + 16 * owners * tk))
                    v = S.E("vector", wv_, (lambda kk, bb: lambda e: e.tensor_copy(
                        out=vo[bb][:, :], in_=pg[kk][:, :]))(tk, b2), sv)
                    v_last_ev = v
                    for oo in range(owners):
                        owner = (vg * 512) // vcols + oo
                        S.E("gpsimd", [(sv, v)],
                            (lambda ow, kk, bb, o2: lambda e: e.dma_start(
                                out=vs[ow * Tc + kk * 128: ow * Tc + (kk + 1) * 128, :],
                                in_=vo[bb][:, o2 * vcols:(o2 + 1) * vcols]))(owner, tk, b2, oo),
                            gv, 16)
            v_staged = (gv, gv.n)
    return dict(xn_staged=xn_staged, qk_staged=qk_staged, v_staged=v_staged,
                xn_done=(sv, xn_done_sv), xo_done=[(gx[0], gx[0].n), (gx[1], gx[1].n)],
                sems=dict(sv=sv, sc=sc, mm=mm))


def emit_coll(m, name, kind, ins, outs, waits):
    cc = m.sem("cc_" + name)

    def fn(e):
        with allow_ici_all_to_all():
            return e.collective_compute(
                kind, mybir.AluOpType.bypass,
                replica_groups=[list(range(NCORE))],
                ins=[t.ap().opt() for t in ins],
                outs=[t.ap().opt() for t in outs])
    v = m.S.E("gpsimd", waits, fn, cc, 1)
    return (cc, v)


def emit_attn(m, L, T_, n_kv, gsz, scale, ka, va, qa, dcol, qkio, ctx_dst, cc_waits):
    nc, S = m.nc, m.S
    Tc = TC
    NQT = T_ // 512
    NCH = T_ // 512
    sfx = f"A{L}"
    dk = m.sem("ds0")
    dq = m.sem("ds1")
    mm = m.sem("mm")
    se = m.sem("sc")
    svv = m.sem("sv")
    gma = [[m.sem(f"gma{s_}{p_}") for p_ in range(2)] for s_ in range(2)]
    vcols = n_kv * 128
    with ExitStack() as es:
        sb = lambda n, s_, d=F32: es.enter_context(nc.sbuf_tensor(n + sfx, s_, d))
        ps_ = lambda n, s_: es.enter_context(nc.psum_tensor(n + sfx, s_, F32))
        k_sb = sb("k_sb", [128, T_], BF16)
        v_sb = sb("v_sb", [128, T_], BF16)
        q_sb = [sb(f"q_sb{s_}", [128, T_], BF16) for s_ in range(2)]
        p_b = [[sb(f"p_{s_}{i}", [128, 512], BF16) for i in range(2)] for s_ in range(2)]
        pm_b = [[sb(f"pm_{s_}{i}", [128, 512], BF16) for i in range(2)] for s_ in range(2)]
        cbuf = [[sb(f"cbuf_{s_}{i}", [128, 512], BF16) for i in range(2)] for s_ in range(2)]
        bc_sb = [sb(f"bc_sb{s_}", [128, 512]) for s_ in range(2)]
        bias_sb = [sb(f"bias_sb{s_}", [128, 1]) for s_ in range(2)]
        nsq = [sb(f"nsq{i}", [128, 512], BF16) for i in range(2)]
        nrow_k = sb("nrow_k", [1, NCH])
        nrow_q = [sb(f"nrow_q{s_}", [1, NCH]) for s_ in range(2)]
        nk_t = sb("nk_t", [1, 1])
        nq_t = [sb(f"nq_t{s_}", [1, 1]) for s_ in range(2)]
        prod = [sb(f"prod{s_}", [1, 1]) for s_ in range(2)]
        cmax = [sb(f"cmax{s_}", [1, 1]) for s_ in range(2)]
        negc = [sb(f"negc{s_}", [1, 1]) for s_ in range(2)]
        srow = [sb(f"srow{s_}", [1, 512]) for s_ in range(2)]
        rr = [sb(f"rr{s_}", [1, 512]) for s_ in range(2)]
        lnr = [sb(f"lnr{s_}", [1, 512]) for s_ in range(2)]
        ones_b = sb("ones_b", [128, 1], BF16)
        ones_r = sb("ones_r", [1, 128], F32)
        s_ps = [[ps_(f"s_ps{s_}_{i}", [128, 512]) for i in range(2)] for s_ in range(2)]
        ctx_ps = [ps_(f"ctx_ps{s_}", [128, 512]) for s_ in range(2)]
        sum_ps = [ps_(f"sum_ps{s_}", [128, 512]) for s_ in range(2)]

        S.start_phase(m.gate())
        S.E("vector", [], lambda e: e.memset(ones_b[:, :], 1.0))
        v_ones = S.E("vector", [], lambda e: e.memset(ones_r[:, :], 1.0), svv)

        def norms(src_sb, n_row, nt, dwait):
            base_mm = mm.n
            base_sv = svv.n
            mmv, sev, rdv = [], [], []
            for c in range(NCH):
                b = c % 2
                w = [dwait]
                w.append((mm, mmv[c - 2]) if c >= 2 else (mm, base_mm))
                sev.append(S.E("scalar", w,
                               (lambda cc, bb: lambda e: e.activation(
                                   out=nsq[bb][:, :], in_=src_sb[:, cc * 512:(cc + 1) * 512],
                                   func=AF.Square))(c, b), se))
                w2 = [(se, sev[c]), (svv, v_ones)]
                w2.append((svv, rdv[c - 2]) if c >= 2 else (svv, base_sv))
                mmv.append(S.E("tensor", w2,
                               (lambda bb: lambda e: e.matmul(sum_ps[bb][0:1, :], ones_b[:, :],
                                                              nsq[bb][:, :], start=True, stop=True))(b), mm))
                rdv.append(S.E("vector", [(mm, mmv[c])],
                               (lambda cc, bb: lambda e: e.tensor_reduce(
                                   out=n_row[:, cc:cc + 1], in_=sum_ps[bb][0:1, :],
                                   op=OP.max, axis=AX.X))(c, b), svv))
            return S.E("vector", [(svv, svv.n)],
                       lambda e: e.tensor_reduce(out=nt[:, :], in_=n_row[:, 0:NCH],
                                                 op=OP.max, axis=AX.X), svv)

        prev_exp_p0 = [None, None]
        prev_exp_p1 = [None, None]
        prev_bc1 = [None, None]
        prev_norm_v = [None, None]
        prev_sum_v = [None, None]
        dma_v = {}
        first = True
        for kv in range(n_kv):
            wk_ = list(cc_waits) if first else [(mm, mm.n), (se, se.n), (svv, svv.n)]
            S.E("sync", wk_, (lambda h: lambda e: e.dma_start(
                out=k_sb[:, :],
                in_=bass.AP(ka, h * 128 * Tc,
                            [[Tc, 128], [n_kv * 128 * Tc, NCORE], [1, Tc]])))(kv), dk, 16)
            S.E("sync", [], (lambda h: lambda e: e.dma_start(
                out=v_sb[:, :],
                in_=bass.AP(va, h * 128,
                            [[vcols, 128], [128 * vcols, T_ // 128], [1, 128]])))(kv), dk, 16)
            dk_done = (dk, dk.n)
            nkv = norms(k_sb, nrow_k, nk_t, dk_done)
            for p2 in range(gsz // 2):
                wq_ = [] if first else [(mm, mm.n), (se, se.n)]
                first = False
                for s_ in range(2):
                    hloc = kv * gsz + 2 * p2 + s_
                    S.E("sync", wq_ if s_ == 0 else [],
                        (lambda hh, ss: lambda e: e.dma_start(
                            out=q_sb[ss][:, :],
                            in_=bass.AP(qa, hh * 128 * Tc,
                                        [[Tc, 128], [512 * Tc, NCORE], [1, Tc]])))(hloc, s_),
                        dq, 16)
                dq_done = (dq, dq.n)
                nqv = [norms(q_sb[s_], nrow_q[s_], nq_t[s_], dq_done) for s_ in range(2)]
                bias_ready = [None, None]
                for s_ in range(2):
                    v1 = S.E("vector", [(svv, nqv[s_]), (svv, nkv)],
                             (lambda ss: lambda e: e.tensor_tensor(out=prod[ss][:, :], in0=nk_t[:, :],
                                                                   in1=nq_t[ss][:, :], op=OP.mult))(s_), svv)
                    c1 = S.E("scalar", [(svv, v1)],
                             (lambda ss: lambda e: e.activation(out=cmax[ss][:, :], in_=prod[ss][:, :],
                                                                func=AF.Sqrt, scale=scale * scale))(s_), se)
                    v2 = S.E("vector", [(se, c1)],
                             (lambda ss: lambda e: e.tensor_scalar(out=negc[ss][:, :], in0=cmax[ss][:, :],
                                                                   scalar1=-1.0, scalar2=None,
                                                                   op0=OP.mult))(s_), svv)
                    m1 = S.E("tensor", [(svv, v2), (se, prev_bc1[s_])],
                             (lambda ss: lambda e: e.matmul(sum_ps[ss][:, 0:1], ones_r[:, :],
                                                            negc[ss][:, :], start=True, stop=True))(s_), mm)
                    bias_ready[s_] = S.E("scalar", [(mm, m1)],
                                         (lambda ss: lambda e: e.copy(out=bias_sb[ss][:, :],
                                                                      in_=sum_ps[ss][:, 0:1]))(s_), se)
                for qt in range(NQT):
                    nkt = 4 * qt + 4
                    exp_v = {}
                    pv_v = {}
                    psrc = {}
                    pready = {}

                    def emit_pv(i):
                        r = None
                        for s_ in range(2):
                            pbuf = psrc[(s_, i)]
                            w = [pready[(s_, i)]]
                            if i == 0:
                                w += [(svv, prev_norm_v[s_]), (svv, prev_sum_v[s_]), dk_done]
                            S.E("tensor", w,
                                (lambda ss, ii, nn, pb: lambda e: e.matmul(
                                    ctx_ps[ss][:, :], v_sb[:, ii * 128:(ii + 1) * 128],
                                    pb[:, :], start=(ii == 0), stop=(ii == nn - 1)))(s_, i, nkt, pbuf))
                            r = S.E("tensor",
                                    [(se, bias_ready[s_]), (se, prev_bc1[s_])] if i == 0 else [],
                                    (lambda ss, ii, nn, pb: lambda e: e.matmul(
                                        sum_ps[ss][0:1, :], ones_b[:, :],
                                        pb[:, :], start=(ii == 0), stop=(ii == nn - 1)))(s_, i, nkt, pbuf), mm)
                        pv_v[i] = r

                    for i in range(nkt):
                        b = i % 2
                        for s_ in range(2):
                            w = []
                            if i >= 2:
                                w.append((se, exp_v[(s_, i - 2)]))
                            elif b == 0:
                                w.append((se, prev_exp_p0[s_]))
                            else:
                                w.append((se, prev_exp_p1[s_]))
                            if i == 0:
                                w += [dq_done, dk_done]
                            sv_ = S.E("tensor", w,
                                      (lambda ss, ii, qq: lambda e: e.matmul(
                                          s_ps[ss][ii % 2][:, :], k_sb[:, ii * 128:(ii + 1) * 128],
                                          q_sb[ss][:, qq * 512:(qq + 1) * 512],
                                          start=True, stop=True))(s_, i, qt), mm)
                            exp_w = [(mm, sv_), (se, bias_ready[s_])]
                            if i >= 2:
                                exp_w.append((mm, pv_v[i - 2]))
                            else:
                                exp_w.append((svv, prev_norm_v[s_]))
                            exp_v[(s_, i)] = S.E("scalar", exp_w,
                                                 (lambda ss, ii: lambda e: e.activation(
                                                     out=p_b[ss][ii % 2][:, :], in_=s_ps[ss][ii % 2][:, :],
                                                     func=AF.Exp, scale=scale, bias=bias_sb[ss][:, :]))(s_, i), se)
                        diag = i >= 4 * qt
                        for s_ in range(2):
                            if diag:
                                o = i - 4 * qt
                                w = [(se, exp_v[(s_, i)])]
                                if i >= 2:
                                    w.append((mm, pv_v[i - 2]))
                                else:
                                    w.append((svv, prev_norm_v[s_]))
                                mv = S.E("vector", w,
                                         (lambda ss, ii, oo: lambda e: e.scalar_tensor_tensor(
                                             out=pm_b[ss][ii % 2][:, :], in0=qkio[:, :],
                                             scalar=dcol[:, oo:oo + 1], in1=p_b[ss][ii % 2][:, :],
                                             op0=OP.is_ge, op1=OP.mult))(s_, i, o), svv)
                                psrc[(s_, i)] = pm_b[s_][i % 2]
                                pready[(s_, i)] = (svv, mv)
                            else:
                                psrc[(s_, i)] = p_b[s_][i % 2]
                                pready[(s_, i)] = (se, exp_v[(s_, i)])
                        if i >= 1:
                            emit_pv(i - 1)
                    emit_pv(nkt - 1)
                    par = qt % 2
                    for s_ in range(2):
                        ihead = kv * gsz + 2 * p2 + s_
                        v1 = S.E("vector", [(mm, pv_v[nkt - 1])],
                                 (lambda ss: lambda e: e.tensor_copy(out=srow[ss][:, :],
                                                                     in_=sum_ps[ss][0:1, :]))(s_), svv)
                        prev_sum_v[s_] = v1
                        c_ln = S.E("scalar", [(svv, v1)],
                                   (lambda ss: lambda e: e.activation(
                                       out=lnr[ss][:, :], in_=srow[ss][:, :], func=AF.Ln))(s_), se)
                        v2 = S.E("scalar", [(se, c_ln)],
                                 (lambda ss: lambda e: e.activation(
                                     out=rr[ss][:, :], in_=lnr[ss][:, :], func=AF.Exp,
                                     scale=-1.0))(s_), se)
                        m1 = S.E("tensor", [(se, v2)],
                                 (lambda ss: lambda e: e.matmul(sum_ps[ss][:, :], ones_r[:, :],
                                                                rr[ss][:, :], start=True, stop=True))(s_), mm)
                        c1 = S.E("scalar", [(mm, m1)],
                                 (lambda ss: lambda e: e.copy(out=bc_sb[ss][:, :],
                                                              in_=sum_ps[ss][:, :]))(s_), se)
                        prev_bc1[s_] = c1
                        w = [(se, c1)]
                        if (s_, par) in dma_v:
                            w.append((gma[s_][par], dma_v[(s_, par)]))
                        v3 = S.E("vector", w,
                                 (lambda ss, pp: lambda e: e.tensor_tensor(
                                     out=cbuf[ss][pp][:, :], in0=ctx_ps[ss][:, :],
                                     in1=bc_sb[ss][:, :], op=OP.mult))(s_, par), svv)
                        prev_norm_v[s_] = v3
                        dma_v[(s_, par)] = S.E("gpsimd", [(svv, v3)],
                                               (lambda ih, pp, ss, qq: lambda e: e.dma_start(
                                                   out=ctx_dst(ih, qq),
                                                   in_=cbuf[ss][pp][:, :]))(ihead, par, s_, qt),
                                               gma[s_][par], 16)
                        prev_exp_p0[s_] = exp_v[(s_, nkt - 2)]
                        prev_exp_p1[s_] = exp_v[(s_, nkt - 1)]
    staged = [(gma[s_][p_], gma[s_][p_].n) for s_ in range(2) for p_ in range(2)]
    return dict(staged=staged)


def emit_ffn0(m, qw, scz, zrz, fb, xg, ys, cc_xg):
    nc, S = m.nc, m.S
    Tc = TC
    nthc = Tc // 512
    NT = T // 512
    sfx = "N0"
    dqw = [m.sem("ds0"), m.sem("ds1")]
    drow = m.sem("d4x")
    dx = [m.sem(f"ds{i}") for i in range(2, 6)]
    sv = m.sem("sv")
    mm = m.sem("mm")
    gy = m.sem("gy")
    with ExitStack() as es:
        sb = lambda n, s_, d=F32: es.enter_context(nc.sbuf_tensor(n + sfx, s_, d))
        ones_r = sb("ones_r", [1, 128], F32)
        wt_all = sb("wt_all", [128, NF * 512], BF16)
        scz_sb = sb("scz_sb", [1, NF * 512])
        zrz_sb = sb("zrz_sb", [1, NF * 512])
        bcol = sb("bcol", [128, 4])
        qwb = [sb(f"qwb{i}", [128, 512], I32) for i in range(2)]
        qf = [sb(f"qf{i}", [128, 512]) for i in range(2)]
        t2 = [sb(f"t2_{i}", [128, 512]) for i in range(2)]
        xb = [sb(f"xbf{i}", [128, 2048], BF16) for i in range(4)]
        ob = [sb(f"obf{i}", [128, 512], BF16) for i in range(8)]
        S.start_phase(m.gate())
        pgf = [es.enter_context(nc.psum_tensor(f"pgf{c}{sfx}", [128, 512], F32)) for c in range(8)]
        with ExitStack() as e1:
            psc = [pgf[4], pgf[5]]
            pzr = [pgf[6], pgf[7]]
            S.E("sync", [], lambda e: e.dma_start(out=scz_sb[:, :], in_=scz[:, :]), drow, 16)
            S.E("sync", [], lambda e: e.dma_start(out=zrz_sb[:, :], in_=zrz[:, :]), drow, 16)
            S.E("sync", [], lambda e: e.dma_start(out=bcol[:, :], in_=fb[:, :]), drow, 16)
            v_ones = S.E("vector", [], lambda e: e.memset(ones_r[:, :], 1.0), sv)
            drow_all = (drow, drow.n)
            dq_v = []; mm_v = []; w_v = []
            for g in range(NF):
                b = g % 2
                wv_ = [(sv, w_v[g - 2])] if g >= 2 else []
                dv = S.E("sync", wv_, (lambda gg, bb: lambda e: e.dma_start(
                    out=qwb[bb][:, :], in_=qw[gg * 128:(gg + 1) * 128, :]))(g, b), dqw[b], 16)
                mv = S.E("tensor", [drow_all, (sv, v_ones)] +
                         ([(sv, w_v[g - 2])] if g >= 2 else []),
                         (lambda gg, bb: lambda e: e.matmul(
                             psc[bb][:, :], ones_r[:, :], scz_sb[:, gg * 512:(gg + 1) * 512],
                             start=True, stop=True))(g, b))
                mv = S.E("tensor", [],
                         (lambda gg, bb: lambda e: e.matmul(
                             pzr[bb][:, :], ones_r[:, :], zrz_sb[:, gg * 512:(gg + 1) * 512],
                             start=True, stop=True))(g, b), mm)
                mm_v.append(mv)
                v = S.E("vector", [(dqw[b], dv)], (lambda bb: lambda e: e.tensor_copy(
                    out=qf[bb][:, :], in_=qwb[bb][:, :]))(b), sv)
                v = S.E("vector", [(sv, v), (mm, mv)], (lambda bb: lambda e: e.tensor_tensor(
                    out=t2[bb][:, :], in0=qf[bb][:, :], in1=psc[bb][:, :], op=OP.mult))(b), sv)
                v = S.E("vector", [(sv, v)], (lambda gg, bb: lambda e: e.tensor_tensor(
                    out=wt_all[:, gg * 512:(gg + 1) * 512], in0=t2[bb][:, :], in1=pzr[bb][:, :],
                    op=OP.add))(g, b), sv)
                w_v.append(v)
            dequant_done = w_v[-1]
        with ExitStack() as e2:
            ev_v = []; stg = []; mmx_v = []
            for tt in range(NT):
                j = tt // nthc
                lcol = (tt % nthc) * 512
                pb0 = (tt % 2) * 4
                mv = None
                for g4 in range(NF // 4):
                    n4 = tt * (NF // 4) + g4
                    bx = n4 % 4
                    wv_ = [(mm, mmx_v[4 * (n4 - 4) + 3])] if n4 >= 4 else [cc_xg]
                    dv = S.E("sync", wv_, (lambda jj, gg, ll, bb: lambda e: e.dma_start(
                        out=xb[bb][:, :],
                        in_=bass.AP(xg, (jj * D + gg * 4 * 128) * TC + ll,
                                    [[TC, 128], [128 * TC, 4], [1, 512]])))(j, g4, lcol, bx),
                        dx[bx], 16)
                    for jg in range(4):
                        g = g4 * 4 + jg
                        wv_ = [(dx[bx], dv)] if jg == 0 else []
                        if tt == 0 and jg == 0:
                            wv_.append((sv, w_v[min(g4 * 4 + 3, NF - 1)]))
                        if g == 0 and tt > 1:
                            wv_.append((sv, ev_v[4 * (tt - 2) + 3]))
                        for c in range(4):
                            last = (c == 3)
                            mv = S.E("tensor", wv_ if c == 0 else [],
                                     (lambda gg, bb, jj2, cc, pp: lambda e: e.matmul(
                                         pgf[pp + cc][:, :], wt_all[:, gg * 512 + cc * 128: gg * 512 + (cc + 1) * 128],
                                         xb[bb][:, jj2 * 512:(jj2 + 1) * 512],
                                         start=(gg == 0), stop=(gg == NF - 1)))(g, bx, jg, c, pb0),
                                     mm if last else None)
                        mmx_v.append(mv)
                for c in range(4):
                    k = (tt % 2) * 4 + c
                    wv_ = [(mm, mmx_v[-1]), drow_all]
                    if c == 0 and tt >= 2:
                        wv_.append((gy, gy.n))
                    v = S.E("vector", wv_, (lambda kk, cc, pp: lambda e: e.tensor_scalar(
                        out=ob[kk][:, :], in0=pgf[pp + cc][:, :], scalar1=bcol[:, cc:cc + 1],
                        scalar2=None, op0=OP.add))(k, c, pb0), sv)
                    ev_v.append(v)
                    stg.append(S.E("gpsimd", [(sv, v)],
                                   (lambda jj, cc, ll, kk: lambda e: e.dma_start(
                                       out=ys[jj * 512 + cc * 128: jj * 512 + (cc + 1) * 128, ll:ll + 512],
                                       in_=ob[kk][:, :]))(j, c, lcol, k), gy, 16))
    return dict(staged=(gy, gy.n))


def emit_wo_cast(m, wo, wb_, wo_all, wb_all, wf4w):
    nc, S = m.nc, m.S
    dwc = m.sem("cc_wocd")
    pwc = m.sem("cc_wocc")
    cast_v = []
    for j in range(2 * (NF // 4)):
        b = j % 2
        src = wb_ if j < NF // 4 else wo
        dst = wb_all if j < NF // 4 else wo_all
        g4 = j % (NF // 4)
        wv_ = [(pwc, cast_v[j - 2])] if j >= 2 else []
        dv = S.E("gpsimd", wv_, (lambda sr, gg, bb: lambda e: e.dma_start(
            out=wf4w[bb][:, :],
            in_=bass.AP(sr, gg * 4 * 128 * 512, [[512, 128], [128 * 512, 4], [1, 512]])))(src, g4, b),
            dwc, 16)
        cast_v.append(S.E("gpsimd", [(dwc, dv)],
                          (lambda ds, gg, bb: lambda e: e.tensor_copy(
                              out=ds[:, gg * 2048:(gg + 1) * 2048], in_=wf4w[bb][:, :]))(dst, g4, b),
                          pwc))
    return (pwc, pwc.n)


def emit_wo_base(m, xg, cg, zs, wo_all, wb_all, cast_done, cc_xg, cc_cg):
    nc, S = m.nc, m.S
    Tc = TC
    nthc = Tc // 512
    NT = T // 512
    sfx = "W1"
    dc_ = [m.sem(f"ds{i}") for i in range(4, 8)]
    dxx = [m.sem(f"ds{i}") for i in range(0, 4)]
    sv = m.sem("sv")
    mm = m.sem("mm")
    gz = m.sem("gy")
    with ExitStack() as es:
        sb = lambda n, s_, d=F32: es.enter_context(nc.sbuf_tensor(n + sfx, s_, d))
        cb = [sb(f"cbw{i}", [128, 2048], BF16) for i in range(4)]
        xb = [sb(f"xbw{i}", [128, 2048], BF16) for i in range(4)]
        ob = [sb(f"obw{i}", [128, 512], BF16) for i in range(8)]
        S.start_phase(m.gate())
        with ExitStack() as e2:
            pgw = [e2.enter_context(nc.psum_tensor(f"pgw{c}{sfx}", [128, 512], F32)) for c in range(8)]
            ev_v = []
            mmx_v = []
            mmc_v = []
            for tt in range(NT):
                j = tt // nthc
                lcol = (tt % nthc) * 512
                pb0 = (tt % 2) * 4
                for g4 in range(NF // 4):
                    n4 = tt * (NF // 4) + g4
                    bx = n4 % 4
                    wv_ = [(mm, mmx_v[4 * (n4 - 4) + 3])] if n4 >= 4 else [cc_xg, cast_done]
                    dv = S.E("sync", wv_, (lambda jj2, gg, ll, bb: lambda e: e.dma_start(
                        out=xb[bb][:, :],
                        in_=bass.AP(xg, (jj2 * D + gg * 4 * 128) * TC + ll,
                                    [[TC, 128], [128 * TC, 4], [1, 512]])))(j, g4, lcol, bx),
                        dxx[bx], 16)
                    for jg in range(4):
                        g = g4 * 4 + jg
                        wv_ = [(dxx[bx], dv)] if jg == 0 else []
                        if g == 0 and tt > 1:
                            wv_.append((sv, ev_v[4 * (tt - 2) + 3]))
                        for c in range(4):
                            mv = S.E("tensor", wv_ if c == 0 else [],
                                     (lambda gg, bb, jj3, cc, pp: lambda e: e.matmul(
                                         pgw[pp + cc][:, :], wb_all[:, gg * 512 + cc * 128: gg * 512 + (cc + 1) * 128],
                                         xb[bb][:, jj3 * 512:(jj3 + 1) * 512],
                                         start=(gg == 0), stop=False))(g, bx, jg, c, pb0),
                                     mm if c == 3 else None)
                        mmx_v.append(mv)
                for g4 in range(NF // 4):
                    n4 = tt * (NF // 4) + g4
                    bx = n4 % 4
                    wv_ = [(mm, mmc_v[4 * (n4 - 4) + 3])] if n4 >= 4 else [cc_cg]
                    dv = S.E("sync", wv_, (lambda gg, bb, tt2: lambda e: e.dma_start(
                        out=cb[bb][:, :],
                        in_=bass.AP(cg, (gg * 4 * 128) * T + tt2 * 512,
                                    [[T, 128], [128 * T, 4], [1, 512]])))(g4, bx, tt),
                        dc_[bx], 16)
                    for jg in range(4):
                        g = g4 * 4 + jg
                        wv_ = [(dc_[bx], dv)] if jg == 0 else []
                        for c in range(4):
                            mv = S.E("tensor", wv_ if c == 0 else [],
                                     (lambda gg, bb, jj3, cc, pp: lambda e: e.matmul(
                                         pgw[pp + cc][:, :], wo_all[:, gg * 512 + cc * 128: gg * 512 + (cc + 1) * 128],
                                         cb[bb][:, jj3 * 512:(jj3 + 1) * 512],
                                         start=False, stop=(gg == NF - 1)))(g, bx, jg, c, pb0),
                                     mm if c == 3 else None)
                        mmc_v.append(mv)
                for c in range(4):
                    k = (tt % 2) * 4 + c
                    wv_ = [(mm, mmc_v[-1])]
                    if c == 0 and tt >= 2:
                        wv_.append((gz, gz.n))
                    v = S.E("vector", wv_, (lambda kk, cc, pp: lambda e: e.tensor_copy(
                        out=ob[kk][:, :], in_=pgw[pp + cc][:, :]))(k, c, pb0), sv)
                    ev_v.append(v)
                    S.E("gpsimd", [(sv, v)],
                        (lambda jj2, cc, ll, kk: lambda e: e.dma_start(
                            out=zs[jj2 * 512 + cc * 128: jj2 * 512 + (cc + 1) * 128, ll:ll + 512],
                            in_=ob[kk][:, :]))(j, c, lcol, k), gz, 16)
    return dict(staged=(gz, gz.n))


def emit_final(m, xo1, za, lod, outd, waits):
    nc, S = m.nc, m.S
    Tc = TC
    sfx = "FN"
    dm_ = [m.sem("ds0"), m.sem("ds1")]
    sv = m.sem("sv")
    go = [m.sem("go0"), m.sem("go1")]
    with ExitStack() as es:
        sb = lambda n, s_, d=F32: es.enter_context(nc.sbuf_tensor(n + sfx, s_, d))
        x_b = [sb(f"x_b{i}", [128, Tc]) for i in range(2)]
        z_b = [sb(f"z_b{i}", [128, Tc], BF16) for i in range(2)]
        l_b = [sb(f"l_b{i}", [128, Tc], BF16) for i in range(2)]
        o_b = [sb(f"o_b{i}", [128, Tc]) for i in range(2)]
        S.start_phase(m.gate())
        sv_v = []
        for f in range(NF):
            b = f % 2
            wv_ = list(waits) if f == 0 else []
            if f >= 2:
                wv_.append((sv, sv_v[f - 2]))
            S.E("sync", wv_, (lambda ff, bb: lambda e: e.dma_start(
                out=x_b[bb][:, :], in_=xo1[ff * 128:(ff + 1) * 128, :]))(f, b), dm_[b], 16)
            S.E("sync", [], (lambda ff, bb: lambda e: e.dma_start(
                out=z_b[bb][:, :], in_=za[ff * 128:(ff + 1) * 128, :]))(f, b), dm_[b], 16)
            S.E("sync", [], (lambda ff, bb: lambda e: e.dma_start(
                out=l_b[bb][:, :], in_=lod[ff * 128:(ff + 1) * 128, :]))(f, b), dm_[b], 16)
            dcur = (dm_[b], dm_[b].n)
            wv_ = [dcur]
            if f >= 2:
                wv_.append((go[b], 16 * (f // 2)))
            v = S.E("vector", wv_, (lambda bb: lambda e: e.tensor_tensor(
                out=o_b[bb][:, :], in0=x_b[bb][:, :], in1=z_b[bb][:, :], op=OP.add))(b), sv)
            v = S.E("vector", [(sv, v)], (lambda bb: lambda e: e.tensor_tensor(
                out=o_b[bb][:, :], in0=o_b[bb][:, :], in1=l_b[bb][:, :], op=OP.add))(b), sv)
            sv_v.append(v)
            S.E("gpsimd", [(sv, v)], (lambda ff, bb: lambda e: e.dma_start(
                out=outd[ff * 128:(ff + 1) * 128, :], in_=o_b[bb][:, :]))(f, b), go[b], 16)
        S.E("gpsimd", [(go[0], go[0].n)], lambda e: e.nop())
        S.E("gpsimd", [(go[1], go[1].n)], lambda e: e.nop())


def emit_lora(m, xn_sb, xn_ready, gwd, lad, lb1d, lb2d, sel1d, sel2d, identd, lod):
    nc, S = m.nc, m.S
    Tc = TC
    NTC = Tc // 128
    twL = min(512, Tc)
    nth = Tc // twL
    sfx = "L1"
    dl = m.sem("d4x")
    dwl = [m.sem("ds0"), m.sem("ds1")]
    svl = m.sem("sv")
    scl = m.sem("sc")
    mml = m.sem("mm")
    gml = [m.sem("glo0"), m.sem("glo1")]
    with ExitStack() as es:
        sb = lambda n, s_, d=F32: es.enter_context(nc.sbuf_tensor(n + sfx, s_, d))
        ps_ = lambda n, s_, d=F32: es.enter_context(nc.psum_tensor(n + sfx, s_, d))
        gwb = sb("gwb", [128, NF * 28], BF16)
        lab = sb("lab", [128, NF * 224], BF16)
        lb1b = sb("lb1b", [128, D], BF16)
        lb2b = sb("lb2b", [96, D], BF16)
        sel1 = sb("sel1_sb", [28, 128], BF16)
        sel2 = sb("sel2_sb", [28, 96], BF16)
        ident = sb("ident_sb", [128, 128], BF16)
        gt_sb = sb("gt_sb", [28, Tc], BF16)
        hs = [sb("hs1", [128, Tc], BF16), sb("hs2", [96, Tc], BF16)]
        ge_sb = [sb("ge_sb1", [128, Tc]), sb("ge_sb2", [96, Tc])]
        wf = [sb(f"wfl{i}", [128, 512]) for i in range(2)]
        lg = sb("lg", [128, 28])
        m1 = sb("m1c", [128, 1]); m2 = sb("m2c", [128, 1])
        eq1 = sb("eq1", [128, 28]); eq2 = sb("eq2", [128, 28])
        l2m = sb("l2m", [128, 28]); dd = sb("ddc", [128, 1])
        p2c = sb("p2c", [128, 1]); p1c = sb("p1c", [128, 1])
        g1t = sb("g1t", [128, 28]); gat = sb("gat", [128, 28])
        gab = sb("gab", [128, 28], BF16)
        ob = [sb(f"obl{i}", [128, twL], BF16) for i in range(4)]
        S.start_phase(m.gate())
        S.E("sync", [], lambda e: e.dma_start(out=sel1[:, :], in_=sel1d[:, :]), dl, 16)
        S.E("sync", [], lambda e: e.dma_start(out=sel2[:, :], in_=sel2d[:, :]), dl, 16)
        S.E("sync", [], lambda e: e.dma_start(out=ident[:, :], in_=identd[:, :]), dl, 16)
        dl_done = (dl, dl.n)
        jobs = []
        for g in range(NF):
            jobs.append((gwd, g * 128, 0, gwb, g * 28, 28, 128))
        for g in range(NF):
            jobs.append((lad, g * 128, 0, lab, g * 224, 224, 128))
        for t in range(D // 512):
            jobs.append((lb1d, 0, t * 512, lb1b, t * 512, 512, 128))
        for t in range(D // 512):
            jobs.append((lb2d, 0, t * 512, lb2b, t * 512, 512, 96))
        castv = []
        for j, (src, soff, coff, dst, doff, w, rows) in enumerate(jobs):
            b = j % 2
            wv = [] if j < 2 else [(svl, castv[j - 2])]
            dv = S.E("sync", wv, (lambda sr, so, co, ww, rr, bb: lambda e: e.dma_start(
                out=wf[bb][0:rr, 0:ww], in_=sr[so:so + rr, co:co + ww]))(src, soff, coff, w, rows, b),
                dwl[b], 16)
            castv.append(S.E("vector", [(dwl[b], dv)],
                             (lambda ds, do, ww, rr, bb: lambda e: e.tensor_copy(
                                 out=ds[0:rr, do:do + ww], in_=wf[bb][0:rr, 0:ww]))(dst, doff, w, rows, b),
                             svl))
        cast_gw = castv[NF - 1]
        cast_la = castv[2 * NF - 1]
        cast_lb = castv[-1]

        es_gate = ExitStack()
        pl = es_gate.enter_context(nc.psum_tensor("plL" + sfx, [128, 28], F32))
        pt = es_gate.enter_context(nc.psum_tensor("ptL" + sfx, [28, 128], BF16))
        prev_lg = None
        prev_tp = None
        prev_gtc = None
        for tch in range(NTC):
            wv = [xn_ready, (svl, cast_gw)]
            if prev_lg is not None:
                wv.append((scl, prev_lg))
            i = None
            for g in range(NF):
                i = S.E("tensor", wv if g == 0 else [],
                        (lambda gg, tt: lambda e: e.matmul(
                            pl[:, :], xn_sb[:, gg * Tc + tt * 128: gg * Tc + (tt + 1) * 128],
                            gwb[:, gg * 28:(gg + 1) * 28],
                            start=(gg == 0), stop=(gg == NF - 1)))(g, tch),
                        mml if g == NF - 1 else None)
            m_pl = i
            prev_lg = S.E("scalar", [(mml, m_pl)],
                          lambda e: e.copy(out=lg[:, :], in_=pl[:, :]), scl)
            v = S.E("vector", [(scl, prev_lg), (svl, svl.n)],
                    lambda e: e.tensor_reduce(out=m1[:, :], in_=lg[:, :], op=OP.max, axis=AX.X), svl)
            v = S.E("vector", [(svl, v)],
                    lambda e: e.tensor_scalar(out=eq1[:, :], in0=lg[:, :], scalar1=m1[:, :],
                                              scalar2=None, op0=OP.is_equal), svl)
            v = S.E("vector", [(svl, v)],
                    lambda e: e.scalar_tensor_tensor(out=l2m[:, :], in0=eq1[:, :], scalar=-1e30,
                                                     in1=lg[:, :], op0=OP.mult, op1=OP.add), svl)
            v = S.E("vector", [(svl, v)],
                    lambda e: e.tensor_reduce(out=m2[:, :], in_=l2m[:, :], op=OP.max, axis=AX.X), svl)
            v_dd = S.E("vector", [(svl, v)],
                       lambda e: e.tensor_tensor(out=dd[:, :], in0=m2[:, :], in1=m1[:, :],
                                                 op=OP.subtract), svl)
            c_p2 = S.E("scalar", [(svl, v_dd)],
                       lambda e: e.activation(out=p2c[:, :], in_=dd[:, :], func=AF.Sigmoid), scl)
            v = S.E("vector", [(scl, c_p2)],
                    lambda e: e.tensor_scalar(out=p1c[:, :], in0=p2c[:, :], scalar1=-1.0,
                                              scalar2=1.0, op0=OP.mult, op1=OP.add), svl)
            v = S.E("vector", [(svl, v)],
                    lambda e: e.tensor_scalar(out=eq2[:, :], in0=l2m[:, :], scalar1=m2[:, :],
                                              scalar2=None, op0=OP.is_equal), svl)
            v = S.E("vector", [(svl, v)],
                    lambda e: e.tensor_scalar(out=g1t[:, :], in0=eq1[:, :], scalar1=p1c[:, :],
                                              scalar2=None, op0=OP.mult), svl)
            v = S.E("vector", [(svl, v)],
                    lambda e: e.scalar_tensor_tensor(out=gat[:, :], in0=eq2[:, :], scalar=p2c[:, :],
                                                     in1=g1t[:, :], op0=OP.mult, op1=OP.add), svl)
            wv = [(svl, v)]
            if prev_tp is not None:
                wv.append((mml, prev_tp))
            v_gab = S.E("vector", wv, lambda e: e.tensor_copy(out=gab[:, :], in_=gat[:, :]), svl)
            wv = [(svl, v_gab), dl_done]
            if prev_gtc is not None:
                wv.append((scl, prev_gtc))
            prev_tp = S.E("tensor", wv,
                          lambda e: e.transpose(pt[:, :], gab[:, :], ident[:, :]), mml)
            prev_gtc = S.E("scalar", [(mml, prev_tp)],
                           (lambda tt: lambda e: e.copy(out=gt_sb[:, tt * 128:(tt + 1) * 128],
                                                        in_=pt[:, :]))(tch), scl)
        gate_done = prev_gtc
        es_gate.close()

        es_h = ExitStack()
        widths = [128, 96]
        ph = [[es_h.enter_context(nc.psum_tensor(f"phL{er}_{t}{sfx}", [widths[er], twL], F32))
               for t in range(nth)] for er in range(2)]
        pge = [[es_h.enter_context(nc.psum_tensor(f"pgeL{er}_{t}{sfx}", [widths[er], twL], F32))
                for t in range(nth)] for er in range(2)]
        sels = [sel1, sel2]
        hs_v = []
        for er in range(2):
            wdt = widths[er]
            for t in range(nth):
                i = None
                for g in range(NF):
                    i = S.E("tensor", [xn_ready, (svl, cast_la), (scl, gate_done)] if g == 0 else [],
                            (lambda ee, tt, gg, ww: lambda e: e.matmul(
                                ph[ee][tt][:, :],
                                lab[:, gg * 224 + ee * 128: gg * 224 + ee * 128 + ww],
                                xn_sb[:, gg * Tc + tt * twL: gg * Tc + (tt + 1) * twL],
                                start=(gg == 0), stop=(gg == NF - 1)))(er, t, g, wdt),
                            mml if g == NF - 1 else None)
                m_ph = i
                m_ge = S.E("tensor", [(scl, gate_done)],
                           (lambda ee, tt, ww: lambda e: e.matmul(
                               pge[ee][tt][:, :], sels[ee][:, 0:ww],
                               gt_sb[:, tt * twL:(tt + 1) * twL],
                               start=True, stop=True))(er, t, wdt), mml)
                c_ge = S.E("scalar", [(mml, m_ge)],
                           (lambda ee, tt, ww: lambda e: e.copy(
                               out=ge_sb[ee][0:ww, tt * twL:(tt + 1) * twL],
                               in_=pge[ee][tt][:, :]))(er, t, wdt), scl)
                hs_v.append(S.E("vector", [(mml, m_ph), (scl, c_ge), (svl, svl.n)],
                                (lambda ee, tt, ww: lambda e: e.tensor_tensor(
                                    out=hs[ee][0:ww, tt * twL:(tt + 1) * twL],
                                    in0=ph[ee][tt][:, :],
                                    in1=ge_sb[ee][0:ww, tt * twL:(tt + 1) * twL],
                                    op=OP.mult))(er, t, wdt), svl))
        hs_done = hs_v[-1]
        es_h.close()

        plo = [ps_(f"ploL{i}", [128, twL]) for i in range(2)]
        prev_ev = {}
        dma_vL = {}
        for oc in range(NF):
            for t in range(nth):
                n = oc * nth + t
                b = n % 2
                wv = [(svl, hs_done), (svl, cast_lb)]
                if b in prev_ev and prev_ev[b] is not None:
                    wv.append((svl, prev_ev[b]))
                S.E("tensor", wv,
                    (lambda o_, tt: lambda e: e.matmul(
                        plo[(o_ * nth + tt) % 2][:, :], lb1b[:, o_ * 128:(o_ + 1) * 128],
                        hs[0][:, tt * twL:(tt + 1) * twL], start=True, stop=False))(oc, t))
                m_lo = S.E("tensor", [],
                           (lambda o_, tt: lambda e: e.matmul(
                               plo[(o_ * nth + tt) % 2][:, :], lb2b[:, o_ * 128:(o_ + 1) * 128],
                               hs[1][:, tt * twL:(tt + 1) * twL], start=False, stop=True))(oc, t), mml)
                wv = [(mml, m_lo)]
                if (b,) in dma_vL:
                    wv.append((gml[b], dma_vL[(b,)]))
                ve = S.E("vector", wv,
                         (lambda bb: lambda e: e.tensor_copy(out=ob[bb][:, 0:twL],
                                                             in_=plo[bb][:, :]))(b), svl)
                prev_ev[b] = ve
                dma_vL[(b,)] = S.E("gpsimd", [(svl, ve)],
                                   (lambda o_, tt, bb: lambda e: e.dma_start(
                                       out=lod[o_ * 128:(o_ + 1) * 128, tt * twL:(tt + 1) * twL],
                                       in_=ob[bb][:, 0:twL]))(oc, t, b), gml[b], 16)
    return dict(staged=[(gml[0], gml[0].n), (gml[1], gml[1].n)])


def build_mega():
    m = Mega()
    nc = m.nc
    f = lambda n, s_, d=F32: nc.dram_tensor(n, s_, d, kind="ExternalInput")
    x = f("x", [D, TC])
    wq0 = f("wq0", [D, D]); wk0 = f("wk0", [D, 2048]); wv0 = f("wv0", [D, 2048])
    bq0 = f("bq0", [128, NF]); bk0 = f("bk0", [128, 16]); bvr0 = f("bvr0", [1, 2048])
    qw = nc.dram_tensor("qw", [D, 512], I32, kind="ExternalInput")
    scz = f("scz", [1, NF * 512]); zrz = f("zrz", [1, NF * 512]); fb = f("fb", [128, 4])
    wq1 = f("wq1", [D, D]); wk1 = f("wk1", [D, 1024]); wv1 = f("wv1", [D, 1024])
    cosd = f("cos", [64, TC]); sind = f("sin", [64, TC])
    wo = f("wo", [D, 512]); wb_ = f("wb", [D, 512])
    gw = f("gw", [D, 28]); la = f("la", [D, 224])
    lb1 = f("lb1", [128, D]); lb2 = f("lb2", [96, D])
    sel1 = nc.dram_tensor("sel1", [28, 128], BF16, kind="ExternalInput")
    sel2 = nc.dram_tensor("sel2", [28, 96], BF16, kind="ExternalInput")
    ident = nc.dram_tensor("ident", [128, 128], BF16, kind="ExternalInput")
    dcold = f("dcol", [128, 4]); qkiod = f("qkio", [128, 512])
    outd = nc.dram_tensor("out", [D, TC], F32, kind="ExternalOutput")
    g = lambda n, s_, d=BF16: nc.dram_tensor(n, s_, d)
    xns0 = g("xns0", [D, TC])
    xg0 = nc.dram_tensor("xg0", [NCORE * D, TC], BF16, addr_space="Shared")
    Qs0 = g("Qs0", [D, TC]); Qa0 = g("Qa0", [D, TC])
    Ks0 = g("Ks0", [2048, TC]); Ka0 = g("Ka0", [2048, TC])
    Vs0 = g("Vs0", [NCORE * TC, 256]); Va0 = g("Va0", [NCORE * TC, 256])
    Cs0 = g("Cs0", [D, TC]); Ca0 = g("Ca0", [D, TC])
    Ys0 = g("Ys0", [D, TC]); Ya0 = g("Ya0", [D, TC])
    xo1 = g("xo1", [D, TC], F32)
    xns1 = g("xns1", [D, TC])
    xg1 = nc.dram_tensor("xg1", [NCORE * D, TC], BF16, addr_space="Shared")
    Qs1 = g("Qs1", [D, TC]); Qa1 = g("Qa1", [D, TC])
    Ks1 = g("Ks1", [1024, TC]); Ka1 = g("Ka1", [1024, TC])
    Vs1 = g("Vs1", [NCORE * TC, 128]); Va1 = g("Va1", [NCORE * TC, 128])
    c1s = g("c1s", [512, T])
    cg1 = nc.dram_tensor("cg1", [D, T], BF16, addr_space="Shared")
    Zs1 = g("Zs1", [D, TC]); Za1 = g("Za1", [D, TC])
    lod = g("lod", [D, TC])

    with m.es:
        dcol_sb = m.es.enter_context(nc.sbuf_tensor("dcol_g", [128, 4], F32))
        qkio_sb = m.es.enter_context(nc.sbuf_tensor("qkio_g", [128, 512], F32))
        dpre = m.sem("d4x")
        m.S.E("sync", [], lambda e: e.dma_start(out=dcol_sb[:, :], in_=dcold[:, :]), dpre, 16)
        m.S.E("sync", [], lambda e: e.dma_start(out=qkio_sb[:, :], in_=qkiod[:, :]), dpre, 16)

        with ExitStack() as s0:
            xn0_sb = s0.enter_context(nc.sbuf_tensor("xn0_sb", [128, NF * TC], BF16))
            f0 = emit_front(m, 0, x, None, wq0, wk0, wv0, 2048, True, (bq0, bk0, bvr0),
                            False, None, xn0_sb, xns0, None, Qs0, Ks0, Vs0)
        ccxg0 = emit_coll(m, "agx0", "AllGather", [xns0], [xg0], [f0["xn_staged"]])
        cck0 = emit_coll(m, "a2ak0", "AllToAll", [Ks0], [Ka0], [f0["qk_staged"][1]])
        ccq0 = emit_coll(m, "a2aq0", "AllToAll", [Qs0], [Qa0], [f0["qk_staged"][0]])
        ccv0 = emit_coll(m, "a2av0", "AllToAll", [Vs0], [Va0], [f0["v_staged"]])
        nthc = TC // 512
        n0 = emit_ffn0(m, qw, scz, zrz, fb, xg0, Ys0, ccxg0)
        ccy0 = emit_coll(m, "a2ay0", "AllToAll", [Ys0], [Ya0], [n0["staged"]])

        def ctx0_dst(ih, qt):
            r = 512 * (qt // nthc) + 128 * ih
            c = (qt % nthc) * 512
            return Cs0[r:r + 128, c:c + 512]

        a0 = emit_attn(m, 0, T, 2, 2, float(1.0 / np.sqrt(np.float32(32))),
                       Ka0, Va0, Qa0, dcol_sb, qkio_sb, ctx0_dst,
                       [ccq0, cck0, ccv0, (dpre, dpre.n)])
        ccc0 = emit_coll(m, "a2ac0", "AllToAll", [Cs0], [Ca0], a0["staged"])

        def add0(j, ff):
            if j == 1:
                return Ya0[ff * 128:(ff + 1) * 128, :]
            h = ff
            c = (h % 16) // 2
            i = 2 * ((h % 16) % 2) + (h // 16)
            r = 512 * c + 128 * i
            return Ca0[r:r + 128, :]

        cc_hold = {}

        def hook_k1():
            gk_ = m.sem("gk")
            cc_hold["k"] = emit_coll(m, "a2ak1", "AllToAll", [Ks1], [Ka1], [(gk_, gk_.n)])

        def hook_q1():
            gq_ = m.sem("gq")
            cc_hold["q"] = emit_coll(m, "a2aq1", "AllToAll", [Qs1], [Qa1], [(gq_, gq_.n)])

        with ExitStack() as s1:
            xn1_sb = s1.enter_context(nc.sbuf_tensor("xn1_sb", [128, NF * TC], BF16))
            f1 = emit_front(m, 1, x, add0, wq1, wk1, wv1, 1024, False, None,
                            True, (cosd, sind), xn1_sb, xns1, xo1, Qs1, Ks1, Vs1,
                            add_waits=[ccc0, ccy0], hook_k=hook_k1, hook_q=hook_q1)
            cck1 = cc_hold["k"]
            ccq1 = cc_hold["q"]
            ccv1 = emit_coll(m, "a2av1", "AllToAll", [Vs1], [Va1], [f1["v_staged"]])
            ccxg1 = emit_coll(m, "agx1", "AllGather", [xns1], [xg1], [f1["xn_staged"]])
            lr = emit_lora(m, xn1_sb, f1["xn_done"], gw, la, lb1, lb2, sel1, sel2, ident, lod)

        m.S.start_phase(m.gate())
        sw = ExitStack()
        wo_all = sw.enter_context(nc.sbuf_tensor("wo_allW", [128, NF * 512], BF16))
        wb_all = sw.enter_context(nc.sbuf_tensor("wb_allW", [128, NF * 512], BF16))
        wf4w = [sw.enter_context(nc.sbuf_tensor(f"wf4w{i}", [128, 2048], F32)) for i in range(2)]
        wo_cast_done = emit_wo_cast(m, wo, wb_, wo_all, wb_all, wf4w)

        def ctx1_dst(ih, qt):
            return c1s[128 * ih:128 * (ih + 1), qt * 512:(qt + 1) * 512]

        a1 = emit_attn(m, 1, T, 1, 4, float(1.0 / np.sqrt(np.float32(128))),
                       Ka1, Va1, Qa1, dcol_sb, qkio_sb, ctx1_dst,
                       [ccq1, cck1, ccv1])
        cccg1 = emit_coll(m, "agc1", "AllGather", [c1s], [cg1], a1["staged"])
        w1 = emit_wo_base(m, xg1, cg1, Zs1, wo_all, wb_all, wo_cast_done, ccxg1, cccg1)
        ccz1 = emit_coll(m, "a2az1", "AllToAll", [Zs1], [Za1], [w1["staged"]])
        sw.close()
        emit_final(m, xo1, Za1, lod, outd,
                   [ccz1] + lr["staged"] + f1["xo_done"])
        blk = m.es.enter_context(nc.Block())
        m.S.emit(blk)
    return m.nc


def _launch(nc, in_maps, trace=False):
    return run_bass_kernel_spmd(nc, in_maps, core_ids=list(range(NCORE)), trace=trace)


_progs = {}


def _get(key, builder, *args):
    if key not in _progs:
        _progs[key] = builder(*args)
    return _progs[key]


def _rope_tables(T_):
    import math
    DH = 128
    inv = 1.0 / (500000.0 ** (np.arange(0, DH, 2, dtype=np.float64) / DH))
    wl = 2 * math.pi / inv
    inv = np.where(wl > 8192.0, inv / 32.0, inv)
    smooth = (8192.0 / wl - 1.0) / 3.0
    smoothed = (1 - smooth) * inv / 32.0 + smooth * inv
    med = (wl <= 8192.0) & (wl >= 2048.0)
    inv = np.where(med, smoothed, inv)
    pos = np.arange(T_, dtype=np.float64)
    fr = pos[:, None] * inv[None, :]
    return (np.ascontiguousarray(np.cos(fr).T.astype(np.float32)),
            np.ascontiguousarray(np.sin(fr).T.astype(np.float32)))


def _cc(a):
    return np.ascontiguousarray(a)


def kernel_unfused(L0_attn_x, L0_attn_Wq, L0_attn_bq, L0_attn_Wk, L0_attn_bk, L0_attn_Wv,
           L0_attn_bv, L0_ffn_qweight, L0_ffn_scales, L0_ffn_scaled_zeros, L0_ffn_bias,
           L1_attn_Wq, L1_attn_Wk, L1_attn_Wv, L1_attn_Wo,
           L1_ffn_base_W, L1_ffn_gate_W, L1_ffn_lora_A, L1_ffn_lora_B):
    f32 = np.float32
    T_, Tc = T, TC
    x0 = _cc(np.asarray(L0_attn_x, f32)[0].T)

    wq0T = _cc(np.asarray(L0_attn_Wq, f32).T)
    wk0T = _cc(np.asarray(L0_attn_Wk, f32).T)
    wv0T = _cc(np.asarray(L0_attn_Wv, f32).T)
    bq0 = _cc(np.asarray(L0_attn_bq, f32).reshape(NF, 128).T)
    bk0 = _cc(np.asarray(L0_attn_bk, f32).reshape(16, 128).T)
    bv0 = _cc(np.asarray(L0_attn_bv, f32).reshape(16, 128).T)
    qwT = np.asarray(L0_ffn_qweight, np.int32).reshape(D, D).T
    s2 = np.asarray(L0_ffn_scales, f32)[:, :, 0]
    z2 = np.asarray(L0_ffn_scaled_zeros, f32)[:, :, 0]
    fb = np.asarray(L0_ffn_bias, f32)
    wq1T = _cc(np.asarray(L1_attn_Wq, f32).T)
    wk1T = _cc(np.asarray(L1_attn_Wk, f32).T)
    wv1T = _cc(np.asarray(L1_attn_Wv, f32).T)
    woT = np.asarray(L1_attn_Wo, f32).T
    baseT = np.asarray(L1_ffn_base_W, f32).T
    gwT = _cc(np.asarray(L1_ffn_gate_W, f32).T)
    laA = _cc(np.asarray(L1_ffn_lora_A, f32).transpose(2, 0, 1).reshape(D, 224))
    lbB = np.asarray(L1_ffn_lora_B, f32).transpose(0, 2, 1).reshape(224, D)
    lb1 = _cc(lbB[:128]); lb2 = _cc(lbB[128:])
    sel = np.zeros((28, 224), f32)
    for e_ in range(28):
        sel[e_, e_ * 8:(e_ + 1) * 8] = 2.0
    sel1 = _cc(sel[:, :128].astype(bf16)); sel2 = _cc(sel[:, 128:].astype(bf16))
    ident = np.eye(128, dtype=np.float32).astype(bf16)
    cosT, sinT = _rope_tables(T_)
    dcol = _cc(np.tile((np.arange(4) * 128).astype(f32), (128, 1)))
    qkio = _cc((np.arange(512)[None, :] - np.arange(128)[:, None]).astype(f32))

    def tok(c):
        return slice(c * Tc, (c + 1) * Tc)

    nc = _get(("front0", Tc), build_front, Tc, 0, False, 2048, True)
    maps = [{"x": _cc(x0[:, tok(c)]), "wq": wq0T, "wk": wk0T, "wv": wv0T,
             "bq": bq0, "bk": bk0, "bv": bv0} for c in range(NCORE)]
    r = _launch(nc, maps)
    xn0 = np.concatenate([np.asarray(r.results[c]["xn"]) for c in range(NCORE)], axis=1)
    q0 = np.concatenate([np.asarray(r.results[c]["q"]) for c in range(NCORE)], axis=1)
    k0 = np.concatenate([np.asarray(r.results[c]["k"]) for c in range(NCORE)], axis=1)
    v0 = np.concatenate([np.asarray(r.results[c]["v"]) for c in range(NCORE)], axis=1)

    def vtile(vfull, kv):
        vt = _cc(vfull.T[:, kv * 128:(kv + 1) * 128])
        return _cc(vt.reshape(T_ // 128, 128, 128).transpose(1, 0, 2).reshape(128, T_))

    nc = _get(("attn", T_, 2, 2), build_attn, T_, 2, 2, 1.0 / np.sqrt(np.float32(32)))
    maps = []
    for c in range(NCORE):
        m = {"dcol": dcol, "qkio": qkio}
        for j, kv in enumerate((2 * c, 2 * c + 1)):
            m[f"k{j}"] = _cc(k0[kv * 128:(kv + 1) * 128, :])
            m[f"v{j}"] = vtile(v0, kv)
            m[f"q{2 * j}"] = _cc(q0[kv * 128:(kv + 1) * 128, :])
            m[f"q{2 * j + 1}"] = _cc(q0[(16 + kv) * 128:(16 + kv + 1) * 128, :])
        maps.append(m)
    r = _launch(nc, maps)
    ctx0 = np.empty((D, T_), bf16)
    for c in range(NCORE):
        for j, kv in enumerate((2 * c, 2 * c + 1)):
            ctx0[kv * 128:(kv + 1) * 128] = np.asarray(r.results[c][f"ctx{2 * j}"])
            ctx0[(16 + kv) * 128:(16 + kv + 1) * 128] = np.asarray(r.results[c][f"ctx{2 * j + 1}"])

    nc = _get(("ffn0", T_), build_ffn0, T_)
    maps = []
    for c in range(NCORE):
        osl = slice(512 * c, 512 * (c + 1))
        maps.append({"xn": xn0, "qw": _cc(qwT[:, osl]),
                     "scz": _cc(s2[osl, :].T).reshape(1, -1),
                     "zrz": _cc(z2[osl, :].T).reshape(1, -1),
                     "bias": _cc(fb[osl].reshape(4, 128).T)})
    r = _launch(nc, maps)
    y0 = np.concatenate([np.asarray(r.results[c]["y"]) for c in range(NCORE)], axis=0)

    nc = _get(("front1", Tc), build_front, Tc, 2, True, 1024, False)
    maps = [{"x": _cc(x0[:, tok(c)]), "a0": _cc(ctx0[:, tok(c)]), "a1": _cc(y0[:, tok(c)]),
             "wq": wq1T, "wk": wk1T, "wv": wv1T,
             "cos": _cc(cosT[:, tok(c)]), "sin": _cc(sinT[:, tok(c)])}
            for c in range(NCORE)]
    r = _launch(nc, maps)
    x1 = [np.asarray(r.results[c]["xo"]) for c in range(NCORE)]
    xn1 = np.concatenate([np.asarray(r.results[c]["xn"]) for c in range(NCORE)], axis=1)
    q1 = np.concatenate([np.asarray(r.results[c]["q"]) for c in range(NCORE)], axis=1)
    k1 = np.concatenate([np.asarray(r.results[c]["k"]) for c in range(NCORE)], axis=1)
    v1 = np.concatenate([np.asarray(r.results[c]["v"]) for c in range(NCORE)], axis=1)

    nc = _get(("attn", T_, 1, 4), build_attn, T_, 1, 4, 1.0 / np.sqrt(np.float32(128)))
    maps = []
    for c in range(NCORE):
        m = {"dcol": dcol, "qkio": qkio,
             "k0": _cc(k1[c * 128:(c + 1) * 128, :]), "v0": vtile(v1, c)}
        for j in range(4):
            m[f"q{j}"] = _cc(q1[(4 * c + j) * 128:(4 * c + j + 1) * 128, :])
        maps.append(m)
    r = _launch(nc, maps)
    ctx1 = np.empty((D, T_), bf16)
    for c in range(NCORE):
        for j in range(4):
            h = 4 * c + j
            ctx1[h * 128:(h + 1) * 128] = np.asarray(r.results[c][f"ctx{j}"])

    nc = _get(("wo_base", T_), build_wo_base, T_)
    maps = [{"cx": ctx1, "xn": xn1,
             "wo": _cc(woT[:, 512 * c:512 * (c + 1)]),
             "wb": _cc(baseT[:, 512 * c:512 * (c + 1)])} for c in range(NCORE)]
    r = _launch(nc, maps)
    z1 = np.concatenate([np.asarray(r.results[c]["z"]) for c in range(NCORE)], axis=0)

    nc = _get(("lora", Tc), build_lora, Tc)
    maps = [{"xn": _cc(xn1[:, tok(c)]), "gw": gwT, "la": laA, "lb1": lb1, "lb2": lb2,
             "sel1": sel1, "sel2": sel2, "ident": ident} for c in range(NCORE)]
    r = _launch(nc, maps)
    lo1 = [np.asarray(r.results[c]["lo"]) for c in range(NCORE)]

    nc = _get(("final", Tc), build_final, Tc)
    maps = [{"x": x1[c], "z": _cc(z1[:, tok(c)]), "lo": lo1[c]} for c in range(NCORE)]
    r = _launch(nc, maps)
    out = np.concatenate([np.asarray(r.results[c]["out"]) for c in range(NCORE)], axis=1)
    return _cc(out.T).reshape(1, T_, D)


_mega_cache = {}


def _host_prep(inputs):
    f32 = np.float32
    p = {}
    p["x0"] = _cc(np.asarray(inputs["L0_attn_x"], f32)[0].T)
    p["wq0T"] = _cc(np.asarray(inputs["L0_attn_Wq"], f32).T)
    p["wk0T"] = _cc(np.asarray(inputs["L0_attn_Wk"], f32).T)
    p["wv0T"] = _cc(np.asarray(inputs["L0_attn_Wv"], f32).T)
    p["bq0"] = _cc(np.asarray(inputs["L0_attn_bq"], f32).reshape(NF, 128).T)
    p["bk0"] = _cc(np.asarray(inputs["L0_attn_bk"], f32).reshape(16, 128).T)
    p["bvr0"] = _cc(np.asarray(inputs["L0_attn_bv"], f32).reshape(1, 2048))
    p["qwT"] = np.asarray(inputs["L0_ffn_qweight"], np.int32).reshape(D, D).T
    p["s2"] = np.asarray(inputs["L0_ffn_scales"], f32)[:, :, 0]
    p["z2"] = np.asarray(inputs["L0_ffn_scaled_zeros"], f32)[:, :, 0]
    p["fb"] = np.asarray(inputs["L0_ffn_bias"], f32)
    p["wq1T"] = _cc(np.asarray(inputs["L1_attn_Wq"], f32).T)
    p["wk1T"] = _cc(np.asarray(inputs["L1_attn_Wk"], f32).T)
    p["wv1T"] = _cc(np.asarray(inputs["L1_attn_Wv"], f32).T)
    p["woT"] = np.asarray(inputs["L1_attn_Wo"], f32).T
    p["baseT"] = np.asarray(inputs["L1_ffn_base_W"], f32).T
    p["gwT"] = _cc(np.asarray(inputs["L1_ffn_gate_W"], f32).T)
    p["laA"] = _cc(np.asarray(inputs["L1_ffn_lora_A"], f32).transpose(2, 0, 1).reshape(D, 224))
    lbB = np.asarray(inputs["L1_ffn_lora_B"], f32).transpose(0, 2, 1).reshape(224, D)
    p["lb1"] = _cc(lbB[:128]); p["lb2"] = _cc(lbB[128:])
    sel = np.zeros((28, 224), f32)
    for e_ in range(28):
        sel[e_, e_ * 8:(e_ + 1) * 8] = 2.0
    p["sel1"] = _cc(sel[:, :128].astype(bf16)); p["sel2"] = _cc(sel[:, 128:].astype(bf16))
    p["ident"] = np.eye(128, dtype=np.float32).astype(bf16)
    p["cosT"], p["sinT"] = _rope_tables(T)
    p["dcol"] = _cc(np.tile((np.arange(4) * 128).astype(f32), (128, 1)))
    p["qkio"] = _cc((np.arange(512)[None, :] - np.arange(128)[:, None]).astype(f32))
    return p


def kernel_fused(**inputs):
    p = _host_prep(inputs)
    if "nc" not in _mega_cache:
        _mega_cache["nc"] = build_mega()
    nc = _mega_cache["nc"]
    maps = []
    for c in range(NCORE):
        sl = slice(c * TC, (c + 1) * TC)
        osl = slice(512 * c, 512 * (c + 1))
        maps.append({
            "x": _cc(p["x0"][:, sl]),
            "wq0": p["wq0T"], "wk0": p["wk0T"], "wv0": p["wv0T"],
            "bq0": p["bq0"], "bk0": p["bk0"], "bvr0": p["bvr0"],
            "qw": _cc(p["qwT"][:, osl]),
            "scz": _cc(p["s2"][osl, :].T).reshape(1, -1),
            "zrz": _cc(p["z2"][osl, :].T).reshape(1, -1),
            "fb": _cc(p["fb"][osl].reshape(4, 128).T),
            "wq1": p["wq1T"], "wk1": p["wk1T"], "wv1": p["wv1T"],
            "cos": _cc(p["cosT"][:, sl]), "sin": _cc(p["sinT"][:, sl]),
            "wo": _cc(p["woT"][:, osl]), "wb": _cc(p["baseT"][:, osl]),
            "gw": p["gwT"], "la": p["laA"], "lb1": p["lb1"], "lb2": p["lb2"],
            "sel1": p["sel1"], "sel2": p["sel2"], "ident": p["ident"],
            "dcol": p["dcol"], "qkio": p["qkio"],
        })
    r = _launch(nc, maps)
    out = np.concatenate([np.asarray(r.results[c]["out"]) for c in range(NCORE)], axis=1)
    return _cc(out.T).reshape(1, T, D)


FUSED = True


def kernel(**inputs):
    if FUSED:
        return kernel_fused(**inputs)
    return kernel_unfused(**inputs)


# version 09
# speedup vs baseline: 1.0287x; 1.0287x over previous
import numpy as np
import ml_dtypes
from contextlib import ExitStack
import concourse.bass as bass
import concourse.mybir as mybir
from concourse.bass_utils import run_bass_kernel_spmd

F32 = mybir.dt.float32
BF16 = mybir.dt.bfloat16
I32 = mybir.dt.int32
AF = mybir.ActivationFunctionType
OP = mybir.AluOpType
AX = mybir.AxisListType
bf16 = ml_dtypes.bfloat16

NCORE = 8
D = 4096
NF = D // 128
T = 8192
TC = T // NCORE


class Sem:
    def __init__(self, nc, es, name):
        self.h = es.enter_context(nc.semaphore(name))
        self.n = 0

    def inc(self, inst, k=1):
        inst.then_inc(self.h, k)
        self.n += k
        return self.n


class Sched:
    def __init__(self):
        self.ev = []
        self.pending_gate = {}

    def start_phase(self, gate):
        for eng in ("sync", "tensor", "vector", "scalar", "gpsimd"):
            self.pending_gate[eng] = list(gate)

    def E(self, eng, waits, fn, inc=None, k=1):
        waits = [w for w in waits if w is not None and w[1] is not None]
        if eng in self.pending_gate:
            waits = self.pending_gate.pop(eng) + waits
        self.ev.append((eng, waits, fn, inc, k))
        if inc is not None:
            inc.n += k
            return inc.n
        return None

    def emit(self, blk):
        for name, deco in (("sync", blk.sync), ("tensor", blk.tensor),
                           ("vector", blk.vector), ("scalar", blk.scalar),
                           ("gpsimd", blk.gpsimd)):
            def body(eng, _name=name):
                for (en, waits, fn, inc, k) in self.ev:
                    if en != _name:
                        continue
                    for (sm, val) in waits:
                        eng.wait_ge(sm.h, val)
                    inst = fn(eng)
                    if inc is not None:
                        inst.then_inc(inc.h, k)
            deco(body)


def build_front(Tc, n_adds, rope, kvd, with_bias):
    tw = min(512, Tc)
    nth = Tc // tw
    nkc = kvd // 128
    nc = bass.Bass("TRN2", target_bir_lowering=False)
    x = nc.dram_tensor("x", [D, Tc], F32, kind="ExternalInput")
    adds = [nc.dram_tensor(f"a{i}", [D, Tc], BF16, kind="ExternalInput") for i in range(n_adds)]
    wq = nc.dram_tensor("wq", [D, D], F32, kind="ExternalInput")
    wk = nc.dram_tensor("wk", [D, kvd], F32, kind="ExternalInput")
    wv = nc.dram_tensor("wv", [D, kvd], F32, kind="ExternalInput")
    bqt = nc.dram_tensor("bq", [128, NF], F32, kind="ExternalInput") if with_bias else None
    bkt = nc.dram_tensor("bk", [128, nkc], F32, kind="ExternalInput") if with_bias else None
    bvt = nc.dram_tensor("bv", [128, nkc], F32, kind="ExternalInput") if with_bias else None
    if rope:
        cosd = nc.dram_tensor("cos", [64, Tc], F32, kind="ExternalInput")
        sind = nc.dram_tensor("sin", [64, Tc], F32, kind="ExternalInput")
    xo = nc.dram_tensor("xo", [D, Tc], F32, kind="ExternalOutput") if n_adds else None
    xnd = nc.dram_tensor("xn", [D, Tc], BF16, kind="ExternalOutput")
    qd = nc.dram_tensor("q", [D, Tc], BF16, kind="ExternalOutput")
    kd = nc.dram_tensor("k", [kvd, Tc], BF16, kind="ExternalOutput")
    vd = nc.dram_tensor("v", [kvd, Tc], BF16, kind="ExternalOutput")

    with ExitStack() as es:
        sb = lambda n, s, d=F32: es.enter_context(nc.sbuf_tensor(n, s, d))
        ones_b = sb("ones_b", [128, 1], BF16)
        ones_r = sb("ones_r", [1, 128], F32)
        eps_c = sb("eps_c", [1, 1], F32)
        xn_sb = sb("xn_sb", [128, NF * Tc], BF16)
        mrow = sb("mrow", [1, Tc])
        arow = sb("arow", [1, Tc])
        rrow = sb("rrow", [1, Tc])
        if rope:
            cos_sb = sb("cos_sb", [64, Tc])
            sin_sb = sb("sin_sb", [64, Tc])

        with ExitStack() as eAB:
            mbc = [eAB.enter_context(nc.psum_tensor(f"mbc{t}", [128, tw], F32)) for t in range(nth)]
            rbc = [eAB.enter_context(nc.psum_tensor(f"rbc{t}", [128, tw], F32)) for t in range(nth)]

            with ExitStack() as eA:
                sbA = lambda n, s, d=F32: eA.enter_context(nc.sbuf_tensor(n, s, d))
                x_b = [sbA(f"x_b{i}", [128, Tc]) for i in range(2)]
                a_b = [[sbA(f"a_b{j}_{i}", [128, Tc], BF16) for i in range(2)] for j in range(n_adds)]
                xw_b = [sbA(f"xw_b{i}", [128, Tc]) for i in range(2)]
                xc_b = [sbA(f"xc_b{i}", [128, Tc], BF16) for i in range(2)]
                sq_b = [sbA(f"sq_b{i}", [128, Tc], BF16) for i in range(2)]
                srow = sbA("srow", [1, Tc])
                qrow = sbA("qrow", [1, Tc])
                msq = sbA("msq", [1, Tc])
                vrow = sbA("vrow", [1, Tc])
                stdr = sbA("stdr", [1, Tc])
                sum_ps = [eA.enter_context(nc.psum_tensor(f"sum_ps{t}", [1, tw], F32)) for t in range(nth)]
                ssq_ps = [eA.enter_context(nc.psum_tensor(f"ssq_ps{t}", [1, tw], F32)) for t in range(nth)]
                blkA = eA.enter_context(nc.Block())
                dmx = [Sem(nc, eA, "dmx0"), Sem(nc, eA, "dmx1")]
                drt = Sem(nc, eA, "drt")
                sv = Sem(nc, eA, "svA")
                sc = Sem(nc, eA, "scA")
                mm = Sem(nc, eA, "mmA")
                gm = [Sem(nc, eA, "gmA0"), Sem(nc, eA, "gmA1")]
                per_f = 1 + n_adds
                VA = n_adds + 1

                @blkA.sync
                def _(sync):
                    for f in range(NF):
                        b = f % 2
                        if f >= 2:
                            sync.wait_ge(sv.h, 1 + (f - 1) * VA)
                            if n_adds == 0:
                                sync.wait_ge(sc.h, f - 1)
                        sync.dma_start(out=x_b[b][:, :], in_=x[f * 128:(f + 1) * 128, :]).then_inc(dmx[b].h, 16)
                        for j in range(n_adds):
                            sync.dma_start(out=a_b[j][b][:, :],
                                           in_=adds[j][f * 128:(f + 1) * 128, :]).then_inc(dmx[b].h, 16)
                    if rope:
                        sync.dma_start(out=cos_sb[:, :], in_=cosd[:, :]).then_inc(drt.h, 16)
                        sync.dma_start(out=sin_sb[:, :], in_=sind[:, :]).then_inc(drt.h, 16)
                    sync.wait_ge(dmx[0].h, (NF // 2) * per_f * 16)
                    sync.wait_ge(dmx[1].h, (NF // 2) * per_f * 16)
                    if rope:
                        sync.wait_ge(drt.h, 32)
                dmx[0].n = dmx[1].n = (NF // 2) * per_f * 16
                drt.n = 32 if rope else 0

                @blkA.vector
                def _(vector):
                    vector.memset(ones_b[:, :], 1.0)
                    vector.memset(ones_r[:, :], 1.0)
                    sv.inc(vector.memset(eps_c[:, :], 1e-5))
                    for f in range(NF):
                        b = f % 2
                        vector.wait_ge(dmx[b].h, (f // 2 + 1) * per_f * 16)
                        if n_adds:
                            if f >= 2:
                                vector.wait_ge(sc.h, f - 1)
                                vector.wait_ge(gm[b].h, 16 * (f // 2))
                            sv.inc(vector.tensor_tensor(out=xw_b[b][:, :], in0=x_b[b][:, :],
                                                        in1=a_b[0][b][:, :], op=OP.add))
                            if n_adds == 2:
                                vector.wait_ge(sv.h, sv.n)
                                sv.inc(vector.tensor_tensor(out=xw_b[b][:, :], in0=xw_b[b][:, :],
                                                            in1=a_b[1][b][:, :], op=OP.add))
                            vector.wait_ge(sv.h, sv.n)
                            src = xw_b[b]
                        else:
                            if f >= 2:
                                vector.wait_ge(sc.h, f - 1)
                            src = x_b[b]
                        if f >= 2:
                            vector.wait_ge(mm.h, f - 1)
                        sv.inc(vector.tensor_copy(out=xc_b[b][:, :], in_=src[:, :]))
                    vector.wait_ge(mm.h, NF)
                    for t in range(nth):
                        sv.inc(vector.tensor_copy(out=srow[:, t * tw:(t + 1) * tw], in_=sum_ps[t][:, :]))
                    vector.wait_ge(sv.h, sv.n)
                    sv.inc(vector.tensor_scalar(out=mrow[:, :], in0=srow[:, :],
                                                scalar1=1.0 / D, scalar2=None, op0=OP.mult))
                    for t in range(nth):
                        sv.inc(vector.tensor_copy(out=qrow[:, t * tw:(t + 1) * tw], in_=ssq_ps[t][:, :]))
                    vector.wait_ge(sv.h, sv.n)
                    sv.inc(vector.tensor_scalar(out=arow[:, :], in0=qrow[:, :],
                                                scalar1=1.0 / D, scalar2=None, op0=OP.mult))
                    vector.wait_ge(sv.h, sv.n)
                    vector.wait_ge(sc.h, NF + 1)
                    sv.inc(vector.tensor_tensor(out=vrow[:, :], in0=arow[:, :],
                                                in1=msq[:, :], op=OP.subtract))
                    vector.wait_ge(sc.h, NF + 2)
                    sv.inc(vector.reciprocal(out=rrow[:, :], in_=stdr[:, :]))

                @blkA.scalar
                def _(scalar):
                    for f in range(NF):
                        b = f % 2
                        if n_adds:
                            scalar.wait_ge(sv.h, 1 + f * VA + n_adds)
                            src = xw_b[b]
                        else:
                            scalar.wait_ge(dmx[b].h, (f // 2 + 1) * per_f * 16)
                            src = x_b[b]
                        if f >= 2:
                            scalar.wait_ge(mm.h, f - 1)
                        sc.inc(scalar.activation(out=sq_b[b][:, :], in_=src[:, :], func=AF.Square))
                    sv0 = 1 + NF * VA
                    e_mrow = sv0 + nth + 1
                    e_vrow = e_mrow + nth + 2
                    scalar.wait_ge(sv.h, e_mrow)
                    sc.inc(scalar.activation(out=msq[:, :], in_=mrow[:, :], func=AF.Square))
                    scalar.wait_ge(sv.h, e_vrow)
                    sc.inc(scalar.activation(out=stdr[:, :], in_=vrow[:, :], func=AF.Sqrt, bias=eps_c[:, :]))

                @blkA.tensor
                def _(tensor):
                    for f in range(NF):
                        b = f % 2
                        tensor.wait_ge(sv.h, 1 + (f + 1) * VA)
                        tensor.wait_ge(sc.h, f + 1)
                        for t in range(nth):
                            tensor.matmul(sum_ps[t][:, :], ones_b[:, :], xc_b[b][:, t * tw:(t + 1) * tw],
                                          start=(f == 0), stop=(f == NF - 1))
                            i = tensor.matmul(ssq_ps[t][:, :], ones_b[:, :], sq_b[b][:, t * tw:(t + 1) * tw],
                                              start=(f == 0), stop=(f == NF - 1))
                        mm.inc(i)
                    sv0 = 1 + NF * VA
                    e_mrow = sv0 + nth + 1
                    e_rrow = e_mrow + nth + 3
                    tensor.wait_ge(sv.h, e_mrow)
                    for t in range(nth):
                        mm.inc(tensor.matmul(mbc[t][:, :], ones_r[:, :], mrow[:, t * tw:(t + 1) * tw],
                                             start=True, stop=True))
                    tensor.wait_ge(sv.h, e_rrow)
                    for t in range(nth):
                        mm.inc(tensor.matmul(rbc[t][:, :], ones_r[:, :], rrow[:, t * tw:(t + 1) * tw],
                                             start=True, stop=True))

                @blkA.gpsimd
                def _(gpsimd):
                    if n_adds:
                        for f in range(NF):
                            b = f % 2
                            gpsimd.wait_ge(sv.h, 1 + f * VA + n_adds)
                            gm[b].inc(gpsimd.dma_start(out=xo[f * 128:(f + 1) * 128, :],
                                                       in_=xw_b[b][:, :]), 16)
                        gpsimd.wait_ge(gm[0].h, gm[0].n)
                        gpsimd.wait_ge(gm[1].h, gm[1].n)

            with ExitStack() as eB:
                sbB = lambda n, s, d=F32: eB.enter_context(nc.sbuf_tensor(n, s, d))
                y_b = [sbB(f"y_b{i}", [128, Tc]) for i in range(2)]
                tmp = [sbB(f"tmpB{t}", [128, tw]) for t in range(nth)]
                blkB = eB.enter_context(nc.Block())
                dmb = [Sem(nc, eB, "dmb0"), Sem(nc, eB, "dmb1")]
                svb = Sem(nc, eB, "svB")
                gmb = Sem(nc, eB, "gmB")
                xsrc = xo if n_adds else x
                VB = 2 * nth

                @blkB.sync
                def _(sync):
                    for f in range(NF):
                        b = f % 2
                        if f >= 2:
                            sync.wait_ge(svb.h, (f - 1) * VB)
                        sync.dma_start(out=y_b[b][:, :],
                                       in_=xsrc[f * 128:(f + 1) * 128, :]).then_inc(dmb[b].h, 16)
                    sync.wait_ge(dmb[0].h, (NF // 2) * 16)
                    sync.wait_ge(dmb[1].h, (NF // 2) * 16)
                dmb[0].n = dmb[1].n = (NF // 2) * 16

                @blkB.vector
                def _(vector):
                    for f in range(NF):
                        b = f % 2
                        vector.wait_ge(dmb[b].h, (f // 2 + 1) * 16)
                        for t in range(nth):
                            vector.wait_ge(svb.h, svb.n)
                            svb.inc(vector.tensor_tensor(out=tmp[t][:, :],
                                                         in0=y_b[b][:, t * tw:(t + 1) * tw],
                                                         in1=mbc[t][:, :], op=OP.subtract))
                            vector.wait_ge(svb.h, svb.n)
                            svb.inc(vector.tensor_tensor(
                                out=xn_sb[:, f * Tc + t * tw: f * Tc + (t + 1) * tw],
                                in0=tmp[t][:, :], in1=rbc[t][:, :], op=OP.mult))

                @blkB.gpsimd
                def _(gpsimd):
                    for f in range(NF):
                        gpsimd.wait_ge(svb.h, (f + 1) * VB)
                        gmb.inc(gpsimd.dma_start(out=xnd[f * 128:(f + 1) * 128, :],
                                                 in_=xn_sb[:, f * Tc:(f + 1) * Tc]), 16)
                    gpsimd.wait_ge(gmb.h, gmb.n)

        groups = []
        for c0 in range(0, NF, 4):
            groups.append((wq, qd, c0, "q", rope))
        for c0 in range(0, nkc, 4):
            groups.append((wk, kd, c0, "k", rope))
        for c0 in range(0, nkc, 4):
            groups.append((wv, vd, c0, "v", False))
        boff = {"q": 0, "k": NF, "v": NF + nkc}
        bmap = {"q": bqt, "k": bkt, "v": bvt}

        with ExitStack() as eC:
            sbC = lambda n, s, d=F32: eC.enter_context(nc.sbuf_tensor(n, s, d))
            wf = [sbC(f"wf{i}", [128, 512]) for i in range(2)]
            wb = [sbC(f"wb{i}", [128, 512], BF16) for i in range(2)]
            ob = [sbC(f"obC{i}", [128, tw], BF16) for i in range(4 * nth)]
            rt = [sbC(f"rt{i}", [64, tw]) for i in range(4)]
            bcol = sbC("bcol", [128, NF + 2 * nkc]) if with_bias else None
            pg = [[eC.enter_context(nc.psum_tensor(f"pg{c}_{t}", [128, tw], F32)) for t in range(nth)]
                  for c in range(4)]
            blkC = eC.enter_context(nc.Block())
            dw = [Sem(nc, eC, "dw0"), Sem(nc, eC, "dw1")]
            dbias = Sem(nc, eC, "dbias")
            pw = Sem(nc, eC, "pw")
            mmc = Sem(nc, eC, "mmc")
            evv = Sem(nc, eC, "evv")
            gmc = Sem(nc, eC, "gmc")
            NG = len(groups)
            bias_dmas = 3 if with_bias else 0

            def ev_per_tile(gi):
                return 6 if groups[gi][4] else 1

            ev_before = [0] * (NG + 1)
            for i in range(NG):
                ev_before[i + 1] = ev_before[i] + 4 * nth * ev_per_tile(i)

            @blkC.sync
            def _(sync):
                if with_bias:
                    sync.dma_start(out=bcol[:, 0:NF], in_=bqt[:, :]).then_inc(dbias.h, 16)
                    sync.dma_start(out=bcol[:, NF:NF + nkc], in_=bkt[:, :]).then_inc(dbias.h, 16)
                    sync.dma_start(out=bcol[:, NF + nkc:NF + 2 * nkc], in_=bvt[:, :]).then_inc(dbias.h, 16)
                for gi, (wt, ot, c0, bk_, rp) in enumerate(groups):
                    for f in range(NF):
                        b = f % 2
                        n = gi * NF + f
                        if n >= 2:
                            sync.wait_ge(pw.h, n - 1)
                        sync.dma_start(out=wf[b][:, :],
                                       in_=wt[f * 128:(f + 1) * 128, c0 * 128:c0 * 128 + 512]
                                       ).then_inc(dw[n % 2].h, 16)
                sync.wait_ge(dw[0].h, ((NG * NF + 1) // 2) * 16)
                sync.wait_ge(dw[1].h, (NG * NF // 2) * 16)
            dw[0].n = ((NG * NF + 1) // 2) * 16
            dw[1].n = (NG * NF // 2) * 16

            @blkC.gpsimd
            def _(gpsimd):
                for gi, (wt, ot, c0, bk_, rp) in enumerate(groups):
                    ept = ev_per_tile(gi)
                    for f in range(NF):
                        b = f % 2
                        n = gi * NF + f
                        gpsimd.wait_ge(dw[n % 2].h, (n // 2 + 1) * 16)
                        if n >= 2:
                            gpsimd.wait_ge(mmc.h, n - 1)
                        pw.inc(gpsimd.tensor_copy(out=wb[b][:, :], in_=wf[b][:, :]))
                    for c in range(4):
                        for t in range(nth):
                            k = c * nth + t
                            gpsimd.wait_ge(evv.h, ev_before[gi] + (k + 1) * ept)
                            gmc.inc(gpsimd.dma_start(
                                out=ot[(c0 + c) * 128:(c0 + c + 1) * 128, t * tw:(t + 1) * tw],
                                in_=ob[k][:, :]), 16)
                gpsimd.wait_ge(gmc.h, gmc.n)

            @blkC.tensor
            def _(tensor):
                for gi in range(NG):
                    for f in range(NF):
                        n = gi * NF + f
                        b = f % 2
                        tensor.wait_ge(pw.h, n + 1)
                        if f == 0 and gi > 0:
                            tensor.wait_ge(evv.h, ev_before[gi])
                        for c in range(4):
                            for t in range(nth):
                                i = tensor.matmul(pg[c][t][:, :], wb[b][:, c * 128:(c + 1) * 128],
                                                  xn_sb[:, f * Tc + t * tw: f * Tc + (t + 1) * tw],
                                                  start=(f == 0), stop=(f == NF - 1))
                        mmc.inc(i)

            @blkC.vector
            def _(vector):
                if with_bias:
                    vector.wait_ge(dbias.h, 48)
                for gi, (wt, ot, c0, bk_, rp) in enumerate(groups):
                    ept = ev_per_tile(gi)
                    vector.wait_ge(mmc.h, (gi + 1) * NF)
                    for c in range(4):
                        for t in range(nth):
                            k = c * nth + t
                            if gi > 0 and k == 0:
                                vector.wait_ge(gmc.h, 16 * gi * 4 * nth)
                            p = pg[c][t]
                            if rp:
                                vector.wait_ge(evv.h, evv.n)
                                evv.inc(vector.tensor_tensor(out=rt[0][:, :], in0=p[0:64, :],
                                                             in1=cos_sb[:, t * tw:(t + 1) * tw], op=OP.mult))
                                evv.inc(vector.tensor_tensor(out=rt[1][:, :], in0=p[64:128, :],
                                                             in1=sin_sb[:, t * tw:(t + 1) * tw], op=OP.mult))
                                evv.inc(vector.tensor_tensor(out=rt[2][:, :], in0=p[64:128, :],
                                                             in1=cos_sb[:, t * tw:(t + 1) * tw], op=OP.mult))
                                evv.inc(vector.tensor_tensor(out=rt[3][:, :], in0=p[0:64, :],
                                                             in1=sin_sb[:, t * tw:(t + 1) * tw], op=OP.mult))
                                vector.wait_ge(evv.h, evv.n)
                                evv.inc(vector.tensor_tensor(out=ob[k][0:64, :], in0=rt[0][:, :],
                                                             in1=rt[1][:, :], op=OP.subtract))
                                evv.inc(vector.tensor_tensor(out=ob[k][64:128, :], in0=rt[2][:, :],
                                                             in1=rt[3][:, :], op=OP.add))
                            elif bk_ is not None and bmap[bk_] is not None:
                                evv.inc(vector.tensor_scalar(
                                    out=ob[k][:, :], in0=p[:, :],
                                    scalar1=bcol[:, boff[bk_] + c0 + c: boff[bk_] + c0 + c + 1],
                                    scalar2=None, op0=OP.add))
                            else:
                                evv.inc(vector.tensor_copy(out=ob[k][:, :], in_=p[:, :]))
    return nc


ATTN_MARKS = []
ATTN_TRUNC = None
ATTN_DEBUG = False

def build_attn(T_, n_kv, gsz, scale):
    NQT = T_ // 512
    NCH = T_ // 512
    nc = bass.Bass("TRN2", target_bir_lowering=False)
    kin = [nc.dram_tensor(f"k{h}", [128, T_], BF16, kind="ExternalInput") for h in range(n_kv)]
    vin = [nc.dram_tensor(f"v{h}", [128, T_], BF16, kind="ExternalInput") for h in range(n_kv)]
    qin = [nc.dram_tensor(f"q{i}", [128, T_], BF16, kind="ExternalInput") for i in range(n_kv * gsz)]
    dcold = nc.dram_tensor("dcol", [128, 4], F32, kind="ExternalInput")
    qkiod = nc.dram_tensor("qkio", [128, 512], F32, kind="ExternalInput")
    ctxd = [nc.dram_tensor(f"ctx{i}", [128, T_], BF16, kind="ExternalOutput") for i in range(n_kv * gsz)]
    if ATTN_DEBUG:
        dbgp = [nc.dram_tensor(f"dbgp{i}", [128, 512], BF16, kind="ExternalOutput") for i in range(4)]
        dbgs = [nc.dram_tensor(f"dbgs{i}", [128, 512], F32, kind="ExternalOutput") for i in range(4)]
        dbgrow = nc.dram_tensor("dbgrow", [1, 512], F32, kind="ExternalOutput")
        dbgrr = nc.dram_tensor("dbgrr", [1, 512], F32, kind="ExternalOutput")
        dbgbc = nc.dram_tensor("dbgbc", [128, 512], F32, kind="ExternalOutput")
        dbgctx = nc.dram_tensor("dbgctx", [128, 512], F32, kind="ExternalOutput")
    dbg_idx = [0]

    ev = []

    def E(eng, waits, fn, inc=None, k=1):
        ev.append((eng, [w for w in waits if w is not None and w[1] is not None], fn, inc, k))
        if inc is not None:
            inc.n += k
            return inc.n
        return None

    with ExitStack() as es:
        sb = lambda n, s_, d=F32: es.enter_context(nc.sbuf_tensor(n, s_, d))
        ps_ = lambda n, s_: es.enter_context(nc.psum_tensor(n, s_, F32))
        k_sb = sb("k_sb", [128, T_], BF16)
        v_sb = sb("v_sb", [128, T_], BF16)
        q_sb = [sb(f"q_sb{s_}", [128, T_], BF16) for s_ in range(2)]
        p_b = [[sb(f"p_{s_}{i}", [128, 512], BF16) for i in range(2)] for s_ in range(2)]
        pm_b = [[sb(f"pm_{s_}{i}", [128, 512], BF16) for i in range(2)] for s_ in range(2)]
        cbuf = [[sb(f"cbuf_{s_}{i}", [128, 512], BF16) for i in range(2)] for s_ in range(2)]
        bc_sb = [sb(f"bc_sb{s_}", [128, 512]) for s_ in range(2)]
        bias_sb = [sb(f"bias_sb{s_}", [128, 1]) for s_ in range(2)]
        nsq = [sb(f"nsq{i}", [128, 512], BF16) for i in range(2)]
        nrow_k = sb("nrow_k", [1, NCH])
        nrow_q = [sb(f"nrow_q{s_}", [1, NCH]) for s_ in range(2)]
        nk_t = sb("nk_t", [1, 1])
        nq_t = [sb(f"nq_t{s_}", [1, 1]) for s_ in range(2)]
        prod = [sb(f"prod{s_}", [1, 1]) for s_ in range(2)]
        cmax = [sb(f"cmax{s_}", [1, 1]) for s_ in range(2)]
        negc = [sb(f"negc{s_}", [1, 1]) for s_ in range(2)]
        srow = [sb(f"srow{s_}", [1, 512]) for s_ in range(2)]
        rr = [sb(f"rr{s_}", [1, 512]) for s_ in range(2)]
        dcol = sb("dcol_sb", [128, 4])
        dbgctx_sb = sb("dbgctx_sb", [128, 512]) if ATTN_DEBUG else None
        qkio = sb("qkio_sb", [128, 512])
        ones_b = sb("ones_battn", [128, 1], BF16)
        ones_r = sb("ones_rattn", [1, 128], F32)
        s_ps = [[ps_(f"s_ps{s_}_{i}", [128, 512]) for i in range(2)] for s_ in range(2)]
        ctx_ps = [ps_(f"ctx_ps{s_}", [128, 512]) for s_ in range(2)]
        sum_ps = [ps_(f"sum_ps{s_}", [128, 512]) for s_ in range(2)]
        blk = es.enter_context(nc.Block())
        dk = Sem(nc, es, "dk")
        dq = Sem(nc, es, "dq")
        dc = Sem(nc, es, "dc")
        mm = Sem(nc, es, "mmat")
        se = Sem(nc, es, "seat")
        svv = Sem(nc, es, "svat")
        gma = [[Sem(nc, es, f"gmat{s_}{p_}") for p_ in range(2)] for s_ in range(2)]

        E("vector", [], lambda e: e.memset(ones_b[:, :], 1.0))
        E("vector", [], lambda e: e.memset(ones_r[:, :], 1.0), svv)
        E("sync", [], lambda e: e.dma_start(out=dcol[:, :], in_=dcold[:, :]), dc, 16)
        E("sync", [], lambda e: e.dma_start(out=qkio[:, :], in_=qkiod[:, :]), dc, 16)

        def norms(src_sb, n_row, nt, dwait):
            base_mm = mm.n
            base_sv = svv.n
            mmv, sev, rdv = [], [], []
            for c in range(NCH):
                b = c % 2
                w = [dwait]
                w.append((mm, mmv[c - 2]) if c >= 2 else (mm, base_mm))
                sev.append(E("scalar", w,
                             (lambda cc, bb: lambda e: e.activation(
                                 out=nsq[bb][:, :], in_=src_sb[:, cc * 512:(cc + 1) * 512],
                                 func=AF.Square))(c, b), se))
                w2 = [(se, sev[c])]
                w2.append((svv, rdv[c - 2]) if c >= 2 else (svv, base_sv))
                mmv.append(E("tensor", w2,
                             (lambda bb: lambda e: e.matmul(sum_ps[bb][:, :], ones_b[:, :],
                                                            nsq[bb][:, :], start=True, stop=True))(b), mm))
                rdv.append(E("vector", [(mm, mmv[c])],
                             (lambda cc, bb: lambda e: e.tensor_reduce(
                                 out=n_row[:, cc:cc + 1], in_=sum_ps[bb][:, :],
                                 op=OP.max, axis=AX.X))(c, b), svv))
            return E("vector", [(svv, svv.n)],
                     lambda e: e.tensor_reduce(out=nt[:, :], in_=n_row[:, 0:NCH],
                                               op=OP.max, axis=AX.X), svv)

        prev_exp_last = [None, None]
        prev_bc_c = [None, None]
        prev_norm_v = [None, None]
        prev_sum_v = [None, None]
        dma_v = {}
        first = True

        for kv in range(n_kv):
            ATTN_MARKS.append(("kv", kv, len(ev)))
            wk_ = [] if first else [(mm, mm.n), (se, se.n), (svv, svv.n)]
            E("sync", wk_, (lambda h: lambda e: e.dma_start(out=k_sb[:, :], in_=kin[h][:, :]))(kv), dk, 16)
            E("sync", [], (lambda h: lambda e: e.dma_start(out=v_sb[:, :], in_=vin[h][:, :]))(kv), dk, 16)
            dk_done = (dk, dk.n)
            nkv = norms(k_sb, nrow_k, nk_t, dk_done)
            for p2 in range(gsz // 2):
                hA = kv * gsz + 2 * p2
                hB = hA + 1
                wq_ = [] if first else [(mm, mm.n), (se, se.n)]
                first = False
                E("sync", wq_, (lambda h: lambda e: e.dma_start(out=q_sb[0][:, :], in_=qin[h][:, :]))(hA), dq, 16)
                E("sync", [], (lambda h: lambda e: e.dma_start(out=q_sb[1][:, :], in_=qin[h][:, :]))(hB), dq, 16)
                dq_done = (dq, dq.n)
                nqv = [norms(q_sb[s_], nrow_q[s_], nq_t[s_], dq_done) for s_ in range(2)]
                bias_ready = [None, None]
                for s_ in range(2):
                    v1 = E("vector", [(svv, nqv[s_]), (svv, nkv)],
                           (lambda ss: lambda e: e.tensor_tensor(out=prod[ss][:, :], in0=nk_t[:, :],
                                                                 in1=nq_t[ss][:, :], op=OP.mult))(s_), svv)
                    c1 = E("scalar", [(svv, v1)],
                           (lambda ss: lambda e: e.activation(out=cmax[ss][:, :], in_=prod[ss][:, :],
                                                              func=AF.Sqrt, scale=scale * scale))(s_), se)
                    v2 = E("vector", [(se, c1)],
                           (lambda ss: lambda e: e.tensor_scalar(out=negc[ss][:, :], in0=cmax[ss][:, :],
                                                                 scalar1=-1.0, scalar2=None,
                                                                 op0=OP.mult))(s_), svv)
                    m1 = E("tensor", [(svv, v2), (se, prev_bc_c[s_]), (se, prev_exp_last[s_])],
                           (lambda ss: lambda e: e.matmul(s_ps[ss][0][:, 0:1], ones_r[:, :],
                                                          negc[ss][:, :], start=True, stop=True))(s_), mm)
                    bias_ready[s_] = E("scalar", [(mm, m1)],
                                       (lambda ss: lambda e: e.copy(out=bias_sb[ss][:, :],
                                                                    in_=s_ps[ss][0][:, 0:1]))(s_), se)
                    prev_bc_c[s_] = bias_ready[s_]
                dbg_wait = {}
                for qt in range(NQT):
                    ATTN_MARKS.append(("qt", (kv, p2, qt), len(ev)))
                    nkt = 4 * qt + 4
                    exp_v = {}
                    pv_v = {}
                    psrc = {}
                    pready = {}

                    def emit_pv(i):
                        for s_ in range(2):
                            pbuf = psrc[(s_, i)]
                            w = [pready[(s_, i)]]
                            if i == 0:
                                w += [(svv, prev_norm_v[s_]), (svv, prev_sum_v[s_]), dk_done]
                            E("tensor", w,
                              (lambda ss, ii, nn, pb: lambda e: e.matmul(
                                  ctx_ps[ss][:, :], v_sb[:, ii * 128:(ii + 1) * 128],
                                  pb[:, :], start=(ii == 0), stop=(ii == nn - 1)))(s_, i, nkt, pbuf), mm)
                            r = E("tensor", [],
                                  (lambda ss, ii, nn, pb: lambda e: e.matmul(
                                      sum_ps[ss][:, :], ones_b[:, :],
                                      pb[:, :], start=(ii == 0), stop=(ii == nn - 1)))(s_, i, nkt, pbuf), mm)
                        pv_v[i] = r

                    for i in range(nkt):
                        b = i % 2
                        for s_ in range(2):
                            w = []
                            if i >= 2:
                                w.append((se, exp_v[(s_, i - 2)]))
                            elif b == 0:
                                w.append((se, prev_exp_p0[s_]))
                            else:
                                w.append((se, prev_exp_p1[s_]))
                            if i == 0:
                                w += [dq_done, dk_done]
                            if (("s", i % 2) in dbg_wait) and s_ == 0:
                                w.append(dbg_wait[("s", i % 2)])
                            sv_ = E("tensor", w,
                                    (lambda ss, ii, qq: lambda e: e.matmul(
                                        s_ps[ss][ii % 2][:, :], k_sb[:, ii * 128:(ii + 1) * 128],
                                        q_sb[ss][:, qq * 512:(qq + 1) * 512],
                                        start=True, stop=True))(s_, i, qt), mm)
                            exp_w = [(mm, sv_), (se, bias_ready[s_])]
                            if i >= 2:
                                exp_w.append((mm, pv_v[i - 2]))
                            else:
                                exp_w.append((svv, prev_norm_v[s_]))
                            exp_v[(s_, i)] = E("scalar", exp_w,
                                               (lambda ss, ii: lambda e: e.activation(
                                                   out=p_b[ss][ii % 2][:, :], in_=s_ps[ss][ii % 2][:, :],
                                                   func=AF.Exp, scale=scale, bias=bias_sb[ss][:, :]))(s_, i), se)
                        diag = i >= 4 * qt
                        for s_ in range(2):
                            if diag:
                                o = i - 4 * qt
                                w = [(se, exp_v[(s_, i)]), (dc, 32)]
                                if i >= 2:
                                    w.append((mm, pv_v[i - 2]))
                                else:
                                    w.append((svv, prev_norm_v[s_]))
                                if (("p", i % 2) in dbg_wait) and s_ == 0:
                                    w.append(dbg_wait[("p", i % 2)])
                                mv = E("vector", w,
                                       (lambda ss, ii, oo: lambda e: e.scalar_tensor_tensor(
                                           out=pm_b[ss][ii % 2][:, :], in0=qkio[:, :],
                                           scalar=dcol[:, oo:oo + 1], in1=p_b[ss][ii % 2][:, :],
                                           op0=OP.is_ge, op1=OP.mult))(s_, i, o), svv)
                                psrc[(s_, i)] = pm_b[s_][i % 2]
                                pready[(s_, i)] = (svv, mv)
                                if ATTN_DEBUG and kv == 0 and p2 == 0 and qt == 0 and s_ == 0:
                                    dseq = gma[1][1]
                                    dv = E("gpsimd", [(svv, mv), (dseq, dseq.n)],
                                           (lambda ii: lambda e: e.dma_start(
                                               out=dbgp[ii][:, :], in_=pm_b[0][ii % 2][:, :]))(i),
                                           dseq, 16)
                                    dbg_wait[("p", i % 2)] = (dseq, dv)
                                    wdb = [(mm, sv_), (svv, mv)]
                                    if ("bc1",) in dbg_wait:
                                        wdb.append(dbg_wait[("bc1",)])
                                    cdb = E("scalar", wdb,
                                            (lambda ii: lambda e: e.copy(
                                                out=bc_sb[1][:, :], in_=s_ps[0][ii % 2][:, :]))(i), se)
                                    dv2 = E("gpsimd", [(se, cdb), (dseq, dseq.n)],
                                           (lambda ii: lambda e: e.dma_start(
                                               out=dbgs[ii][:, :], in_=bc_sb[1][:, :]))(i),
                                           dseq, 16)
                                    dbg_wait[("s", i % 2)] = (se, cdb)
                                    dbg_wait[("bc1",)] = (dseq, dv2)
                            else:
                                psrc[(s_, i)] = p_b[s_][i % 2]
                                pready[(s_, i)] = (se, exp_v[(s_, i)])
                        if i >= 1:
                            emit_pv(i - 1)
                    emit_pv(nkt - 1)
                    par = qt % 2
                    for s_ in range(2):
                        wv1 = [(mm, pv_v[nkt - 1])]
                        if s_ == 0 and ("epidma",) in dbg_wait:
                            wv1.append(dbg_wait[("epidma",)])
                        v1 = E("vector", wv1,
                               (lambda ss: lambda e: e.tensor_copy(out=srow[ss][:, :],
                                                                   in_=sum_ps[ss][:, :]))(s_), svv)
                        prev_sum_v[s_] = v1
                        v2 = E("vector", [(svv, v1)],
                               (lambda ss: lambda e: e.reciprocal(out=rr[ss][:, :],
                                                                  in_=srow[ss][:, :]))(s_), svv)
                        m1 = E("tensor", [(svv, v2), (se, exp_v[(s_, nkt - 1)]), (se, exp_v[(s_, nkt - 2)])],
                               (lambda ss: lambda e: e.matmul(s_ps[ss][0][:, :], ones_r[:, :],
                                                              rr[ss][:, :], start=True, stop=True))(s_), mm)
                        wbc = [(mm, m1)]
                        if s_ == 1 and ("bc1",) in dbg_wait:
                            wbc.append(dbg_wait[("bc1",)])
                        c1 = E("scalar", wbc,
                               (lambda ss: lambda e: e.copy(out=bc_sb[ss][:, :],
                                                            in_=s_ps[ss][0][:, :]))(s_), se)
                        prev_bc_c[s_] = c1
                        if ATTN_DEBUG and kv == 0 and p2 == 0 and qt == 0 and s_ == 0:
                            dseq = gma[1][1]
                            E("gpsimd", [(svv, v1), (dseq, dseq.n)],
                              lambda e: e.dma_start(out=dbgrow[:, :], in_=srow[0][:, :]), dseq, 16)
                            E("gpsimd", [(svv, v2), (dseq, dseq.n)],
                              lambda e: e.dma_start(out=dbgrr[:, :], in_=rr[0][:, :]), dseq, 16)
                            E("gpsimd", [(se, c1), (dseq, dseq.n)],
                              lambda e: e.dma_start(out=dbgbc[:, :], in_=bc_sb[0][:, :]), dseq, 16)
                            cdbg = E("scalar", [(mm, pv_v[nkt - 1]), (se, se.n)],
                                     lambda e: e.copy(out=dbgctx_sb[:, :], in_=ctx_ps[0][:, :]), se)
                            dbg_wait[("epidma",)] = (dseq, E("gpsimd", [(se, cdbg), (dseq, dseq.n)],
                              lambda e: e.dma_start(out=dbgctx[:, :], in_=dbgctx_sb[:, :]), dseq, 16))
                        w = [(se, c1)]
                        if (s_, par) in dma_v:
                            w.append((gma[s_][par], dma_v[(s_, par)]))
                        v3 = E("vector", w,
                               (lambda ss, pp: lambda e: e.tensor_tensor(
                                   out=cbuf[ss][pp][:, :], in0=ctx_ps[ss][:, :],
                                   in1=bc_sb[ss][:, :], op=OP.mult))(s_, par), svv)
                        prev_norm_v[s_] = v3
                        h = hA if s_ == 0 else hB
                        dma_v[(s_, par)] = E("gpsimd", [(svv, v3)],
                                             (lambda ss, pp, hh, qq: lambda e: e.dma_start(
                                                 out=ctxd[hh][:, qq * 512:(qq + 1) * 512],
                                                 in_=cbuf[ss][pp][:, :]))(s_, par, h, qt), gma[s_][par], 16)
                        prev_exp_last[s_] = exp_v[(s_, nkt - 1)]

        if ATTN_TRUNC is not None:
            del ev[ATTN_TRUNC:]
        for s_ in range(2):
            for p_ in range(2):
                gv = max([e_[4] if False else 0 for e_ in ev] + [0])
                tot = sum(k_ for (en_, w_, f_, i_, k_) in ev if i_ is gma[s_][p_])
                E("gpsimd", [(gma[s_][p_], tot)], lambda e: e.nop())
        ATTN_MARKS.append(("end", None, len(ev)))

        engines = {"sync": blk.sync, "tensor": blk.tensor, "vector": blk.vector,
                   "scalar": blk.scalar, "gpsimd": blk.gpsimd}
        for name, deco in engines.items():
            def body(eng, _name=name):
                for (en, waits, fn, inc, k) in ev:
                    if en != _name:
                        continue
                    for (sm, val) in waits:
                        eng.wait_ge(sm.h, val)
                    inst = fn(eng)
                    if inc is not None:
                        inst.then_inc(inc.h, k)
            deco(body)
    return nc


def build_ffn0(T_):
    NT = T_ // 512
    nc = bass.Bass("TRN2", target_bir_lowering=False)
    xnd = nc.dram_tensor("xn", [D, T_], BF16, kind="ExternalInput")
    qwd = nc.dram_tensor("qw", [D, 512], I32, kind="ExternalInput")
    sczd = nc.dram_tensor("scz", [1, NF * 512], F32, kind="ExternalInput")
    zrzd = nc.dram_tensor("zrz", [1, NF * 512], F32, kind="ExternalInput")
    bd = nc.dram_tensor("bias", [128, 4], F32, kind="ExternalInput")
    yd = nc.dram_tensor("y", [512, T_], BF16, kind="ExternalOutput")

    with ExitStack() as es:
        sb = lambda n, s_, d=F32: es.enter_context(nc.sbuf_tensor(n, s_, d))
        ones_r = sb("ones_rf", [1, 128], F32)
        wt_all = sb("wt_all", [128, NF * 512], BF16)
        scz = sb("scz_sb", [1, NF * 512])
        zrz = sb("zrz_sb", [1, NF * 512])
        bcol = sb("bcolf", [128, 4])

        with ExitStack() as e1:
            sb1 = lambda n, s_, d=F32: e1.enter_context(nc.sbuf_tensor(n, s_, d))
            qwb = [sb1(f"qwb{i}", [128, 512], I32) for i in range(2)]
            qf = [sb1(f"qf{i}", [128, 512]) for i in range(2)]
            t2 = [sb1(f"t2_{i}", [128, 512]) for i in range(2)]
            psc = [e1.enter_context(nc.psum_tensor(f"psc{i}", [128, 512], F32)) for i in range(2)]
            pzr = [e1.enter_context(nc.psum_tensor(f"pzr{i}", [128, 512], F32)) for i in range(2)]
            blk1 = e1.enter_context(nc.Block())
            dq_ = [Sem(nc, e1, "dqw0"), Sem(nc, e1, "dqw1")]
            drow = Sem(nc, e1, "drow")
            mm1 = Sem(nc, e1, "mmf1")
            sv1 = Sem(nc, e1, "svf1")

            @blk1.sync
            def _(sync):
                sync.dma_start(out=scz[:, :], in_=sczd[:, :]).then_inc(drow.h, 16)
                sync.dma_start(out=zrz[:, :], in_=zrzd[:, :]).then_inc(drow.h, 16)
                sync.dma_start(out=bcol[:, :], in_=bd[:, :]).then_inc(drow.h, 16)
                for g in range(NF):
                    b = g % 2
                    if g >= 2:
                        sync.wait_ge(sv1.h, 3 * (g - 1))
                    sync.dma_start(out=qwb[b][:, :],
                                   in_=qwd[g * 128:(g + 1) * 128, :]).then_inc(dq_[b].h, 16)
                sync.wait_ge(dq_[0].h, (NF // 2) * 16)
                sync.wait_ge(dq_[1].h, (NF // 2) * 16)
                sync.wait_ge(drow.h, 48)

            @blk1.tensor
            def _(tensor):
                tensor.wait_ge(drow.h, 48)
                tensor.wait_ge(sv1.h, 1)
                for g in range(NF):
                    b = g % 2
                    if g >= 2:
                        tensor.wait_ge(sv1.h, 1 + 3 * (g - 1))
                    tensor.matmul(psc[b][:, :], ones_r[:, :], scz[:, g * 512:(g + 1) * 512],
                                  start=True, stop=True)
                    mm1.inc(tensor.matmul(pzr[b][:, :], ones_r[:, :], zrz[:, g * 512:(g + 1) * 512],
                                          start=True, stop=True))

            @blk1.vector
            def _(vector):
                sv1.inc(vector.memset(ones_r[:, :], 1.0))
                for g in range(NF):
                    b = g % 2
                    vector.wait_ge(dq_[b].h, (g // 2 + 1) * 16)
                    sv1.inc(vector.tensor_copy(out=qf[b][:, :], in_=qwb[b][:, :]))
                    vector.wait_ge(mm1.h, g + 1)
                    vector.wait_ge(sv1.h, sv1.n)
                    sv1.inc(vector.tensor_tensor(out=t2[b][:, :], in0=qf[b][:, :],
                                                 in1=psc[b][:, :], op=OP.mult))
                    vector.wait_ge(sv1.h, sv1.n)
                    sv1.inc(vector.tensor_tensor(out=wt_all[:, g * 512:(g + 1) * 512],
                                                 in0=t2[b][:, :], in1=pzr[b][:, :], op=OP.add))

        with ExitStack() as e2:
            sb2 = lambda n, s_, d=F32: e2.enter_context(nc.sbuf_tensor(n, s_, d))
            xb = [sb2(f"xbf{i}", [128, 512], BF16) for i in range(2)]
            ob = [sb2(f"obf{i}", [128, 512], BF16) for i in range(8)]
            pg = [e2.enter_context(nc.psum_tensor(f"pgf{c}", [128, 512], F32)) for c in range(4)]
            blk2 = e2.enter_context(nc.Block())
            dx = [Sem(nc, e2, "dxf0"), Sem(nc, e2, "dxf1")]
            mm2 = Sem(nc, e2, "mmf2")
            ev2 = Sem(nc, e2, "evf2")
            gm2 = Sem(nc, e2, "gmf2")

            @blk2.sync
            def _(sync):
                for t in range(NT):
                    for g in range(NF):
                        n = t * NF + g
                        b = n % 2
                        if n >= 2:
                            sync.wait_ge(mm2.h, n - 1)
                        sync.dma_start(out=xb[b][:, :],
                                       in_=xnd[g * 128:(g + 1) * 128, t * 512:(t + 1) * 512]
                                       ).then_inc(dx[b].h, 16)
                sync.wait_ge(dx[0].h, (NT * NF // 2) * 16)
                sync.wait_ge(dx[1].h, (NT * NF // 2) * 16)

            @blk2.tensor
            def _(tensor):
                for t in range(NT):
                    for g in range(NF):
                        n = t * NF + g
                        b = n % 2
                        tensor.wait_ge(dx[b].h, (n // 2 + 1) * 16)
                        if g == 0 and t > 0:
                            tensor.wait_ge(ev2.h, 4 * t)
                        for c in range(4):
                            i = tensor.matmul(pg[c][:, :], wt_all[:, g * 512 + c * 128: g * 512 + (c + 1) * 128],
                                              xb[b][:, :], start=(g == 0), stop=(g == NF - 1))
                        mm2.inc(i)

            @blk2.vector
            def _(vector):
                for t in range(NT):
                    vector.wait_ge(mm2.h, (t + 1) * NF)
                    for c in range(4):
                        k = (t % 2) * 4 + c
                        if t >= 2:
                            vector.wait_ge(gm2.h, 16 * ((t - 2) * 4 + c + 1))
                        ev2.inc(vector.tensor_scalar(out=ob[k][:, :], in0=pg[c][:, :],
                                                     scalar1=bcol[:, c:c + 1], scalar2=None,
                                                     op0=OP.add))

            @blk2.gpsimd
            def _(gpsimd):
                for t in range(NT):
                    for c in range(4):
                        k = (t % 2) * 4 + c
                        gpsimd.wait_ge(ev2.h, t * 4 + c + 1)
                        gm2.inc(gpsimd.dma_start(out=yd[c * 128:(c + 1) * 128, t * 512:(t + 1) * 512],
                                                 in_=ob[k][:, :]), 16)
                gpsimd.wait_ge(gm2.h, gm2.n)
    return nc


def build_wo_base(T_):
    NT = T_ // 512
    nc = bass.Bass("TRN2", target_bir_lowering=False)
    cxd = nc.dram_tensor("cx", [D, T_], BF16, kind="ExternalInput")
    xnd = nc.dram_tensor("xn", [D, T_], BF16, kind="ExternalInput")
    wod = nc.dram_tensor("wo", [D, 512], F32, kind="ExternalInput")
    wbd = nc.dram_tensor("wb", [D, 512], F32, kind="ExternalInput")
    zd = nc.dram_tensor("z", [512, T_], BF16, kind="ExternalOutput")

    with ExitStack() as es:
        sb = lambda n, s_, d=F32: es.enter_context(nc.sbuf_tensor(n, s_, d))
        wo_all = sb("wo_all", [128, NF * 512], BF16)
        wb_all = sb("wb_all", [128, NF * 512], BF16)

        with ExitStack() as e1:
            sb1 = lambda n, s_, d=F32: e1.enter_context(nc.sbuf_tensor(n, s_, d))
            wf = [sb1(f"wfw{i}", [128, 512]) for i in range(2)]
            blk1 = e1.enter_context(nc.Block())
            dw_ = [Sem(nc, e1, "dww0"), Sem(nc, e1, "dww1")]
            sv1 = Sem(nc, e1, "svw1")

            @blk1.sync
            def _(sync):
                for j in range(2 * NF):
                    b = j % 2
                    src = wod if j < NF else wbd
                    g = j % NF
                    if j >= 2:
                        sync.wait_ge(sv1.h, j - 1)
                    sync.dma_start(out=wf[b][:, :],
                                   in_=src[g * 128:(g + 1) * 128, :]).then_inc(dw_[b].h, 16)
                sync.wait_ge(dw_[0].h, NF * 16)
                sync.wait_ge(dw_[1].h, NF * 16)

            @blk1.vector
            def _(vector):
                for j in range(2 * NF):
                    b = j % 2
                    dst = wo_all if j < NF else wb_all
                    g = j % NF
                    vector.wait_ge(dw_[b].h, (j // 2 + 1) * 16)
                    sv1.inc(vector.tensor_copy(out=dst[:, g * 512:(g + 1) * 512], in_=wf[b][:, :]))

        with ExitStack() as e2:
            sb2 = lambda n, s_, d=F32: e2.enter_context(nc.sbuf_tensor(n, s_, d))
            cb = [sb2(f"cbw{i}", [128, 512], BF16) for i in range(2)]
            xb = [sb2(f"xbw{i}", [128, 512], BF16) for i in range(2)]
            ob = [sb2(f"obw{i}", [128, 512], BF16) for i in range(8)]
            pg = [e2.enter_context(nc.psum_tensor(f"pgw{c}", [128, 512], F32)) for c in range(4)]
            blk2 = e2.enter_context(nc.Block())
            dc_ = [Sem(nc, e2, "dcw0"), Sem(nc, e2, "dcw1")]
            dx_ = [Sem(nc, e2, "dxw0"), Sem(nc, e2, "dxw1")]
            mm2 = Sem(nc, e2, "mmw2")
            ev2 = Sem(nc, e2, "evw2")
            gm2 = Sem(nc, e2, "gmw2")

            @blk2.sync
            def _(sync):
                for t in range(NT):
                    for g in range(NF):
                        n = t * NF + g
                        b = n % 2
                        if n >= 2:
                            sync.wait_ge(mm2.h, n - 1)
                        sync.dma_start(out=cb[b][:, :],
                                       in_=cxd[g * 128:(g + 1) * 128, t * 512:(t + 1) * 512]
                                       ).then_inc(dc_[b].h, 16)
                        sync.dma_start(out=xb[b][:, :],
                                       in_=xnd[g * 128:(g + 1) * 128, t * 512:(t + 1) * 512]
                                       ).then_inc(dx_[b].h, 16)
                sync.wait_ge(dc_[0].h, (NT * NF // 2) * 16)
                sync.wait_ge(dc_[1].h, (NT * NF // 2) * 16)
                sync.wait_ge(dx_[0].h, (NT * NF // 2) * 16)
                sync.wait_ge(dx_[1].h, (NT * NF // 2) * 16)

            @blk2.tensor
            def _(tensor):
                for t in range(NT):
                    for g in range(NF):
                        n = t * NF + g
                        b = n % 2
                        tensor.wait_ge(dc_[b].h, (n // 2 + 1) * 16)
                        tensor.wait_ge(dx_[b].h, (n // 2 + 1) * 16)
                        if g == 0 and t > 0:
                            tensor.wait_ge(ev2.h, 4 * t)
                        for c in range(4):
                            tensor.matmul(pg[c][:, :], wo_all[:, g * 512 + c * 128: g * 512 + (c + 1) * 128],
                                          cb[b][:, :], start=(g == 0), stop=False)
                            i = tensor.matmul(pg[c][:, :], wb_all[:, g * 512 + c * 128: g * 512 + (c + 1) * 128],
                                              xb[b][:, :], start=False, stop=(g == NF - 1))
                        mm2.inc(i)

            @blk2.vector
            def _(vector):
                for t in range(NT):
                    vector.wait_ge(mm2.h, (t + 1) * NF)
                    for c in range(4):
                        k = (t % 2) * 4 + c
                        if t >= 2:
                            vector.wait_ge(gm2.h, 16 * ((t - 2) * 4 + c + 1))
                        ev2.inc(vector.tensor_copy(out=ob[k][:, :], in_=pg[c][:, :]))

            @blk2.gpsimd
            def _(gpsimd):
                for t in range(NT):
                    for c in range(4):
                        k = (t % 2) * 4 + c
                        gpsimd.wait_ge(ev2.h, t * 4 + c + 1)
                        gm2.inc(gpsimd.dma_start(out=zd[c * 128:(c + 1) * 128, t * 512:(t + 1) * 512],
                                                 in_=ob[k][:, :]), 16)
                gpsimd.wait_ge(gm2.h, gm2.n)
    return nc


def build_lora(Tc):
    NTC = Tc // 128
    twL = min(512, Tc)
    nth = Tc // twL
    nc = bass.Bass("TRN2", target_bir_lowering=False)
    xnd = nc.dram_tensor("xn", [D, Tc], BF16, kind="ExternalInput")
    gwd = nc.dram_tensor("gw", [D, 28], F32, kind="ExternalInput")
    lad = nc.dram_tensor("la", [D, 224], F32, kind="ExternalInput")
    lb1d = nc.dram_tensor("lb1", [128, D], F32, kind="ExternalInput")
    lb2d = nc.dram_tensor("lb2", [96, D], F32, kind="ExternalInput")
    sel1d = nc.dram_tensor("sel1", [28, 128], BF16, kind="ExternalInput")
    sel2d = nc.dram_tensor("sel2", [28, 96], BF16, kind="ExternalInput")
    identd = nc.dram_tensor("ident", [128, 128], BF16, kind="ExternalInput")
    lod = nc.dram_tensor("lo", [D, Tc], BF16, kind="ExternalOutput")

    S = Sched()
    with ExitStack() as es:
        sb = lambda n, s_, d=F32: es.enter_context(nc.sbuf_tensor(n, s_, d))
        ps_ = lambda n, s_, d=F32: es.enter_context(nc.psum_tensor(n, s_, d))
        xn_sb = sb("xn_sbL", [128, NF * Tc], BF16)
        gwb = sb("gwb", [128, NF * 28], BF16)
        lab = sb("lab", [128, NF * 224], BF16)
        lb1b = sb("lb1b", [128, D], BF16)
        lb2b = sb("lb2b", [96, D], BF16)
        sel1 = sb("sel1_sb", [28, 128], BF16)
        sel2 = sb("sel2_sb", [28, 96], BF16)
        ident = sb("ident_sb", [128, 128], BF16)
        gt_sb = sb("gt_sb", [28, Tc], BF16)
        hs = [sb("hs1", [128, Tc], BF16), sb("hs2", [96, Tc], BF16)]
        ge_sb = [sb("ge_sb1", [128, Tc]), sb("ge_sb2", [96, Tc])]
        wf = [sb(f"wfl{i}", [128, 512]) for i in range(2)]
        lg = sb("lg", [128, 28])
        m1 = sb("m1c", [128, 1]); m2 = sb("m2c", [128, 1])
        eq1 = sb("eq1", [128, 28]); eq2 = sb("eq2", [128, 28])
        l2m = sb("l2m", [128, 28]); dd = sb("ddc", [128, 1])
        p2c = sb("p2c", [128, 1]); p1c = sb("p1c", [128, 1])
        g1t = sb("g1t", [128, 28]); gat = sb("gat", [128, 28])
        gab = sb("gab", [128, 28], BF16)
        ob = [sb(f"obl{i}", [128, 512], BF16) for i in range(4)]
        dl = Sem(nc, es, "dll")
        dwl = [Sem(nc, es, "dwl0"), Sem(nc, es, "dwl1")]
        svl = Sem(nc, es, "svl")
        scl = Sem(nc, es, "scl")
        mml = Sem(nc, es, "mml")
        gml = [Sem(nc, es, "gml0"), Sem(nc, es, "gml1")]

        S.E("sync", [], lambda e: e.dma_start(
            out=xn_sb[:, :], in_=bass.AP(xnd, 0, [[Tc, 128], [128 * Tc, NF], [1, Tc]])), dl, 16)
        S.E("sync", [], lambda e: e.dma_start(out=sel1[:, :], in_=sel1d[:, :]), dl, 16)
        S.E("sync", [], lambda e: e.dma_start(out=sel2[:, :], in_=sel2d[:, :]), dl, 16)
        S.E("sync", [], lambda e: e.dma_start(out=ident[:, :], in_=identd[:, :]), dl, 16)
        dl_done = (dl, dl.n)
        jobs = []
        for g in range(NF):
            jobs.append((gwd, g * 128, 0, gwb, g * 28, 28, 128))
        for g in range(NF):
            jobs.append((lad, g * 128, 0, lab, g * 224, 224, 128))
        for t in range(D // 512):
            jobs.append((lb1d, 0, t * 512, lb1b, t * 512, 512, 128))
        for t in range(D // 512):
            jobs.append((lb2d, 0, t * 512, lb2b, t * 512, 512, 96))
        castv = []
        for j, (src, soff, coff, dst, doff, w, rows) in enumerate(jobs):
            b = j % 2
            wv = [] if j < 2 else [(svl, castv[j - 2])]
            dv = S.E("sync", wv, (lambda sr, so, co, ww, rr, bb: lambda e: e.dma_start(
                out=wf[bb][0:rr, 0:ww], in_=sr[so:so + rr, co:co + ww]))(src, soff, coff, w, rows, b),
                dwl[b], 16)
            castv.append(S.E("vector", [(dwl[b], dv)],
                             (lambda ds, do, ww, rr, bb: lambda e: e.tensor_copy(
                                 out=ds[0:rr, do:do + ww], in_=wf[bb][0:rr, 0:ww]))(dst, doff, w, rows, b),
                             svl))
        cast_gw = castv[NF - 1]
        cast_la = castv[2 * NF - 1]
        cast_lb = castv[-1]

        es_gate = ExitStack()
        pl = es_gate.enter_context(nc.psum_tensor("plL", [128, 28], F32))
        pt = es_gate.enter_context(nc.psum_tensor("ptL", [28, 128], BF16))
        prev_lg = None
        prev_tp = None
        prev_gtc = None
        for tch in range(NTC):
            wv = [dl_done, (svl, cast_gw)]
            if prev_lg is not None:
                wv.append((scl, prev_lg))
            for g in range(NF):
                i = S.E("tensor", wv if g == 0 else [],
                        (lambda gg, tt: lambda e: e.matmul(
                            pl[:, :], xn_sb[:, gg * Tc + tt * 128: gg * Tc + (tt + 1) * 128],
                            gwb[:, gg * 28:(gg + 1) * 28],
                            start=(gg == 0), stop=(gg == NF - 1)))(g, tch), mml)
            m_pl = i
            prev_lg = S.E("scalar", [(mml, m_pl)],
                          lambda e: e.copy(out=lg[:, :], in_=pl[:, :]), scl)
            v = S.E("vector", [(scl, prev_lg), (svl, svl.n)],
                    lambda e: e.tensor_reduce(out=m1[:, :], in_=lg[:, :], op=OP.max, axis=AX.X), svl)
            v = S.E("vector", [(svl, v)],
                    lambda e: e.tensor_scalar(out=eq1[:, :], in0=lg[:, :], scalar1=m1[:, :],
                                              scalar2=None, op0=OP.is_equal), svl)
            v = S.E("vector", [(svl, v)],
                    lambda e: e.scalar_tensor_tensor(out=l2m[:, :], in0=eq1[:, :], scalar=-1e30,
                                                     in1=lg[:, :], op0=OP.mult, op1=OP.add), svl)
            v = S.E("vector", [(svl, v)],
                    lambda e: e.tensor_reduce(out=m2[:, :], in_=l2m[:, :], op=OP.max, axis=AX.X), svl)
            v_dd = S.E("vector", [(svl, v)],
                       lambda e: e.tensor_tensor(out=dd[:, :], in0=m2[:, :], in1=m1[:, :],
                                                 op=OP.subtract), svl)
            c_p2 = S.E("scalar", [(svl, v_dd)],
                       lambda e: e.activation(out=p2c[:, :], in_=dd[:, :], func=AF.Sigmoid), scl)
            v = S.E("vector", [(scl, c_p2)],
                    lambda e: e.tensor_scalar(out=p1c[:, :], in0=p2c[:, :], scalar1=-1.0,
                                              scalar2=1.0, op0=OP.mult, op1=OP.add), svl)
            v = S.E("vector", [(svl, v)],
                    lambda e: e.tensor_scalar(out=eq2[:, :], in0=l2m[:, :], scalar1=m2[:, :],
                                              scalar2=None, op0=OP.is_equal), svl)
            v = S.E("vector", [(svl, v)],
                    lambda e: e.tensor_scalar(out=g1t[:, :], in0=eq1[:, :], scalar1=p1c[:, :],
                                              scalar2=None, op0=OP.mult), svl)
            v = S.E("vector", [(svl, v)],
                    lambda e: e.scalar_tensor_tensor(out=gat[:, :], in0=eq2[:, :], scalar=p2c[:, :],
                                                     in1=g1t[:, :], op0=OP.mult, op1=OP.add), svl)
            wv = [(svl, v)]
            if prev_tp is not None:
                wv.append((mml, prev_tp))
            v_gab = S.E("vector", wv, lambda e: e.tensor_copy(out=gab[:, :], in_=gat[:, :]), svl)
            wv = [(svl, v_gab), dl_done]
            if prev_gtc is not None:
                wv.append((scl, prev_gtc))
            prev_tp = S.E("tensor", wv,
                          lambda e: e.transpose(pt[:, :], gab[:, :], ident[:, :]), mml)
            prev_gtc = S.E("scalar", [(mml, prev_tp)],
                           (lambda tt: lambda e: e.copy(out=gt_sb[:, tt * 128:(tt + 1) * 128],
                                                        in_=pt[:, :]))(tch), scl)
        gate_done = prev_gtc
        es_gate.close()

        es_h = ExitStack()
        widths = [128, 96]
        ph = [[es_h.enter_context(nc.psum_tensor(f"phL{er}_{t}", [widths[er], twL], F32))
               for t in range(nth)] for er in range(2)]
        pge = [[es_h.enter_context(nc.psum_tensor(f"pgeL{er}_{t}", [widths[er], twL], F32))
                for t in range(nth)] for er in range(2)]
        sels = [sel1, sel2]
        hs_v = []
        for er in range(2):
            wdt = widths[er]
            for t in range(nth):
                for g in range(NF):
                    i = S.E("tensor", [dl_done, (svl, cast_la), (scl, gate_done)] if g == 0 else [],
                            (lambda ee, tt, gg, ww: lambda e: e.matmul(
                                ph[ee][tt][:, :],
                                lab[:, gg * 224 + ee * 128: gg * 224 + ee * 128 + ww],
                                xn_sb[:, gg * Tc + tt * twL: gg * Tc + (tt + 1) * twL],
                                start=(gg == 0), stop=(gg == NF - 1)))(er, t, g, wdt), mml)
                m_ph = i
                m_ge = S.E("tensor", [(scl, gate_done)],
                           (lambda ee, tt, ww: lambda e: e.matmul(
                               pge[ee][tt][:, :], sels[ee][:, 0:ww],
                               gt_sb[:, tt * twL:(tt + 1) * twL],
                               start=True, stop=True))(er, t, wdt), mml)
                c_ge = S.E("scalar", [(mml, m_ge)],
                           (lambda ee, tt, ww: lambda e: e.copy(
                               out=ge_sb[ee][0:ww, tt * twL:(tt + 1) * twL],
                               in_=pge[ee][tt][:, :]))(er, t, wdt), scl)
                hs_v.append(S.E("vector", [(mml, m_ph), (scl, c_ge), (svl, svl.n)],
                                (lambda ee, tt, ww: lambda e: e.tensor_tensor(
                                    out=hs[ee][0:ww, tt * twL:(tt + 1) * twL],
                                    in0=ph[ee][tt][:, :],
                                    in1=ge_sb[ee][0:ww, tt * twL:(tt + 1) * twL],
                                    op=OP.mult))(er, t, wdt), svl))
        hs_done = hs_v[-1]
        es_h.close()

        plo = [ps_(f"ploL{i}", [128, twL]) for i in range(2)]
        prev_ev = {}
        dma_vL = {}
        for oc in range(NF):
            for t in range(nth):
                n = oc * nth + t
                b = n % 2
                wv = [(svl, hs_done), (svl, cast_lb)]
                if b in prev_ev and prev_ev[b] is not None:
                    wv.append((svl, prev_ev[b]))
                S.E("tensor", wv,
                    (lambda o_, tt: lambda e: e.matmul(
                        plo[(o_ * nth + tt) % 2][:, :], lb1b[:, o_ * 128:(o_ + 1) * 128],
                        hs[0][:, tt * twL:(tt + 1) * twL], start=True, stop=False))(oc, t), mml)
                m_lo = S.E("tensor", [],
                           (lambda o_, tt: lambda e: e.matmul(
                               plo[(o_ * nth + tt) % 2][:, :], lb2b[:, o_ * 128:(o_ + 1) * 128],
                               hs[1][:, tt * twL:(tt + 1) * twL], start=False, stop=True))(oc, t), mml)
                wv = [(mml, m_lo)]
                if (b,) in dma_vL:
                    wv.append((gml[b], dma_vL[(b,)]))
                ve = S.E("vector", wv,
                         (lambda bb: lambda e: e.tensor_copy(out=ob[bb][:, 0:twL],
                                                             in_=plo[bb][:, :]))(b), svl)
                prev_ev[b] = ve
                dma_vL[(b,)] = S.E("gpsimd", [(svl, ve)],
                                   (lambda o_, tt, bb: lambda e: e.dma_start(
                                       out=lod[o_ * 128:(o_ + 1) * 128, tt * twL:(tt + 1) * twL],
                                       in_=ob[bb][:, 0:twL]))(oc, t, b), gml[b], 16)
        S.E("gpsimd", [(gml[0], gml[0].n)], lambda e: e.nop())
        S.E("gpsimd", [(gml[1], gml[1].n)], lambda e: e.nop())

        blk = es.enter_context(nc.Block())
        S.emit(blk)
    return nc


def build_final(Tc):
    nc = bass.Bass("TRN2", target_bir_lowering=False)
    xd = nc.dram_tensor("x", [D, Tc], F32, kind="ExternalInput")
    zd = nc.dram_tensor("z", [D, Tc], BF16, kind="ExternalInput")
    ld = nc.dram_tensor("lo", [D, Tc], BF16, kind="ExternalInput")
    od = nc.dram_tensor("out", [D, Tc], F32, kind="ExternalOutput")
    with ExitStack() as es:
        sb = lambda n, s_, d=F32: es.enter_context(nc.sbuf_tensor(n, s_, d))
        x_b = [sb(f"x_bF{i}", [128, Tc]) for i in range(2)]
        z_b = [sb(f"z_bF{i}", [128, Tc], BF16) for i in range(2)]
        l_b = [sb(f"l_bF{i}", [128, Tc], BF16) for i in range(2)]
        o_b = [sb(f"o_bF{i}", [128, Tc]) for i in range(2)]
        blk = es.enter_context(nc.Block())
        dm = [Sem(nc, es, "dmF0"), Sem(nc, es, "dmF1")]
        sv = Sem(nc, es, "svF")
        gm = [Sem(nc, es, "gmF0"), Sem(nc, es, "gmF1")]

        @blk.sync
        def _(sync):
            for f in range(NF):
                b = f % 2
                if f >= 2:
                    sync.wait_ge(sv.h, 2 * (f - 1))
                sync.dma_start(out=x_b[b][:, :], in_=xd[f * 128:(f + 1) * 128, :]).then_inc(dm[b].h, 16)
                sync.dma_start(out=z_b[b][:, :], in_=zd[f * 128:(f + 1) * 128, :]).then_inc(dm[b].h, 16)
                sync.dma_start(out=l_b[b][:, :], in_=ld[f * 128:(f + 1) * 128, :]).then_inc(dm[b].h, 16)
            sync.wait_ge(dm[0].h, (NF // 2) * 48)
            sync.wait_ge(dm[1].h, (NF // 2) * 48)

        @blk.vector
        def _(vector):
            for f in range(NF):
                b = f % 2
                vector.wait_ge(dm[b].h, (f // 2 + 1) * 48)
                if f >= 2:
                    vector.wait_ge(gm[b].h, (f // 2) * 16)
                sv.inc(vector.tensor_tensor(out=o_b[b][:, :], in0=x_b[b][:, :],
                                            in1=z_b[b][:, :], op=OP.add))
                vector.wait_ge(sv.h, sv.n)
                sv.inc(vector.tensor_tensor(out=o_b[b][:, :], in0=o_b[b][:, :],
                                            in1=l_b[b][:, :], op=OP.add))

        @blk.gpsimd
        def _(gpsimd):
            for f in range(NF):
                b = f % 2
                gpsimd.wait_ge(sv.h, 2 * (f + 1))
                gm[b].inc(gpsimd.dma_start(out=od[f * 128:(f + 1) * 128, :],
                                           in_=o_b[b][:, :]), 16)
            gpsimd.wait_ge(gm[0].h, gm[0].n)
            gpsimd.wait_ge(gm[1].h, gm[1].n)
    return nc


from concourse.aot_env import allow_ici_all_to_all


class Mega:
    def __init__(self):
        self.nc = bass.Bass("TRN2", target_bir_lowering=False)
        self.S = Sched()
        self.es = ExitStack()
        self._sems = {}

    def sem(self, name):
        if name not in self._sems:
            self._sems[name] = Sem(self.nc, self.es, name)
        return self._sems[name]

    def gate(self):
        return [(sm, sm.n) for name, sm in self._sems.items()
                if sm.n > 0 and not name.startswith("cc_")]


def _q_rowloc(L, h):
    if L == 1:
        return 128 * h
    c = (h % 16) // 2
    i = 2 * ((h % 16) % 2) + (h // 16)
    return 512 * c + 128 * i


def emit_front(m, L, x_ap, add_ap_fn, wq, wk, wvt, kvd, with_bias, bias_ts, rope,
               cs_ts, xn_sb, xn_stage, xo, qs, ks, vs, add_waits=(), hook_k=None, hook_q=None):
    nc, S = m.nc, m.S
    Tc = TC
    tw = min(512, Tc)
    nth = Tc // tw
    ntk = Tc // 128
    nkc = kvd // 128
    n_adds = 2 if add_ap_fn else 0
    sfx = f"F{L}"
    dA = [m.sem("ds0"), m.sem("ds1")]
    dB = [m.sem("ds2"), m.sem("ds3")]
    dW = [m.sem(f"ds{i}") for i in range(4, 8)]
    drt = m.sem("d4x")
    sv = m.sem("sv")
    sc = m.sem("sc")
    mm = m.sem("mm")
    gq = m.sem("gq")
    gk = m.sem("gk")
    gv = m.sem("gv")
    gxn = m.sem("gxn")
    gx = [m.sem("gx0"), m.sem("gx1")]
    gp = m.sem("gp")

    with ExitStack() as eF:
        sb = lambda n, s_, d=F32: eF.enter_context(nc.sbuf_tensor(n + sfx, s_, d))
        ones_b = sb("ones_b", [128, 1], BF16)
        ones_r = sb("ones_r", [1, 128], F32)
        onesc = sb("onesc", [1, 128], BF16)
        eps_c = sb("eps_c", [1, 1], F32)
        mrow = sb("mrow", [1, Tc]); arow = sb("arow", [1, Tc]); rrow = sb("rrow", [1, Tc])
        srow = sb("srow", [1, Tc]); qrow = sb("qrow", [1, Tc])
        msq = sb("msq", [1, Tc]); vrow = sb("vrow", [1, Tc]); stdr = sb("stdr", [1, Tc])
        x_b = [sb(f"x_b{i}", [128, Tc]) for i in range(2)]
        a_b = [[sb(f"a_b{j}_{i}", [128, Tc], BF16) for i in range(2)] for j in range(n_adds)]
        xw_b = [sb(f"xw_b{i}", [128, Tc]) for i in range(2)] if n_adds else None
        xc_b = [sb(f"xc_b{i}", [128, Tc], BF16) for i in range(2)]
        sq_b = [sb(f"sq_b{i}", [128, Tc], BF16) for i in range(2)]
        y_b = x_b
        tmpB = [sb(f"tmpB{t}", [128, tw]) for t in range(nth)]
        wf4 = [sb(f"wf4{i}", [128, 2048]) for i in range(3)]
        wb4 = [sb(f"wb4{i}", [128, 2048], BF16) for i in range(3)]
        ob = [sb(f"ob{i}", [128, tw], BF16) for i in range(4 * nth)]
        vo = [sb(f"vo{i}", [128, 512], BF16) for i in range(2)]
        rt = [sb(f"rt{i}", [64, tw]) for i in range(4)]
        pcop = [(sb(f"pcopL{i}", [64, tw]), sb(f"pcopH{i}", [64, tw])) for i in range(2)] if rope else None
        if with_bias:
            bcol = sb("bcol", [128, NF + nkc])
            bvrow = sb("bvrow", [1, kvd], BF16)
            bvrow_f = sb("bvrow_f", [1, kvd])
        if rope:
            cos_sb = sb("cos_sb", [64, Tc])
            sin_sb = sb("sin_sb", [64, Tc])

        m.S.start_phase(m.gate())

        S.E("vector", [], lambda e: e.memset(ones_b[:, :], 1.0))
        S.E("vector", [], lambda e: e.memset(ones_r[:, :], 1.0))
        sv_ones = S.E("vector", [], lambda e: e.memset(onesc[:, :], 1.0), sv)
        sv_eps = S.E("vector", [], lambda e: e.memset(eps_c[:, :], 1e-5), sv)
        if rope:
            S.E("sync", [], lambda e: e.dma_start(out=cos_sb[:, :], in_=cs_ts[0][:, :]), drt, 16)
            S.E("sync", [], lambda e: e.dma_start(out=sin_sb[:, :], in_=cs_ts[1][:, :]), drt, 16)
        sv_bv = None
        if with_bias:
            S.E("sync", [], lambda e: e.dma_start(out=bcol[:, 0:NF], in_=bias_ts[0][:, :]), drt, 16)
            S.E("sync", [], lambda e: e.dma_start(out=bcol[:, NF:NF + nkc], in_=bias_ts[1][:, :]), drt, 16)
            S.E("sync", [], lambda e: e.dma_start(out=bvrow_f[:, :], in_=bias_ts[2][:, :]), drt, 16)
            sv_bv = S.E("vector", [(drt, drt.n)],
                        lambda e: e.tensor_copy(out=bvrow[:, :], in_=bvrow_f[:, :]), sv)
        drt_all = (drt, drt.n)

        with ExitStack() as eAB:
            psA = lambda n, s_: eAB.enter_context(nc.psum_tensor(n + sfx, s_, F32))
            sum_ps = [psA(f"sum_ps{t}", [1, tw]) for t in range(nth)]
            ssq_ps = [psA(f"ssq_ps{t}", [1, tw]) for t in range(nth)]
            mbc = [psA(f"mbc{t}", [128, tw]) for t in range(nth)]
            rbc = [psA(f"rbc{t}", [128, tw]) for t in range(nth)]

            sq_v = []; cast_v = []; mm_v = []
            for f in range(NF):
                b = f % 2
                wv_ = list(add_waits) if f == 0 else []
                if f >= 2:
                    wv_.append((sv, cast_v[f - 2]))
                    wv_.append((sc, sq_v[f - 2]))
                S.E("sync", wv_, (lambda ff, bb: lambda e: e.dma_start(
                    out=x_b[bb][:, :], in_=x_ap[ff * 128:(ff + 1) * 128, :]))(f, b), dA[b], 16)
                for j in range(n_adds):
                    S.E("sync", [], (lambda ff, bb, jj: lambda e: e.dma_start(
                        out=a_b[jj][bb][:, :], in_=add_ap_fn(jj, ff)))(f, b, j), dA[b], 16)
                dcur = (dA[b], dA[b].n)
                if n_adds:
                    wv_ = [dcur]
                    if f >= 2:
                        wv_.append((gx[b], 16 * (f // 2)))
                    v = S.E("vector", wv_, (lambda bb: lambda e: e.tensor_tensor(
                        out=xw_b[bb][:, :], in0=x_b[bb][:, :], in1=a_b[0][bb][:, :], op=OP.add))(b), sv)
                    v = S.E("vector", [(sv, v)], (lambda bb: lambda e: e.tensor_tensor(
                        out=xw_b[bb][:, :], in0=xw_b[bb][:, :], in1=a_b[1][bb][:, :], op=OP.add))(b), sv)
                    S.E("gpsimd", [(sv, v)], (lambda ff, bb: lambda e: e.dma_start(
                        out=xo[ff * 128:(ff + 1) * 128, :], in_=xw_b[bb][:, :]))(f, b), gx[b], 16)
                    src = xw_b[b]
                    src_rdy = [(sv, v)]
                else:
                    src = x_b[b]
                    src_rdy = [dcur]
                sq_v.append(S.E("scalar", src_rdy + ([(mm, mm_v[f - 2])] if f >= 2 else []),
                                (lambda bb, ss: lambda e: e.activation(
                                    out=sq_b[bb][:, :], in_=ss[:, :], func=AF.Square))(b, src), sc))
                cast_v.append(S.E("vector", src_rdy + ([(mm, mm_v[f - 2])] if f >= 2 else []),
                                  (lambda bb, ss: lambda e: e.tensor_copy(
                                      out=xc_b[bb][:, :], in_=ss[:, :]))(b, src), sv))
                wv_ = [(sv, cast_v[f]), (sc, sq_v[f])]
                for t in range(nth):
                    S.E("tensor", wv_ if t == 0 else [],
                        (lambda ff, bb, tt: lambda e: e.matmul(
                            sum_ps[tt][:, :], ones_b[:, :], xc_b[bb][:, tt * tw:(tt + 1) * tw],
                            start=(ff == 0), stop=(ff == NF - 1)))(f, b, t))
                    last = (t == nth - 1)
                    mv = S.E("tensor", [],
                             (lambda ff, bb, tt: lambda e: e.matmul(
                                 ssq_ps[tt][:, :], ones_b[:, :], sq_b[bb][:, tt * tw:(tt + 1) * tw],
                                 start=(ff == 0), stop=(ff == NF - 1)))(f, b, t),
                             mm if last else None)
                mm_v.append(mv)
            v = None
            for t in range(nth):
                v = S.E("vector", [(mm, mm_v[-1])] if t == 0 else [],
                        (lambda tt: lambda e: e.tensor_copy(out=srow[:, tt * tw:(tt + 1) * tw],
                                                            in_=sum_ps[tt][:, :]))(t), sv)
            v_m = S.E("vector", [(sv, v)], lambda e: e.tensor_scalar(
                out=mrow[:, :], in0=srow[:, :], scalar1=1.0 / D, scalar2=None, op0=OP.mult), sv)
            for t in range(nth):
                v = S.E("vector", [],
                        (lambda tt: lambda e: e.tensor_copy(out=qrow[:, tt * tw:(tt + 1) * tw],
                                                            in_=ssq_ps[tt][:, :]))(t), sv)
            v_a = S.E("vector", [(sv, v)], lambda e: e.tensor_scalar(
                out=arow[:, :], in0=qrow[:, :], scalar1=1.0 / D, scalar2=None, op0=OP.mult), sv)
            c_msq = S.E("scalar", [(sv, v_m)], lambda e: e.activation(
                out=msq[:, :], in_=mrow[:, :], func=AF.Square), sc)
            v_vr = S.E("vector", [(sv, v_a), (sc, c_msq)], lambda e: e.tensor_tensor(
                out=vrow[:, :], in0=arow[:, :], in1=msq[:, :], op=OP.subtract), sv)
            c_std = S.E("scalar", [(sv, v_vr), (sv, sv_eps)], lambda e: e.activation(
                out=stdr[:, :], in_=vrow[:, :], func=AF.Sqrt, bias=eps_c[:, :]), sc)
            v_rr = S.E("vector", [(sc, c_std)], lambda e: e.reciprocal(out=rrow[:, :], in_=stdr[:, :]), sv)
            bc_last = None
            for t in range(nth):
                S.E("tensor", [(sv, v_m)] if t == 0 else [],
                    (lambda tt: lambda e: e.matmul(mbc[tt][:, :], ones_r[:, :],
                                                   mrow[:, tt * tw:(tt + 1) * tw], start=True, stop=True))(t))
            for t in range(nth):
                bc_last = S.E("tensor", [(sv, v_rr)] if t == 0 else [],
                              (lambda tt: lambda e: e.matmul(rbc[tt][:, :], ones_r[:, :],
                                                             rrow[:, tt * tw:(tt + 1) * tw],
                                                             start=True, stop=True))(t),
                              mm if t == nth - 1 else None)

            xsrc = xo if n_adds else x_ap
            xn_v = []
            for f in range(NF):
                b = f % 2
                wv_ = []
                if n_adds:
                    wv_.append((gx[b], 16 * (f // 2 + 1)))
                if f < 2:
                    wv_ += [(sv, cast_v[-1]), (sc, sq_v[-1])]
                if f >= 2:
                    wv_.append((sv, xn_v[f - 2]))
                S.E("sync", wv_, (lambda ff, bb: lambda e: e.dma_start(
                    out=y_b[bb][:, :], in_=xsrc[ff * 128:(ff + 1) * 128, :]))(f, b), dB[b], 16)
                dcur = (dB[b], dB[b].n)
                v = None
                for t in range(nth):
                    wv_ = ([dcur, (mm, bc_last)] if t == 0 else []) + [(sv, sv.n)]
                    v0 = S.E("vector", wv_,
                             (lambda bb, tt: lambda e: e.tensor_tensor(
                                 out=tmpB[tt][:, :], in0=y_b[bb][:, tt * tw:(tt + 1) * tw],
                                 in1=mbc[tt][:, :], op=OP.subtract))(b, t), sv)
                    v = S.E("vector", [(sv, v0)],
                            (lambda ff, tt: lambda e: e.tensor_tensor(
                                out=xn_sb[:, ff * Tc + tt * tw: ff * Tc + (tt + 1) * tw],
                                in0=tmpB[tt][:, :], in1=rbc[tt][:, :], op=OP.mult))(f, t), sv)
                xn_v.append(v)
                S.E("gpsimd", [(sv, v)],
                    (lambda ff: lambda e: e.dma_start(
                        out=xn_stage[ff * 128:(ff + 1) * 128, :],
                        in_=xn_sb[:, ff * Tc:(ff + 1) * Tc]))(f), gxn, 16)
            xn_done_v = xn_v[-1]
            xn_done_sv = xn_done_v
        xn_staged = (gxn, gxn.n)

        npg = max(4 * nth, ntk)
        with ExitStack() as eC:
            pg = [eC.enter_context(nc.psum_tensor(f"pg{c}{sfx}", [128, 512], F32))
                  for c in range(npg)]
            groups = [("k", wk, c0) for c0 in range(0, nkc, 2)] + \
                     [("q", wq, c0) for c0 in range(0, NF, 2)]
            NG = len(groups)
            cast_w = []; mmg_v = []; last_ev = None
            prev_ev_bank = [None] * npg
            pc_free = [None, None]
            gpos = 2 * nth
            NB = NF // 8
            for gi, (kind, wt, c0) in enumerate(groups):
                if kind == "q" and gi == NG - (NF // 2) and hook_k is not None:
                    hook_k()
                gsem = gq if kind == "q" else gk
                gq_before = gq.n
                gk_before = gk.n
                half = (gi % 2) * gpos
                wcols = wt.shape[1]
                for f8 in range(NB):
                    n4 = gi * NB + f8
                    bw = n4 % 3
                    wv_ = [] if n4 < 3 else [(sv, cast_w[n4 - 3])]
                    dv = S.E("sync", wv_, (lambda wt_, ff, cc, bb, wc: lambda e: e.dma_start(
                        out=wf4[bb][:, :],
                        in_=bass.AP(wt_, (ff * 8 * 128) * wc + cc * 128,
                                    [[wc, 128], [128 * wc, 8], [1, 256]])
                    ))(wt, f8, c0, bw, wcols), dW[bw], 16)
                    wv_ = [(dW[bw], dv)]
                    if n4 >= 3:
                        wv_.append((mm, mmg_v[8 * (n4 - 3) + 7]))
                    cast_w.append(S.E("vector", wv_, (lambda bb: lambda e: e.tensor_copy(
                        out=wb4[bb][:, :], in_=wf4[bb][:, :]))(bw), sv))
                    for j in range(8):
                        f = f8 * 8 + j
                        wv_ = [(sv, cast_w[n4])] if j == 0 else []
                        if f == 0:
                            wv_.append((sv, xn_done_v))
                        mv = None
                        for c in range(2):
                            for t in range(nth):
                                last = (c == 1 and t == nth - 1)
                                w2 = list(wv_) if (c == 0 and t == 0) else []
                                if f == 0 and prev_ev_bank[half + c * nth + t] is not None:
                                    w2.append(prev_ev_bank[half + c * nth + t])
                                mv = S.E("tensor", w2,
                                         (lambda ff, bb, jj, cc, tt, hh: lambda e: e.matmul(
                                             pg[hh + cc * nth + tt][:, 0:tw],
                                             wb4[bb][:, jj * 256 + cc * 128: jj * 256 + (cc + 1) * 128],
                                             xn_sb[:, ff * Tc + tt * tw: ff * Tc + (tt + 1) * tw],
                                             start=(ff == 0), stop=(ff == NF - 1)))(f, bw, j, c, t, half),
                                         mm if last else None)
                        mmg_v.append(mv)
                prev_full = (gsem, gq_before if kind == "q" else gk_before)
                for c in range(2):
                    h = c0 + c
                    for t in range(nth):
                        k = c * nth + t
                        wv_ = [(mm, mmg_v[-1])]
                        if k == 0:
                            wv_.append(prev_full)
                            wv_.append((gq, gq_before))
                            wv_.append((gk, gk_before))
                        p = pg[half + k]
                        if rope:
                            pcb = (gi * 2 * nth + k) % 2
                            wcp = list(wv_)
                            if pc_free[pcb] is not None:
                                wcp.append((gp, pc_free[pcb]))
                            S.E("scalar", wcp,
                                (lambda pp, bb: lambda e: e.copy(
                                    out=pcop[bb][0][:, :], in_=pp[0:64, 0:tw]))(p, pcb), sc)
                            cpy = S.E("scalar", [],
                                      (lambda pp, bb: lambda e: e.copy(
                                          out=pcop[bb][1][:, :], in_=pp[64:128, 0:tw]))(p, pcb), sc)
                            prev_ev_bank[half + k] = (sc, cpy)
                            wp0 = [(sc, cpy), drt_all, prev_full, (gp, gp.n)]
                            g1 = S.E("gpsimd", wp0, (lambda bb, tt: lambda e: e.tensor_mul(
                                rt[0][:, :], pcop[bb][0][:, :],
                                cos_sb[:, tt * tw:(tt + 1) * tw]))(pcb, t), gp)
                            S.E("gpsimd", [], (lambda bb, tt: lambda e: e.tensor_mul(
                                rt[1][:, :], pcop[bb][1][:, :],
                                sin_sb[:, tt * tw:(tt + 1) * tw]))(pcb, t), gp)
                            S.E("gpsimd", [], (lambda bb, tt: lambda e: e.tensor_mul(
                                rt[2][:, :], pcop[bb][1][:, :],
                                cos_sb[:, tt * tw:(tt + 1) * tw]))(pcb, t), gp)
                            g4 = S.E("gpsimd", [], (lambda bb, tt: lambda e: e.tensor_mul(
                                rt[3][:, :], pcop[bb][0][:, :],
                                sin_sb[:, tt * tw:(tt + 1) * tw]))(pcb, t), gp)
                            pc_free[pcb] = g4
                            S.E("gpsimd", [(gp, g4)], (lambda kk: lambda e: e.tensor_sub(
                                ob[kk][0:64, :], rt[0][:, :], rt[1][:, :]))(k), gp)
                            vpool = S.E("gpsimd", [], (lambda kk: lambda e: e.tensor_add(
                                ob[kk][64:128, :], rt[2][:, :], rt[3][:, :]))(k), gp)
                            last_ev = None
                            S.E("gpsimd", [(gp, vpool)],
                                (lambda dd, rr2, tt, kk: lambda e: e.dma_start(
                                    out=dd[rr2:rr2 + 128, tt * tw:(tt + 1) * tw],
                                    in_=ob[kk][:, :]))(qs if kind == "q" else ks,
                                                       (_q_rowloc(L, c0 + c) if kind == "q" else 128 * (c0 + c)),
                                                       t, k), gsem, 16)
                            continue
                        if with_bias:
                            off = (0 if kind == "q" else NF) + h
                            v = S.E("vector", wv_ + [drt_all],
                                    (lambda pp, kk, oo: lambda e: e.tensor_scalar(
                                        out=ob[kk][:, :], in0=pp[:, 0:tw],
                                        scalar1=bcol[:, oo:oo + 1], scalar2=None, op0=OP.add))(p, k, off), sv)
                        else:
                            v = S.E("vector", wv_, (lambda pp, kk: lambda e: e.tensor_copy(
                                out=ob[kk][:, :], in_=pp[:, 0:tw]))(p, k), sv)
                        last_ev = v
                        prev_ev_bank[half + k] = (sv, v)
                        row = _q_rowloc(L, h) if kind == "q" else 128 * h
                        dst = qs if kind == "q" else ks
                        S.E("gpsimd", [(sv, v)],
                            (lambda dd, rr, tt, kk: lambda e: e.dma_start(
                                out=dd[rr:rr + 128, tt * tw:(tt + 1) * tw],
                                in_=ob[kk][:, :]))(dst, row, t, k), gsem, 16)
            if hook_q is not None:
                hook_q()
            qk_staged = [(gq, gq.n), (gk, gk.n)]

            vcols = kvd // NCORE
            owners = 512 // vcols
            v_last_ev = None
            for vg in range(kvd // 512):
                gv_before = gv.n
                mv = None
                wcols = wvt.shape[1]
                for f4 in range(NF // 4):
                    n4 = len(cast_w)
                    bw = n4 % 3
                    wv_ = [(sv, cast_w[n4 - 3])] if n4 >= 3 else []
                    dv = S.E("sync", wv_, (lambda ff, cc, bb, wc: lambda e: e.dma_start(
                        out=wf4[bb][:, :],
                        in_=bass.AP(wvt, (ff * 4 * 128) * wc + cc * 512,
                                    [[wc, 128], [128 * wc, 4], [1, 512]])
                    ))(f4, vg, bw, wcols), dW[bw], 16)
                    wv_ = [(dW[bw], dv)]
                    if len(mmg_v) >= 9:
                        wv_.append((mm, mmg_v[-9]))
                    cast_w.append(S.E("vector", wv_, (lambda bb: lambda e: e.tensor_copy(
                        out=wb4[bb][:, :], in_=wf4[bb][:, :]))(bw), sv))
                    cw = cast_w[-1]
                    for j in range(4):
                        f = f4 * 4 + j
                        wv_ = [(sv, cw)] if j == 0 else []
                        for tk in range(ntk):
                            last_tk = (tk == ntk - 1)
                            w2 = list(wv_) if tk == 0 else []
                            if f == 0:
                                if v_last_ev is not None:
                                    w2.append((sv, v_last_ev))
                                elif prev_ev_bank[tk] is not None:
                                    w2.append(prev_ev_bank[tk])
                            mv = S.E("tensor", w2,
                                     (lambda ff, bb, jj, kk: lambda e: e.matmul(
                                         pg[kk][:, :], xn_sb[:, ff * Tc + kk * 128: ff * Tc + (kk + 1) * 128],
                                         wb4[bb][:, jj * 512:(jj + 1) * 512], start=(ff == 0),
                                         stop=(ff == NF - 1 and not with_bias)))(f, bw, j, tk),
                                     mm if last_tk else None)
                        mmg_v.append(mv)
                if with_bias:
                    for tk in range(ntk):
                        last = (tk == ntk - 1)
                        mv = S.E("tensor", [(sv, sv_bv), (sv, sv_ones)] if tk == 0 else [],
                                 (lambda kk, cc: lambda e: e.matmul(
                                     pg[kk][:, :], onesc[:, :],
                                     bvrow[:, cc * 512:(cc + 1) * 512], start=False, stop=True))(tk, vg),
                                 mm if last else None)
                    mmg_v[-1] = mv
                m_end = mmg_v[-1]
                for tk in range(ntk):
                    b2 = tk % 2
                    wv_ = [(mm, m_end)]
                    wv_.append((gv, gv_before + 16 * owners * tk))
                    v = S.E("vector", wv_, (lambda kk, bb: lambda e: e.tensor_copy(
                        out=vo[bb][:, :], in_=pg[kk][:, :]))(tk, b2), sv)
                    v_last_ev = v
                    for oo in range(owners):
                        owner = (vg * 512) // vcols + oo
                        S.E("gpsimd", [(sv, v)],
                            (lambda ow, kk, bb, o2: lambda e: e.dma_start(
                                out=vs[ow * Tc + kk * 128: ow * Tc + (kk + 1) * 128, :],
                                in_=vo[bb][:, o2 * vcols:(o2 + 1) * vcols]))(owner, tk, b2, oo),
                            gv, 16)
            v_staged = (gv, gv.n)
    return dict(xn_staged=xn_staged, qk_staged=qk_staged, v_staged=v_staged,
                xn_done=(sv, xn_done_sv), xo_done=[(gx[0], gx[0].n), (gx[1], gx[1].n)],
                sems=dict(sv=sv, sc=sc, mm=mm))


def emit_coll(m, name, kind, ins, outs, waits):
    cc = m.sem("cc_" + name)

    def fn(e):
        with allow_ici_all_to_all():
            return e.collective_compute(
                kind, mybir.AluOpType.bypass,
                replica_groups=[list(range(NCORE))],
                ins=[t.ap().opt() for t in ins],
                outs=[t.ap().opt() for t in outs])
    v = m.S.E("gpsimd", waits, fn, cc, 1)
    return (cc, v)


def emit_attn(m, L, T_, n_kv, gsz, scale, ka, va, qa, dcol, qkio, ctx_dst, cc_waits):
    nc, S = m.nc, m.S
    Tc = TC
    NQT = T_ // 512
    sfx = f"A{L}"
    dk = m.sem("ds0")
    dq = m.sem("ds1")
    mm = m.sem("mm")
    se = m.sem("sc")
    svv = m.sem("sv")
    gma = [[m.sem(f"gma{s_}{p_}") for p_ in range(2)] for s_ in range(2)]
    vcols = n_kv * 128
    CEXP = 40.0
    with ExitStack() as es:
        sb = lambda n, s_, d=F32: es.enter_context(nc.sbuf_tensor(n + sfx, s_, d))
        ps_ = lambda n, s_: es.enter_context(nc.psum_tensor(n + sfx, s_, F32))
        k_sb = sb("k_sb", [128, T_], BF16)
        v_sb = sb("v_sb", [128, T_], BF16)
        q_sb = [sb(f"q_sb{s_}", [128, T_], BF16) for s_ in range(2)]
        p_b = [[sb(f"p_{s_}{i}", [128, 512], BF16) for i in range(2)] for s_ in range(2)]
        pm_b = [[sb(f"pm_{s_}{i}", [128, 512], BF16) for i in range(2)] for s_ in range(2)]
        cbuf = [[sb(f"cbuf_{s_}{i}", [128, 512], BF16) for i in range(2)] for s_ in range(2)]
        bc_sb = [sb(f"bc_sb{s_}", [128, 512]) for s_ in range(2)]
        bias_sb = sb("bias_sb", [128, 1])
        srow = [sb(f"srow{s_}", [1, 512]) for s_ in range(2)]
        rr = [sb(f"rr{s_}", [1, 512]) for s_ in range(2)]
        lnr = [sb(f"lnr{s_}", [1, 512]) for s_ in range(2)]
        ones_b = sb("ones_b", [128, 1], BF16)
        ones_r = sb("ones_r", [1, 128], F32)
        s_ps = [[ps_(f"s_ps{s_}_{i}", [128, 512]) for i in range(2)] for s_ in range(2)]
        ctx_ps = [ps_(f"ctx_ps{s_}", [128, 512]) for s_ in range(2)]
        sum_ps = [ps_(f"sum_ps{s_}", [128, 512]) for s_ in range(2)]

        S.start_phase(m.gate())
        S.E("vector", [], lambda e: e.memset(ones_b[:, :], 1.0))
        S.E("vector", [], lambda e: e.memset(bias_sb[:, :], -CEXP))
        v_ones = S.E("vector", [], lambda e: e.memset(ones_r[:, :], 1.0), svv)

        prev_exp_p0 = [None, None]
        prev_exp_p1 = [None, None]
        prev_bc1 = [None, None]
        prev_norm_v = [None, None]
        prev_sum_v = [None, None]
        dma_v = {}
        first = True
        for kv in range(n_kv):
            wk_ = list(cc_waits) if first else [(mm, mm.n), (se, se.n), (svv, svv.n)]
            S.E("sync", wk_, (lambda h: lambda e: e.dma_start(
                out=k_sb[:, :],
                in_=bass.AP(ka, h * 128 * Tc,
                            [[Tc, 128], [n_kv * 128 * Tc, NCORE], [1, Tc]])))(kv), dk, 16)
            S.E("sync", [], (lambda h: lambda e: e.dma_start(
                out=v_sb[:, :],
                in_=bass.AP(va, h * 128,
                            [[vcols, 128], [128 * vcols, T_ // 128], [1, 128]])))(kv), dk, 16)
            dk_done = (dk, dk.n)
            for p2 in range(gsz // 2):
                wq_ = [] if first else [(mm, mm.n), (se, se.n)]
                first = False
                for s_ in range(2):
                    hloc = kv * gsz + 2 * p2 + s_
                    S.E("sync", wq_ if s_ == 0 else [],
                        (lambda hh, ss: lambda e: e.dma_start(
                            out=q_sb[ss][:, :],
                            in_=bass.AP(qa, hh * 128 * Tc,
                                        [[Tc, 128], [512 * Tc, NCORE], [1, Tc]])))(hloc, s_),
                        dq, 16)
                dq_done = (dq, dq.n)
                for qt in range(NQT):
                    nkt = 4 * qt + 4
                    exp_v = {}
                    pv_v = {}
                    psrc = {}
                    pready = {}

                    def emit_pv(i):
                        r = None
                        for s_ in range(2):
                            pbuf = psrc[(s_, i)]
                            w = [pready[(s_, i)]]
                            if i == 0:
                                w += [(svv, prev_norm_v[s_]), (svv, prev_sum_v[s_]), dk_done]
                            S.E("tensor", w,
                                (lambda ss, ii, nn, pb: lambda e: e.matmul(
                                    ctx_ps[ss][:, :], v_sb[:, ii * 128:(ii + 1) * 128],
                                    pb[:, :], start=(ii == 0), stop=(ii == nn - 1)))(s_, i, nkt, pbuf))
                            r = S.E("tensor",
                                    [(se, prev_bc1[s_])] if i == 0 else [],
                                    (lambda ss, ii, nn, pb: lambda e: e.matmul(
                                        sum_ps[ss][0:1, :], ones_b[:, :],
                                        pb[:, :], start=(ii == 0), stop=(ii == nn - 1)))(s_, i, nkt, pbuf), mm)
                        pv_v[i] = r

                    for i in range(nkt):
                        b = i % 2
                        for s_ in range(2):
                            w = []
                            if i >= 2:
                                w.append((se, exp_v[(s_, i - 2)]))
                            elif b == 0:
                                w.append((se, prev_exp_p0[s_]))
                            else:
                                w.append((se, prev_exp_p1[s_]))
                            if i == 0:
                                w += [dq_done, dk_done]
                            sv_ = S.E("tensor", w,
                                      (lambda ss, ii, qq: lambda e: e.matmul(
                                          s_ps[ss][ii % 2][:, :], k_sb[:, ii * 128:(ii + 1) * 128],
                                          q_sb[ss][:, qq * 512:(qq + 1) * 512],
                                          start=True, stop=True))(s_, i, qt), mm)
                            exp_w = [(mm, sv_), (svv, v_ones)]
                            if i >= 2:
                                exp_w.append((mm, pv_v[i - 2]))
                            else:
                                exp_w.append((svv, prev_norm_v[s_]))
                            exp_v[(s_, i)] = S.E("scalar", exp_w,
                                                 (lambda ss, ii: lambda e: e.activation(
                                                     out=p_b[ss][ii % 2][:, :], in_=s_ps[ss][ii % 2][:, :],
                                                     func=AF.Exp, scale=scale, bias=bias_sb[:, :]))(s_, i), se)
                        diag = i >= 4 * qt
                        for s_ in range(2):
                            if diag:
                                o = i - 4 * qt
                                w = [(se, exp_v[(s_, i)])]
                                if i >= 2:
                                    w.append((mm, pv_v[i - 2]))
                                else:
                                    w.append((svv, prev_norm_v[s_]))
                                mv = S.E("vector", w,
                                         (lambda ss, ii, oo: lambda e: e.scalar_tensor_tensor(
                                             out=pm_b[ss][ii % 2][:, :], in0=qkio[:, :],
                                             scalar=dcol[:, oo:oo + 1], in1=p_b[ss][ii % 2][:, :],
                                             op0=OP.is_ge, op1=OP.mult))(s_, i, o), svv)
                                psrc[(s_, i)] = pm_b[s_][i % 2]
                                pready[(s_, i)] = (svv, mv)
                            else:
                                psrc[(s_, i)] = p_b[s_][i % 2]
                                pready[(s_, i)] = (se, exp_v[(s_, i)])
                        if i >= 1:
                            emit_pv(i - 1)
                    emit_pv(nkt - 1)
                    par = qt % 2
                    for s_ in range(2):
                        ihead = kv * gsz + 2 * p2 + s_
                        wv1 = [(mm, pv_v[nkt - 1])]
                        v1 = S.E("vector", wv1,
                                 (lambda ss: lambda e: e.tensor_copy(out=srow[ss][:, :],
                                                                     in_=sum_ps[ss][0:1, :]))(s_), svv)
                        prev_sum_v[s_] = v1
                        c_ln = S.E("scalar", [(svv, v1)],
                                   (lambda ss: lambda e: e.activation(
                                       out=lnr[ss][:, :], in_=srow[ss][:, :], func=AF.Ln))(s_), se)
                        v2 = S.E("scalar", [(se, c_ln)],
                                 (lambda ss: lambda e: e.activation(
                                     out=rr[ss][:, :], in_=lnr[ss][:, :], func=AF.Exp,
                                     scale=-1.0))(s_), se)
                        m1 = S.E("tensor", [(se, v2)],
                                 (lambda ss: lambda e: e.matmul(sum_ps[ss][:, :], ones_r[:, :],
                                                                rr[ss][:, :], start=True, stop=True))(s_), mm)
                        c1 = S.E("scalar", [(mm, m1)],
                                 (lambda ss: lambda e: e.copy(out=bc_sb[ss][:, :],
                                                              in_=sum_ps[ss][:, :]))(s_), se)
                        prev_bc1[s_] = c1
                        w = [(se, c1)]
                        if (s_, par) in dma_v:
                            w.append((gma[s_][par], dma_v[(s_, par)]))
                        v3 = S.E("vector", w,
                                 (lambda ss, pp: lambda e: e.tensor_tensor(
                                     out=cbuf[ss][pp][:, :], in0=ctx_ps[ss][:, :],
                                     in1=bc_sb[ss][:, :], op=OP.mult))(s_, par), svv)
                        prev_norm_v[s_] = v3
                        dma_v[(s_, par)] = S.E("gpsimd", [(svv, v3)],
                                               (lambda ih, pp, ss, qq: lambda e: e.dma_start(
                                                   out=ctx_dst(ih, qq),
                                                   in_=cbuf[ss][pp][:, :]))(ihead, par, s_, qt),
                                               gma[s_][par], 16)
                        prev_exp_p0[s_] = exp_v[(s_, nkt - 2)]
                        prev_exp_p1[s_] = exp_v[(s_, nkt - 1)]
    staged = [(gma[s_][p_], gma[s_][p_].n) for s_ in range(2) for p_ in range(2)]
    return dict(staged=staged)


def emit_ffn0(m, qw, scz, zrz, fb, xg, ys, cc_xg):
    nc, S = m.nc, m.S
    Tc = TC
    nthc = Tc // 512
    NT = T // 512
    sfx = "N0"
    dqw = [m.sem("ds0"), m.sem("ds1")]
    drow = m.sem("d4x")
    dx = [m.sem(f"ds{i}") for i in range(2, 6)]
    sv = m.sem("sv")
    mm = m.sem("mm")
    gy = m.sem("gy")
    with ExitStack() as es:
        sb = lambda n, s_, d=F32: es.enter_context(nc.sbuf_tensor(n + sfx, s_, d))
        ones_r = sb("ones_r", [1, 128], F32)
        wt_all = sb("wt_all", [128, NF * 512], BF16)
        scz_sb = sb("scz_sb", [1, NF * 512])
        zrz_sb = sb("zrz_sb", [1, NF * 512])
        bcol = sb("bcol", [128, 4])
        qwb = [sb(f"qwb{i}", [128, 512], I32) for i in range(2)]
        qf = [sb(f"qf{i}", [128, 512]) for i in range(2)]
        t2 = [sb(f"t2_{i}", [128, 512]) for i in range(2)]
        xb = [sb(f"xbf{i}", [128, 2048], BF16) for i in range(4)]
        ob = [sb(f"obf{i}", [128, 512], BF16) for i in range(8)]
        S.start_phase(m.gate())
        pgf = [es.enter_context(nc.psum_tensor(f"pgf{c}{sfx}", [128, 512], F32)) for c in range(8)]
        with ExitStack() as e1:
            psc = [pgf[4], pgf[5]]
            pzr = [pgf[6], pgf[7]]
            S.E("sync", [], lambda e: e.dma_start(out=scz_sb[:, :], in_=scz[:, :]), drow, 16)
            S.E("sync", [], lambda e: e.dma_start(out=zrz_sb[:, :], in_=zrz[:, :]), drow, 16)
            S.E("sync", [], lambda e: e.dma_start(out=bcol[:, :], in_=fb[:, :]), drow, 16)
            v_ones = S.E("vector", [], lambda e: e.memset(ones_r[:, :], 1.0), sv)
            drow_all = (drow, drow.n)
            dq_v = []; mm_v = []; w_v = []
            for g in range(NF):
                b = g % 2
                wv_ = [(sv, w_v[g - 2])] if g >= 2 else []
                dv = S.E("sync", wv_, (lambda gg, bb: lambda e: e.dma_start(
                    out=qwb[bb][:, :], in_=qw[gg * 128:(gg + 1) * 128, :]))(g, b), dqw[b], 16)
                mv = S.E("tensor", [drow_all, (sv, v_ones)] +
                         ([(sv, w_v[g - 2])] if g >= 2 else []),
                         (lambda gg, bb: lambda e: e.matmul(
                             psc[bb][:, :], ones_r[:, :], scz_sb[:, gg * 512:(gg + 1) * 512],
                             start=True, stop=True))(g, b))
                mv = S.E("tensor", [],
                         (lambda gg, bb: lambda e: e.matmul(
                             pzr[bb][:, :], ones_r[:, :], zrz_sb[:, gg * 512:(gg + 1) * 512],
                             start=True, stop=True))(g, b), mm)
                mm_v.append(mv)
                v = S.E("vector", [(dqw[b], dv)], (lambda bb: lambda e: e.tensor_copy(
                    out=qf[bb][:, :], in_=qwb[bb][:, :]))(b), sv)
                v = S.E("vector", [(sv, v), (mm, mv)], (lambda bb: lambda e: e.tensor_tensor(
                    out=t2[bb][:, :], in0=qf[bb][:, :], in1=psc[bb][:, :], op=OP.mult))(b), sv)
                v = S.E("vector", [(sv, v)], (lambda gg, bb: lambda e: e.tensor_tensor(
                    out=wt_all[:, gg * 512:(gg + 1) * 512], in0=t2[bb][:, :], in1=pzr[bb][:, :],
                    op=OP.add))(g, b), sv)
                w_v.append(v)
            dequant_done = w_v[-1]
        with ExitStack() as e2:
            ev_v = []; stg = []; mmx_v = []
            for tt in range(NT):
                j = tt // nthc
                lcol = (tt % nthc) * 512
                pb0 = (tt % 2) * 4
                mv = None
                for g4 in range(NF // 4):
                    n4 = tt * (NF // 4) + g4
                    bx = n4 % 4
                    wv_ = [(mm, mmx_v[4 * (n4 - 4) + 3])] if n4 >= 4 else [cc_xg]
                    dv = S.E("sync", wv_, (lambda jj, gg, ll, bb: lambda e: e.dma_start(
                        out=xb[bb][:, :],
                        in_=bass.AP(xg, (jj * D + gg * 4 * 128) * TC + ll,
                                    [[TC, 128], [128 * TC, 4], [1, 512]])))(j, g4, lcol, bx),
                        dx[bx], 16)
                    for jg in range(4):
                        g = g4 * 4 + jg
                        wv_ = [(dx[bx], dv)] if jg == 0 else []
                        if tt == 0 and jg == 0:
                            wv_.append((sv, w_v[min(g4 * 4 + 3, NF - 1)]))
                        if g == 0 and tt > 1:
                            wv_.append((sv, ev_v[4 * (tt - 2) + 3]))
                        for c in range(4):
                            last = (c == 3)
                            mv = S.E("tensor", wv_ if c == 0 else [],
                                     (lambda gg, bb, jj2, cc, pp: lambda e: e.matmul(
                                         pgf[pp + cc][:, :], wt_all[:, gg * 512 + cc * 128: gg * 512 + (cc + 1) * 128],
                                         xb[bb][:, jj2 * 512:(jj2 + 1) * 512],
                                         start=(gg == 0), stop=(gg == NF - 1)))(g, bx, jg, c, pb0),
                                     mm if last else None)
                        mmx_v.append(mv)
                for c in range(4):
                    k = (tt % 2) * 4 + c
                    wv_ = [(mm, mmx_v[-1]), drow_all]
                    if c == 0 and tt >= 2:
                        wv_.append((gy, gy.n))
                    v = S.E("vector", wv_, (lambda kk, cc, pp: lambda e: e.tensor_scalar(
                        out=ob[kk][:, :], in0=pgf[pp + cc][:, :], scalar1=bcol[:, cc:cc + 1],
                        scalar2=None, op0=OP.add))(k, c, pb0), sv)
                    ev_v.append(v)
                    stg.append(S.E("gpsimd", [(sv, v)],
                                   (lambda jj, cc, ll, kk: lambda e: e.dma_start(
                                       out=ys[jj * 512 + cc * 128: jj * 512 + (cc + 1) * 128, ll:ll + 512],
                                       in_=ob[kk][:, :]))(j, c, lcol, k), gy, 16))
    return dict(staged=(gy, gy.n))


def emit_wo_cast(m, wo, wb_, wo_all, wb_all, wf4w):
    nc, S = m.nc, m.S
    dwc = m.sem("cc_wocd")
    pwc = m.sem("cc_wocc")
    cast_v = []
    for j in range(2 * (NF // 4)):
        b = j % 2
        src = wb_ if j < NF // 4 else wo
        dst = wb_all if j < NF // 4 else wo_all
        g4 = j % (NF // 4)
        wv_ = [(pwc, cast_v[j - 2])] if j >= 2 else []
        dv = S.E("gpsimd", wv_, (lambda sr, gg, bb: lambda e: e.dma_start(
            out=wf4w[bb][:, :],
            in_=bass.AP(sr, gg * 4 * 128 * 512, [[512, 128], [128 * 512, 4], [1, 512]])))(src, g4, b),
            dwc, 16)
        cast_v.append(S.E("gpsimd", [(dwc, dv)],
                          (lambda ds, gg, bb: lambda e: e.tensor_copy(
                              out=ds[:, gg * 2048:(gg + 1) * 2048], in_=wf4w[bb][:, :]))(dst, g4, b),
                          pwc))
    return (pwc, pwc.n)


def emit_wo_base(m, xg, cg, zs, wo_all, wb_all, cast_done, cc_xg, cc_cg):
    nc, S = m.nc, m.S
    Tc = TC
    nthc = Tc // 512
    NT = T // 512
    sfx = "W1"
    dc_ = [m.sem(f"ds{i}") for i in range(4, 8)]
    dxx = [m.sem(f"ds{i}") for i in range(0, 4)]
    sv = m.sem("sv")
    mm = m.sem("mm")
    gz = m.sem("gy")
    with ExitStack() as es:
        sb = lambda n, s_, d=F32: es.enter_context(nc.sbuf_tensor(n + sfx, s_, d))
        cb = [sb(f"cbw{i}", [128, 2048], BF16) for i in range(4)]
        xb = [sb(f"xbw{i}", [128, 2048], BF16) for i in range(4)]
        ob = [sb(f"obw{i}", [128, 512], BF16) for i in range(8)]
        S.start_phase(m.gate())
        with ExitStack() as e2:
            pgw = [e2.enter_context(nc.psum_tensor(f"pgw{c}{sfx}", [128, 512], F32)) for c in range(8)]
            ev_v = []
            mmx_v = []
            mmc_v = []
            for tt in range(NT):
                j = tt // nthc
                lcol = (tt % nthc) * 512
                pb0 = (tt % 2) * 4
                for g4 in range(NF // 4):
                    n4 = tt * (NF // 4) + g4
                    bx = n4 % 4
                    wv_ = [(mm, mmx_v[4 * (n4 - 4) + 3])] if n4 >= 4 else [cc_xg, cast_done]
                    dv = S.E("sync", wv_, (lambda jj2, gg, ll, bb: lambda e: e.dma_start(
                        out=xb[bb][:, :],
                        in_=bass.AP(xg, (jj2 * D + gg * 4 * 128) * TC + ll,
                                    [[TC, 128], [128 * TC, 4], [1, 512]])))(j, g4, lcol, bx),
                        dxx[bx], 16)
                    for jg in range(4):
                        g = g4 * 4 + jg
                        wv_ = [(dxx[bx], dv)] if jg == 0 else []
                        if g == 0 and tt > 1:
                            wv_.append((sv, ev_v[4 * (tt - 2) + 3]))
                        for c in range(4):
                            mv = S.E("tensor", wv_ if c == 0 else [],
                                     (lambda gg, bb, jj3, cc, pp: lambda e: e.matmul(
                                         pgw[pp + cc][:, :], wb_all[:, gg * 512 + cc * 128: gg * 512 + (cc + 1) * 128],
                                         xb[bb][:, jj3 * 512:(jj3 + 1) * 512],
                                         start=(gg == 0), stop=False))(g, bx, jg, c, pb0),
                                     mm if c == 3 else None)
                        mmx_v.append(mv)
                for g4 in range(NF // 4):
                    n4 = tt * (NF // 4) + g4
                    bx = n4 % 4
                    wv_ = [(mm, mmc_v[4 * (n4 - 4) + 3])] if n4 >= 4 else [cc_cg]
                    dv = S.E("sync", wv_, (lambda gg, bb, tt2: lambda e: e.dma_start(
                        out=cb[bb][:, :],
                        in_=bass.AP(cg, (gg * 4 * 128) * T + tt2 * 512,
                                    [[T, 128], [128 * T, 4], [1, 512]])))(g4, bx, tt),
                        dc_[bx], 16)
                    for jg in range(4):
                        g = g4 * 4 + jg
                        wv_ = [(dc_[bx], dv)] if jg == 0 else []
                        for c in range(4):
                            mv = S.E("tensor", wv_ if c == 0 else [],
                                     (lambda gg, bb, jj3, cc, pp: lambda e: e.matmul(
                                         pgw[pp + cc][:, :], wo_all[:, gg * 512 + cc * 128: gg * 512 + (cc + 1) * 128],
                                         cb[bb][:, jj3 * 512:(jj3 + 1) * 512],
                                         start=False, stop=(gg == NF - 1)))(g, bx, jg, c, pb0),
                                     mm if c == 3 else None)
                        mmc_v.append(mv)
                for c in range(4):
                    k = (tt % 2) * 4 + c
                    wv_ = [(mm, mmc_v[-1])]
                    if c == 0 and tt >= 2:
                        wv_.append((gz, gz.n))
                    v = S.E("vector", wv_, (lambda kk, cc, pp: lambda e: e.tensor_copy(
                        out=ob[kk][:, :], in_=pgw[pp + cc][:, :]))(k, c, pb0), sv)
                    ev_v.append(v)
                    S.E("gpsimd", [(sv, v)],
                        (lambda jj2, cc, ll, kk: lambda e: e.dma_start(
                            out=zs[jj2 * 512 + cc * 128: jj2 * 512 + (cc + 1) * 128, ll:ll + 512],
                            in_=ob[kk][:, :]))(j, c, lcol, k), gz, 16)
    return dict(staged=(gz, gz.n))


def emit_final(m, xo1, za, lod, outd, waits):
    nc, S = m.nc, m.S
    Tc = TC
    sfx = "FN"
    dm_ = [m.sem("ds0"), m.sem("ds1")]
    sv = m.sem("sv")
    go = [m.sem("go0"), m.sem("go1")]
    with ExitStack() as es:
        sb = lambda n, s_, d=F32: es.enter_context(nc.sbuf_tensor(n + sfx, s_, d))
        x_b = [sb(f"x_b{i}", [128, Tc]) for i in range(2)]
        z_b = [sb(f"z_b{i}", [128, Tc], BF16) for i in range(2)]
        l_b = [sb(f"l_b{i}", [128, Tc], BF16) for i in range(2)]
        o_b = [sb(f"o_b{i}", [128, Tc]) for i in range(2)]
        S.start_phase(m.gate())
        sv_v = []
        for f in range(NF):
            b = f % 2
            wv_ = list(waits) if f == 0 else []
            if f >= 2:
                wv_.append((sv, sv_v[f - 2]))
            S.E("sync", wv_, (lambda ff, bb: lambda e: e.dma_start(
                out=x_b[bb][:, :], in_=xo1[ff * 128:(ff + 1) * 128, :]))(f, b), dm_[b], 16)
            S.E("sync", [], (lambda ff, bb: lambda e: e.dma_start(
                out=z_b[bb][:, :], in_=za[ff * 128:(ff + 1) * 128, :]))(f, b), dm_[b], 16)
            S.E("sync", [], (lambda ff, bb: lambda e: e.dma_start(
                out=l_b[bb][:, :], in_=lod[ff * 128:(ff + 1) * 128, :]))(f, b), dm_[b], 16)
            dcur = (dm_[b], dm_[b].n)
            wv_ = [dcur]
            if f >= 2:
                wv_.append((go[b], 16 * (f // 2)))
            v = S.E("vector", wv_, (lambda bb: lambda e: e.tensor_tensor(
                out=o_b[bb][:, :], in0=x_b[bb][:, :], in1=z_b[bb][:, :], op=OP.add))(b), sv)
            v = S.E("vector", [(sv, v)], (lambda bb: lambda e: e.tensor_tensor(
                out=o_b[bb][:, :], in0=o_b[bb][:, :], in1=l_b[bb][:, :], op=OP.add))(b), sv)
            sv_v.append(v)
            S.E("gpsimd", [(sv, v)], (lambda ff, bb: lambda e: e.dma_start(
                out=outd[ff * 128:(ff + 1) * 128, :], in_=o_b[bb][:, :]))(f, b), go[b], 16)
        S.E("gpsimd", [(go[0], go[0].n)], lambda e: e.nop())
        S.E("gpsimd", [(go[1], go[1].n)], lambda e: e.nop())


def emit_lora(m, xn_sb, xn_ready, gwd, lad, lb1d, lb2d, sel1d, sel2d, identd, lod):
    nc, S = m.nc, m.S
    Tc = TC
    NTC = Tc // 128
    twL = min(512, Tc)
    nth = Tc // twL
    sfx = "L1"
    dl = m.sem("d4x")
    dwl = [m.sem("ds0"), m.sem("ds1")]
    svl = m.sem("sv")
    scl = m.sem("sc")
    mml = m.sem("mm")
    gml = [m.sem("glo0"), m.sem("glo1")]
    with ExitStack() as es:
        sb = lambda n, s_, d=F32: es.enter_context(nc.sbuf_tensor(n + sfx, s_, d))
        ps_ = lambda n, s_, d=F32: es.enter_context(nc.psum_tensor(n + sfx, s_, d))
        gwb = sb("gwb", [128, NF * 28], BF16)
        lab = sb("lab", [128, NF * 224], BF16)
        lb1b = sb("lb1b", [128, D], BF16)
        lb2b = sb("lb2b", [96, D], BF16)
        sel1 = sb("sel1_sb", [28, 128], BF16)
        sel2 = sb("sel2_sb", [28, 96], BF16)
        ident = sb("ident_sb", [128, 128], BF16)
        gt_sb = sb("gt_sb", [28, Tc], BF16)
        hs = [sb("hs1", [128, Tc], BF16), sb("hs2", [96, Tc], BF16)]
        ge_sb = [sb("ge_sb1", [128, Tc]), sb("ge_sb2", [96, Tc])]
        wf = [sb(f"wfl{i}", [128, 512]) for i in range(2)]
        lg = sb("lg", [128, 28])
        m1 = sb("m1c", [128, 1]); m2 = sb("m2c", [128, 1])
        eq1 = sb("eq1", [128, 28]); eq2 = sb("eq2", [128, 28])
        l2m = sb("l2m", [128, 28]); dd = sb("ddc", [128, 1])
        p2c = sb("p2c", [128, 1]); p1c = sb("p1c", [128, 1])
        g1t = sb("g1t", [128, 28]); gat = sb("gat", [128, 28])
        gab = sb("gab", [128, 28], BF16)
        ob = [sb(f"obl{i}", [128, twL], BF16) for i in range(4)]
        S.start_phase(m.gate())
        S.E("sync", [], lambda e: e.dma_start(out=sel1[:, :], in_=sel1d[:, :]), dl, 16)
        S.E("sync", [], lambda e: e.dma_start(out=sel2[:, :], in_=sel2d[:, :]), dl, 16)
        S.E("sync", [], lambda e: e.dma_start(out=ident[:, :], in_=identd[:, :]), dl, 16)
        dl_done = (dl, dl.n)
        jobs = []
        for g in range(NF):
            jobs.append((gwd, g * 128, 0, gwb, g * 28, 28, 128))
        for g in range(NF):
            jobs.append((lad, g * 128, 0, lab, g * 224, 224, 128))
        for t in range(D // 512):
            jobs.append((lb1d, 0, t * 512, lb1b, t * 512, 512, 128))
        for t in range(D // 512):
            jobs.append((lb2d, 0, t * 512, lb2b, t * 512, 512, 96))
        castv = []
        for j, (src, soff, coff, dst, doff, w, rows) in enumerate(jobs):
            b = j % 2
            wv = [] if j < 2 else [(svl, castv[j - 2])]
            dv = S.E("sync", wv, (lambda sr, so, co, ww, rr, bb: lambda e: e.dma_start(
                out=wf[bb][0:rr, 0:ww], in_=sr[so:so + rr, co:co + ww]))(src, soff, coff, w, rows, b),
                dwl[b], 16)
            castv.append(S.E("vector", [(dwl[b], dv)],
                             (lambda ds, do, ww, rr, bb: lambda e: e.tensor_copy(
                                 out=ds[0:rr, do:do + ww], in_=wf[bb][0:rr, 0:ww]))(dst, doff, w, rows, b),
                             svl))
        cast_gw = castv[NF - 1]
        cast_la = castv[2 * NF - 1]
        cast_lb = castv[-1]

        es_gate = ExitStack()
        pl = es_gate.enter_context(nc.psum_tensor("plL" + sfx, [128, 28], F32))
        pt = es_gate.enter_context(nc.psum_tensor("ptL" + sfx, [28, 128], BF16))
        prev_lg = None
        prev_tp = None
        prev_gtc = None
        for tch in range(NTC):
            wv = [xn_ready, (svl, cast_gw)]
            if prev_lg is not None:
                wv.append((scl, prev_lg))
            i = None
            for g in range(NF):
                i = S.E("tensor", wv if g == 0 else [],
                        (lambda gg, tt: lambda e: e.matmul(
                            pl[:, :], xn_sb[:, gg * Tc + tt * 128: gg * Tc + (tt + 1) * 128],
                            gwb[:, gg * 28:(gg + 1) * 28],
                            start=(gg == 0), stop=(gg == NF - 1)))(g, tch),
                        mml if g == NF - 1 else None)
            m_pl = i
            prev_lg = S.E("scalar", [(mml, m_pl)],
                          lambda e: e.copy(out=lg[:, :], in_=pl[:, :]), scl)
            v = S.E("vector", [(scl, prev_lg), (svl, svl.n)],
                    lambda e: e.tensor_reduce(out=m1[:, :], in_=lg[:, :], op=OP.max, axis=AX.X), svl)
            v = S.E("vector", [(svl, v)],
                    lambda e: e.tensor_scalar(out=eq1[:, :], in0=lg[:, :], scalar1=m1[:, :],
                                              scalar2=None, op0=OP.is_equal), svl)
            v = S.E("vector", [(svl, v)],
                    lambda e: e.scalar_tensor_tensor(out=l2m[:, :], in0=eq1[:, :], scalar=-1e30,
                                                     in1=lg[:, :], op0=OP.mult, op1=OP.add), svl)
            v = S.E("vector", [(svl, v)],
                    lambda e: e.tensor_reduce(out=m2[:, :], in_=l2m[:, :], op=OP.max, axis=AX.X), svl)
            v_dd = S.E("vector", [(svl, v)],
                       lambda e: e.tensor_tensor(out=dd[:, :], in0=m2[:, :], in1=m1[:, :],
                                                 op=OP.subtract), svl)
            c_p2 = S.E("scalar", [(svl, v_dd)],
                       lambda e: e.activation(out=p2c[:, :], in_=dd[:, :], func=AF.Sigmoid), scl)
            v = S.E("vector", [(scl, c_p2)],
                    lambda e: e.tensor_scalar(out=p1c[:, :], in0=p2c[:, :], scalar1=-1.0,
                                              scalar2=1.0, op0=OP.mult, op1=OP.add), svl)
            v = S.E("vector", [(svl, v)],
                    lambda e: e.tensor_scalar(out=eq2[:, :], in0=l2m[:, :], scalar1=m2[:, :],
                                              scalar2=None, op0=OP.is_equal), svl)
            v = S.E("vector", [(svl, v)],
                    lambda e: e.tensor_scalar(out=g1t[:, :], in0=eq1[:, :], scalar1=p1c[:, :],
                                              scalar2=None, op0=OP.mult), svl)
            v = S.E("vector", [(svl, v)],
                    lambda e: e.scalar_tensor_tensor(out=gat[:, :], in0=eq2[:, :], scalar=p2c[:, :],
                                                     in1=g1t[:, :], op0=OP.mult, op1=OP.add), svl)
            wv = [(svl, v)]
            if prev_tp is not None:
                wv.append((mml, prev_tp))
            v_gab = S.E("vector", wv, lambda e: e.tensor_copy(out=gab[:, :], in_=gat[:, :]), svl)
            wv = [(svl, v_gab), dl_done]
            if prev_gtc is not None:
                wv.append((scl, prev_gtc))
            prev_tp = S.E("tensor", wv,
                          lambda e: e.transpose(pt[:, :], gab[:, :], ident[:, :]), mml)
            prev_gtc = S.E("scalar", [(mml, prev_tp)],
                           (lambda tt: lambda e: e.copy(out=gt_sb[:, tt * 128:(tt + 1) * 128],
                                                        in_=pt[:, :]))(tch), scl)
        gate_done = prev_gtc
        es_gate.close()

        es_h = ExitStack()
        widths = [128, 96]
        ph = [[es_h.enter_context(nc.psum_tensor(f"phL{er}_{t}{sfx}", [widths[er], twL], F32))
               for t in range(nth)] for er in range(2)]
        pge = [[es_h.enter_context(nc.psum_tensor(f"pgeL{er}_{t}{sfx}", [widths[er], twL], F32))
                for t in range(nth)] for er in range(2)]
        sels = [sel1, sel2]
        hs_v = []
        for er in range(2):
            wdt = widths[er]
            for t in range(nth):
                i = None
                for g in range(NF):
                    i = S.E("tensor", [xn_ready, (svl, cast_la), (scl, gate_done)] if g == 0 else [],
                            (lambda ee, tt, gg, ww: lambda e: e.matmul(
                                ph[ee][tt][:, :],
                                lab[:, gg * 224 + ee * 128: gg * 224 + ee * 128 + ww],
                                xn_sb[:, gg * Tc + tt * twL: gg * Tc + (tt + 1) * twL],
                                start=(gg == 0), stop=(gg == NF - 1)))(er, t, g, wdt),
                            mml if g == NF - 1 else None)
                m_ph = i
                m_ge = S.E("tensor", [(scl, gate_done)],
                           (lambda ee, tt, ww: lambda e: e.matmul(
                               pge[ee][tt][:, :], sels[ee][:, 0:ww],
                               gt_sb[:, tt * twL:(tt + 1) * twL],
                               start=True, stop=True))(er, t, wdt), mml)
                c_ge = S.E("scalar", [(mml, m_ge)],
                           (lambda ee, tt, ww: lambda e: e.copy(
                               out=ge_sb[ee][0:ww, tt * twL:(tt + 1) * twL],
                               in_=pge[ee][tt][:, :]))(er, t, wdt), scl)
                hs_v.append(S.E("vector", [(mml, m_ph), (scl, c_ge), (svl, svl.n)],
                                (lambda ee, tt, ww: lambda e: e.tensor_tensor(
                                    out=hs[ee][0:ww, tt * twL:(tt + 1) * twL],
                                    in0=ph[ee][tt][:, :],
                                    in1=ge_sb[ee][0:ww, tt * twL:(tt + 1) * twL],
                                    op=OP.mult))(er, t, wdt), svl))
        hs_done = hs_v[-1]
        es_h.close()

        plo = [ps_(f"ploL{i}", [128, twL]) for i in range(2)]
        prev_ev = {}
        dma_vL = {}
        for oc in range(NF):
            for t in range(nth):
                n = oc * nth + t
                b = n % 2
                wv = [(svl, hs_done), (svl, cast_lb)]
                if b in prev_ev and prev_ev[b] is not None:
                    wv.append((svl, prev_ev[b]))
                S.E("tensor", wv,
                    (lambda o_, tt: lambda e: e.matmul(
                        plo[(o_ * nth + tt) % 2][:, :], lb1b[:, o_ * 128:(o_ + 1) * 128],
                        hs[0][:, tt * twL:(tt + 1) * twL], start=True, stop=False))(oc, t))
                m_lo = S.E("tensor", [],
                           (lambda o_, tt: lambda e: e.matmul(
                               plo[(o_ * nth + tt) % 2][:, :], lb2b[:, o_ * 128:(o_ + 1) * 128],
                               hs[1][:, tt * twL:(tt + 1) * twL], start=False, stop=True))(oc, t), mml)
                wv = [(mml, m_lo)]
                if (b,) in dma_vL:
                    wv.append((gml[b], dma_vL[(b,)]))
                ve = S.E("vector", wv,
                         (lambda bb: lambda e: e.tensor_copy(out=ob[bb][:, 0:twL],
                                                             in_=plo[bb][:, :]))(b), svl)
                prev_ev[b] = ve
                dma_vL[(b,)] = S.E("gpsimd", [(svl, ve)],
                                   (lambda o_, tt, bb: lambda e: e.dma_start(
                                       out=lod[o_ * 128:(o_ + 1) * 128, tt * twL:(tt + 1) * twL],
                                       in_=ob[bb][:, 0:twL]))(oc, t, b), gml[b], 16)
    return dict(staged=[(gml[0], gml[0].n), (gml[1], gml[1].n)])


def build_mega():
    m = Mega()
    nc = m.nc
    f = lambda n, s_, d=F32: nc.dram_tensor(n, s_, d, kind="ExternalInput")
    x = f("x", [D, TC])
    wq0 = f("wq0", [D, D]); wk0 = f("wk0", [D, 2048]); wv0 = f("wv0", [D, 2048])
    bq0 = f("bq0", [128, NF]); bk0 = f("bk0", [128, 16]); bvr0 = f("bvr0", [1, 2048])
    qw = nc.dram_tensor("qw", [D, 512], I32, kind="ExternalInput")
    scz = f("scz", [1, NF * 512]); zrz = f("zrz", [1, NF * 512]); fb = f("fb", [128, 4])
    wq1 = f("wq1", [D, D]); wk1 = f("wk1", [D, 1024]); wv1 = f("wv1", [D, 1024])
    cosd = f("cos", [64, TC]); sind = f("sin", [64, TC])
    wo = f("wo", [D, 512]); wb_ = f("wb", [D, 512])
    gw = f("gw", [D, 28]); la = f("la", [D, 224])
    lb1 = f("lb1", [128, D]); lb2 = f("lb2", [96, D])
    sel1 = nc.dram_tensor("sel1", [28, 128], BF16, kind="ExternalInput")
    sel2 = nc.dram_tensor("sel2", [28, 96], BF16, kind="ExternalInput")
    ident = nc.dram_tensor("ident", [128, 128], BF16, kind="ExternalInput")
    dcold = f("dcol", [128, 4]); qkiod = f("qkio", [128, 512])
    outd = nc.dram_tensor("out", [D, TC], F32, kind="ExternalOutput")
    g = lambda n, s_, d=BF16: nc.dram_tensor(n, s_, d)
    xns0 = g("xns0", [D, TC])
    xg0 = nc.dram_tensor("xg0", [NCORE * D, TC], BF16, addr_space="Shared")
    Qs0 = g("Qs0", [D, TC]); Qa0 = g("Qa0", [D, TC])
    Ks0 = g("Ks0", [2048, TC]); Ka0 = g("Ka0", [2048, TC])
    Vs0 = g("Vs0", [NCORE * TC, 256]); Va0 = g("Va0", [NCORE * TC, 256])
    Cs0 = g("Cs0", [D, TC]); Ca0 = g("Ca0", [D, TC])
    Ys0 = g("Ys0", [D, TC]); Ya0 = g("Ya0", [D, TC])
    xo1 = g("xo1", [D, TC], F32)
    xns1 = g("xns1", [D, TC])
    xg1 = nc.dram_tensor("xg1", [NCORE * D, TC], BF16, addr_space="Shared")
    Qs1 = g("Qs1", [D, TC]); Qa1 = g("Qa1", [D, TC])
    Ks1 = g("Ks1", [1024, TC]); Ka1 = g("Ka1", [1024, TC])
    Vs1 = g("Vs1", [NCORE * TC, 128]); Va1 = g("Va1", [NCORE * TC, 128])
    c1s = g("c1s", [512, T])
    cg1 = nc.dram_tensor("cg1", [D, T], BF16, addr_space="Shared")
    Zs1 = g("Zs1", [D, TC]); Za1 = g("Za1", [D, TC])
    lod = g("lod", [D, TC])

    with m.es:
        dcol_sb = m.es.enter_context(nc.sbuf_tensor("dcol_g", [128, 4], F32))
        qkio_sb = m.es.enter_context(nc.sbuf_tensor("qkio_g", [128, 512], F32))
        dpre = m.sem("d4x")
        m.S.E("sync", [], lambda e: e.dma_start(out=dcol_sb[:, :], in_=dcold[:, :]), dpre, 16)
        m.S.E("sync", [], lambda e: e.dma_start(out=qkio_sb[:, :], in_=qkiod[:, :]), dpre, 16)

        with ExitStack() as s0:
            xn0_sb = s0.enter_context(nc.sbuf_tensor("xn0_sb", [128, NF * TC], BF16))
            f0 = emit_front(m, 0, x, None, wq0, wk0, wv0, 2048, True, (bq0, bk0, bvr0),
                            False, None, xn0_sb, xns0, None, Qs0, Ks0, Vs0)
        ccxg0 = emit_coll(m, "agx0", "AllGather", [xns0], [xg0], [f0["xn_staged"]])
        cck0 = emit_coll(m, "a2ak0", "AllToAll", [Ks0], [Ka0], [f0["qk_staged"][1]])
        ccq0 = emit_coll(m, "a2aq0", "AllToAll", [Qs0], [Qa0], [f0["qk_staged"][0]])
        ccv0 = emit_coll(m, "a2av0", "AllToAll", [Vs0], [Va0], [f0["v_staged"]])
        nthc = TC // 512
        n0 = emit_ffn0(m, qw, scz, zrz, fb, xg0, Ys0, ccxg0)
        ccy0 = emit_coll(m, "a2ay0", "AllToAll", [Ys0], [Ya0], [n0["staged"]])

        def ctx0_dst(ih, qt):
            r = 512 * (qt // nthc) + 128 * ih
            c = (qt % nthc) * 512
            return Cs0[r:r + 128, c:c + 512]

        a0 = emit_attn(m, 0, T, 2, 2, float(1.0 / np.sqrt(np.float32(32))),
                       Ka0, Va0, Qa0, dcol_sb, qkio_sb, ctx0_dst,
                       [ccq0, cck0, ccv0, (dpre, dpre.n)])
        ccc0 = emit_coll(m, "a2ac0", "AllToAll", [Cs0], [Ca0], a0["staged"])

        def add0(j, ff):
            if j == 1:
                return Ya0[ff * 128:(ff + 1) * 128, :]
            h = ff
            c = (h % 16) // 2
            i = 2 * ((h % 16) % 2) + (h // 16)
            r = 512 * c + 128 * i
            return Ca0[r:r + 128, :]

        cc_hold = {}

        def hook_k1():
            gk_ = m.sem("gk")
            cc_hold["k"] = emit_coll(m, "a2ak1", "AllToAll", [Ks1], [Ka1], [(gk_, gk_.n)])

        def hook_q1():
            gq_ = m.sem("gq")
            cc_hold["q"] = emit_coll(m, "a2aq1", "AllToAll", [Qs1], [Qa1], [(gq_, gq_.n)])

        with ExitStack() as s1:
            xn1_sb = s1.enter_context(nc.sbuf_tensor("xn1_sb", [128, NF * TC], BF16))
            f1 = emit_front(m, 1, x, add0, wq1, wk1, wv1, 1024, False, None,
                            True, (cosd, sind), xn1_sb, xns1, xo1, Qs1, Ks1, Vs1,
                            add_waits=[ccc0, ccy0], hook_k=hook_k1, hook_q=hook_q1)
            cck1 = cc_hold["k"]
            ccq1 = cc_hold["q"]
            ccv1 = emit_coll(m, "a2av1", "AllToAll", [Vs1], [Va1], [f1["v_staged"]])
            ccxg1 = emit_coll(m, "agx1", "AllGather", [xns1], [xg1], [f1["xn_staged"]])
            lr = emit_lora(m, xn1_sb, f1["xn_done"], gw, la, lb1, lb2, sel1, sel2, ident, lod)

        m.S.start_phase(m.gate())
        sw = ExitStack()
        wo_all = sw.enter_context(nc.sbuf_tensor("wo_allW", [128, NF * 512], BF16))
        wb_all = sw.enter_context(nc.sbuf_tensor("wb_allW", [128, NF * 512], BF16))
        wf4w = [sw.enter_context(nc.sbuf_tensor(f"wf4w{i}", [128, 2048], F32)) for i in range(2)]
        wo_cast_done = emit_wo_cast(m, wo, wb_, wo_all, wb_all, wf4w)

        def ctx1_dst(ih, qt):
            return c1s[128 * ih:128 * (ih + 1), qt * 512:(qt + 1) * 512]

        a1 = emit_attn(m, 1, T, 1, 4, float(1.0 / np.sqrt(np.float32(128))),
                       Ka1, Va1, Qa1, dcol_sb, qkio_sb, ctx1_dst,
                       [ccq1, cck1, ccv1])
        cccg1 = emit_coll(m, "agc1", "AllGather", [c1s], [cg1], a1["staged"])
        w1 = emit_wo_base(m, xg1, cg1, Zs1, wo_all, wb_all, wo_cast_done, ccxg1, cccg1)
        ccz1 = emit_coll(m, "a2az1", "AllToAll", [Zs1], [Za1], [w1["staged"]])
        sw.close()
        emit_final(m, xo1, Za1, lod, outd,
                   [ccz1] + lr["staged"] + f1["xo_done"])
        blk = m.es.enter_context(nc.Block())
        m.S.emit(blk)
    return m.nc


def _launch(nc, in_maps, trace=False):
    return run_bass_kernel_spmd(nc, in_maps, core_ids=list(range(NCORE)), trace=trace)


_progs = {}


def _get(key, builder, *args):
    if key not in _progs:
        _progs[key] = builder(*args)
    return _progs[key]


def _rope_tables(T_):
    import math
    DH = 128
    inv = 1.0 / (500000.0 ** (np.arange(0, DH, 2, dtype=np.float64) / DH))
    wl = 2 * math.pi / inv
    inv = np.where(wl > 8192.0, inv / 32.0, inv)
    smooth = (8192.0 / wl - 1.0) / 3.0
    smoothed = (1 - smooth) * inv / 32.0 + smooth * inv
    med = (wl <= 8192.0) & (wl >= 2048.0)
    inv = np.where(med, smoothed, inv)
    pos = np.arange(T_, dtype=np.float64)
    fr = pos[:, None] * inv[None, :]
    return (np.ascontiguousarray(np.cos(fr).T.astype(np.float32)),
            np.ascontiguousarray(np.sin(fr).T.astype(np.float32)))


def _cc(a):
    return np.ascontiguousarray(a)


def kernel_unfused(L0_attn_x, L0_attn_Wq, L0_attn_bq, L0_attn_Wk, L0_attn_bk, L0_attn_Wv,
           L0_attn_bv, L0_ffn_qweight, L0_ffn_scales, L0_ffn_scaled_zeros, L0_ffn_bias,
           L1_attn_Wq, L1_attn_Wk, L1_attn_Wv, L1_attn_Wo,
           L1_ffn_base_W, L1_ffn_gate_W, L1_ffn_lora_A, L1_ffn_lora_B):
    f32 = np.float32
    T_, Tc = T, TC
    x0 = _cc(np.asarray(L0_attn_x, f32)[0].T)

    wq0T = _cc(np.asarray(L0_attn_Wq, f32).T)
    wk0T = _cc(np.asarray(L0_attn_Wk, f32).T)
    wv0T = _cc(np.asarray(L0_attn_Wv, f32).T)
    bq0 = _cc(np.asarray(L0_attn_bq, f32).reshape(NF, 128).T)
    bk0 = _cc(np.asarray(L0_attn_bk, f32).reshape(16, 128).T)
    bv0 = _cc(np.asarray(L0_attn_bv, f32).reshape(16, 128).T)
    qwT = np.asarray(L0_ffn_qweight, np.int32).reshape(D, D).T
    s2 = np.asarray(L0_ffn_scales, f32)[:, :, 0]
    z2 = np.asarray(L0_ffn_scaled_zeros, f32)[:, :, 0]
    fb = np.asarray(L0_ffn_bias, f32)
    wq1T = _cc(np.asarray(L1_attn_Wq, f32).T)
    wk1T = _cc(np.asarray(L1_attn_Wk, f32).T)
    wv1T = _cc(np.asarray(L1_attn_Wv, f32).T)
    woT = np.asarray(L1_attn_Wo, f32).T
    baseT = np.asarray(L1_ffn_base_W, f32).T
    gwT = _cc(np.asarray(L1_ffn_gate_W, f32).T)
    laA = _cc(np.asarray(L1_ffn_lora_A, f32).transpose(2, 0, 1).reshape(D, 224))
    lbB = np.asarray(L1_ffn_lora_B, f32).transpose(0, 2, 1).reshape(224, D)
    lb1 = _cc(lbB[:128]); lb2 = _cc(lbB[128:])
    sel = np.zeros((28, 224), f32)
    for e_ in range(28):
        sel[e_, e_ * 8:(e_ + 1) * 8] = 2.0
    sel1 = _cc(sel[:, :128].astype(bf16)); sel2 = _cc(sel[:, 128:].astype(bf16))
    ident = np.eye(128, dtype=np.float32).astype(bf16)
    cosT, sinT = _rope_tables(T_)
    dcol = _cc(np.tile((np.arange(4) * 128).astype(f32), (128, 1)))
    qkio = _cc((np.arange(512)[None, :] - np.arange(128)[:, None]).astype(f32))

    def tok(c):
        return slice(c * Tc, (c + 1) * Tc)

    nc = _get(("front0", Tc), build_front, Tc, 0, False, 2048, True)
    maps = [{"x": _cc(x0[:, tok(c)]), "wq": wq0T, "wk": wk0T, "wv": wv0T,
             "bq": bq0, "bk": bk0, "bv": bv0} for c in range(NCORE)]
    r = _launch(nc, maps)
    xn0 = np.concatenate([np.asarray(r.results[c]["xn"]) for c in range(NCORE)], axis=1)
    q0 = np.concatenate([np.asarray(r.results[c]["q"]) for c in range(NCORE)], axis=1)
    k0 = np.concatenate([np.asarray(r.results[c]["k"]) for c in range(NCORE)], axis=1)
    v0 = np.concatenate([np.asarray(r.results[c]["v"]) for c in range(NCORE)], axis=1)

    def vtile(vfull, kv):
        vt = _cc(vfull.T[:, kv * 128:(kv + 1) * 128])
        return _cc(vt.reshape(T_ // 128, 128, 128).transpose(1, 0, 2).reshape(128, T_))

    nc = _get(("attn", T_, 2, 2), build_attn, T_, 2, 2, 1.0 / np.sqrt(np.float32(32)))
    maps = []
    for c in range(NCORE):
        m = {"dcol": dcol, "qkio": qkio}
        for j, kv in enumerate((2 * c, 2 * c + 1)):
            m[f"k{j}"] = _cc(k0[kv * 128:(kv + 1) * 128, :])
            m[f"v{j}"] = vtile(v0, kv)
            m[f"q{2 * j}"] = _cc(q0[kv * 128:(kv + 1) * 128, :])
            m[f"q{2 * j + 1}"] = _cc(q0[(16 + kv) * 128:(16 + kv + 1) * 128, :])
        maps.append(m)
    r = _launch(nc, maps)
    ctx0 = np.empty((D, T_), bf16)
    for c in range(NCORE):
        for j, kv in enumerate((2 * c, 2 * c + 1)):
            ctx0[kv * 128:(kv + 1) * 128] = np.asarray(r.results[c][f"ctx{2 * j}"])
            ctx0[(16 + kv) * 128:(16 + kv + 1) * 128] = np.asarray(r.results[c][f"ctx{2 * j + 1}"])

    nc = _get(("ffn0", T_), build_ffn0, T_)
    maps = []
    for c in range(NCORE):
        osl = slice(512 * c, 512 * (c + 1))
        maps.append({"xn": xn0, "qw": _cc(qwT[:, osl]),
                     "scz": _cc(s2[osl, :].T).reshape(1, -1),
                     "zrz": _cc(z2[osl, :].T).reshape(1, -1),
                     "bias": _cc(fb[osl].reshape(4, 128).T)})
    r = _launch(nc, maps)
    y0 = np.concatenate([np.asarray(r.results[c]["y"]) for c in range(NCORE)], axis=0)

    nc = _get(("front1", Tc), build_front, Tc, 2, True, 1024, False)
    maps = [{"x": _cc(x0[:, tok(c)]), "a0": _cc(ctx0[:, tok(c)]), "a1": _cc(y0[:, tok(c)]),
             "wq": wq1T, "wk": wk1T, "wv": wv1T,
             "cos": _cc(cosT[:, tok(c)]), "sin": _cc(sinT[:, tok(c)])}
            for c in range(NCORE)]
    r = _launch(nc, maps)
    x1 = [np.asarray(r.results[c]["xo"]) for c in range(NCORE)]
    xn1 = np.concatenate([np.asarray(r.results[c]["xn"]) for c in range(NCORE)], axis=1)
    q1 = np.concatenate([np.asarray(r.results[c]["q"]) for c in range(NCORE)], axis=1)
    k1 = np.concatenate([np.asarray(r.results[c]["k"]) for c in range(NCORE)], axis=1)
    v1 = np.concatenate([np.asarray(r.results[c]["v"]) for c in range(NCORE)], axis=1)

    nc = _get(("attn", T_, 1, 4), build_attn, T_, 1, 4, 1.0 / np.sqrt(np.float32(128)))
    maps = []
    for c in range(NCORE):
        m = {"dcol": dcol, "qkio": qkio,
             "k0": _cc(k1[c * 128:(c + 1) * 128, :]), "v0": vtile(v1, c)}
        for j in range(4):
            m[f"q{j}"] = _cc(q1[(4 * c + j) * 128:(4 * c + j + 1) * 128, :])
        maps.append(m)
    r = _launch(nc, maps)
    ctx1 = np.empty((D, T_), bf16)
    for c in range(NCORE):
        for j in range(4):
            h = 4 * c + j
            ctx1[h * 128:(h + 1) * 128] = np.asarray(r.results[c][f"ctx{j}"])

    nc = _get(("wo_base", T_), build_wo_base, T_)
    maps = [{"cx": ctx1, "xn": xn1,
             "wo": _cc(woT[:, 512 * c:512 * (c + 1)]),
             "wb": _cc(baseT[:, 512 * c:512 * (c + 1)])} for c in range(NCORE)]
    r = _launch(nc, maps)
    z1 = np.concatenate([np.asarray(r.results[c]["z"]) for c in range(NCORE)], axis=0)

    nc = _get(("lora", Tc), build_lora, Tc)
    maps = [{"xn": _cc(xn1[:, tok(c)]), "gw": gwT, "la": laA, "lb1": lb1, "lb2": lb2,
             "sel1": sel1, "sel2": sel2, "ident": ident} for c in range(NCORE)]
    r = _launch(nc, maps)
    lo1 = [np.asarray(r.results[c]["lo"]) for c in range(NCORE)]

    nc = _get(("final", Tc), build_final, Tc)
    maps = [{"x": x1[c], "z": _cc(z1[:, tok(c)]), "lo": lo1[c]} for c in range(NCORE)]
    r = _launch(nc, maps)
    out = np.concatenate([np.asarray(r.results[c]["out"]) for c in range(NCORE)], axis=1)
    return _cc(out.T).reshape(1, T_, D)


_mega_cache = {}


def _host_prep(inputs):
    f32 = np.float32
    p = {}
    p["x0"] = _cc(np.asarray(inputs["L0_attn_x"], f32)[0].T)
    p["wq0T"] = _cc(np.asarray(inputs["L0_attn_Wq"], f32).T)
    p["wk0T"] = _cc(np.asarray(inputs["L0_attn_Wk"], f32).T)
    p["wv0T"] = _cc(np.asarray(inputs["L0_attn_Wv"], f32).T)
    p["bq0"] = _cc(np.asarray(inputs["L0_attn_bq"], f32).reshape(NF, 128).T)
    p["bk0"] = _cc(np.asarray(inputs["L0_attn_bk"], f32).reshape(16, 128).T)
    p["bvr0"] = _cc(np.asarray(inputs["L0_attn_bv"], f32).reshape(1, 2048))
    p["qwT"] = np.asarray(inputs["L0_ffn_qweight"], np.int32).reshape(D, D).T
    p["s2"] = np.asarray(inputs["L0_ffn_scales"], f32)[:, :, 0]
    p["z2"] = np.asarray(inputs["L0_ffn_scaled_zeros"], f32)[:, :, 0]
    p["fb"] = np.asarray(inputs["L0_ffn_bias"], f32)
    p["wq1T"] = _cc(np.asarray(inputs["L1_attn_Wq"], f32).T)
    p["wk1T"] = _cc(np.asarray(inputs["L1_attn_Wk"], f32).T)
    p["wv1T"] = _cc(np.asarray(inputs["L1_attn_Wv"], f32).T)
    p["woT"] = np.asarray(inputs["L1_attn_Wo"], f32).T
    p["baseT"] = np.asarray(inputs["L1_ffn_base_W"], f32).T
    p["gwT"] = _cc(np.asarray(inputs["L1_ffn_gate_W"], f32).T)
    p["laA"] = _cc(np.asarray(inputs["L1_ffn_lora_A"], f32).transpose(2, 0, 1).reshape(D, 224))
    lbB = np.asarray(inputs["L1_ffn_lora_B"], f32).transpose(0, 2, 1).reshape(224, D)
    p["lb1"] = _cc(lbB[:128]); p["lb2"] = _cc(lbB[128:])
    sel = np.zeros((28, 224), f32)
    for e_ in range(28):
        sel[e_, e_ * 8:(e_ + 1) * 8] = 2.0
    p["sel1"] = _cc(sel[:, :128].astype(bf16)); p["sel2"] = _cc(sel[:, 128:].astype(bf16))
    p["ident"] = np.eye(128, dtype=np.float32).astype(bf16)
    p["cosT"], p["sinT"] = _rope_tables(T)
    p["dcol"] = _cc(np.tile((np.arange(4) * 128).astype(f32), (128, 1)))
    p["qkio"] = _cc((np.arange(512)[None, :] - np.arange(128)[:, None]).astype(f32))
    return p


def kernel_fused(**inputs):
    p = _host_prep(inputs)
    if "nc" not in _mega_cache:
        _mega_cache["nc"] = build_mega()
    nc = _mega_cache["nc"]
    maps = []
    for c in range(NCORE):
        sl = slice(c * TC, (c + 1) * TC)
        osl = slice(512 * c, 512 * (c + 1))
        maps.append({
            "x": _cc(p["x0"][:, sl]),
            "wq0": p["wq0T"], "wk0": p["wk0T"], "wv0": p["wv0T"],
            "bq0": p["bq0"], "bk0": p["bk0"], "bvr0": p["bvr0"],
            "qw": _cc(p["qwT"][:, osl]),
            "scz": _cc(p["s2"][osl, :].T).reshape(1, -1),
            "zrz": _cc(p["z2"][osl, :].T).reshape(1, -1),
            "fb": _cc(p["fb"][osl].reshape(4, 128).T),
            "wq1": p["wq1T"], "wk1": p["wk1T"], "wv1": p["wv1T"],
            "cos": _cc(p["cosT"][:, sl]), "sin": _cc(p["sinT"][:, sl]),
            "wo": _cc(p["woT"][:, osl]), "wb": _cc(p["baseT"][:, osl]),
            "gw": p["gwT"], "la": p["laA"], "lb1": p["lb1"], "lb2": p["lb2"],
            "sel1": p["sel1"], "sel2": p["sel2"], "ident": p["ident"],
            "dcol": p["dcol"], "qkio": p["qkio"],
        })
    r = _launch(nc, maps)
    out = np.concatenate([np.asarray(r.results[c]["out"]) for c in range(NCORE)], axis=1)
    return _cc(out.T).reshape(1, T, D)


FUSED = True


def kernel(**inputs):
    if FUSED:
        return kernel_fused(**inputs)
    return kernel_unfused(**inputs)
